# version 28
# baseline (speedup 1.0000x reference)
import numpy as np
import ml_dtypes
import concourse.bass as bass
import concourse.mybir as mybir
from concourse.bass_utils import run_bass_kernel_spmd

F32 = mybir.dt.float32
BF16 = mybir.dt.bfloat16
FP8 = mybir.dt.float8e4
ALU = mybir.AluOpType
AF = mybir.ActivationFunctionType
AX = mybir.AxisListType
KIB = 1024


def build_program(S=2048, E=2048, n_cores=8):
    assert S == E
    NT = S // 128
    NE = E // 128
    GA = max(NE // 4, 1)
    TPG = NE // GA
    NPS = max(S // 512, 1)
    NPE = max(E // 512, 1)
    QW = 512
    NQP = E // QW
    QB = QW // 128
    U = NT * NPE
    SH = S // 2
    TK = S // 128
    GK = (TK + 7) // 8
    NB = n_cores
    NTO = (NB - 1) * NT

    nc = bass.Bass("TRN2", target_bir_lowering=False)

    x_in = nc.dram_tensor("x", [S, E], F32, kind="ExternalInput").ap()
    xo_in = nc.dram_tensor("xo", [NTO * 128, E], F32, kind="ExternalInput").ap()
    wq_in = nc.dram_tensor("wq", [E, E], F32, kind="ExternalInput").ap()
    wk_in = nc.dram_tensor("wk", [E, E], F32, kind="ExternalInput").ap()
    wv_in = nc.dram_tensor("wv", [E, E], F32, kind="ExternalInput").ap()
    wo_in = nc.dram_tensor("wo", [E, E], F32, kind="ExternalInput").ap()
    wft_in = nc.dram_tensor("wft", [E, E], BF16, kind="ExternalInput").ap()
    bias_in = nc.dram_tensor("bias", [E], BF16, kind="ExternalInput").ap()
    out_hbm = nc.dram_tensor("out", [S, E], F32, kind="ExternalOutput").ap()

    xn_hbm = nc.dram_tensor("xn_scr", [S, E], F32, kind="Internal").ap()
    at_hbm = nc.dram_tensor("at_scr", [NE, 128, S], BF16, kind="Internal").ap()
    ot_hbm = nc.dram_tensor("ot_scr", [NE, 128, S], BF16, kind="Internal").ap()
    wq8_hbm = nc.dram_tensor("wq8_scr", [NE, 128, E], FP8, kind="Internal").ap()
    pscr1 = nc.dram_tensor("pscr1", [128], F32, kind="Internal").ap()
    pscr2 = nc.dram_tensor("pscr2", [128], F32, kind="Internal").ap()

    x_t = x_in.rearrange("(t p) e -> t p e", p=128)
    xo_t = xo_in.rearrange("(t p) e -> t p e", p=128)
    xn_t = xn_hbm.rearrange("(t p) e -> t p e", p=128)
    out_t = out_hbm.rearrange("(t p) e -> t p e", p=128)
    wq_c = wq_in.rearrange("(c p) f -> p c f", p=128)
    wk_c = wk_in.rearrange("(c p) f -> c p f", p=128)
    wv_c = wv_in.rearrange("(c p) f -> p c f", p=128)
    wo_c = wo_in.rearrange("(c p) f -> p c f", p=128)
    wft_c = wft_in.rearrange("(c p) f -> c p f", p=128)

    specs = {}

    def sb(name, shape, dtype):
        specs[name] = (shape, dtype)
        return name

    addr_of = {}

    def place(names, base):
        off = base
        for n in names:
            shape, dtype = specs[n]
            per = 1
            for d in shape[1:]:
                per *= d
            per *= mybir.dt.size(dtype)
            off = (off + 63) & ~63
            addr_of[n] = off
            off += per
        return off

    names_small = []
    for n, shp, dt_ in [
        ("ident_f", [128, 128], F32), ("ident_b", [128, 128], BF16),
        ("ones_f", [1, 128], F32), ("ones_b", [1, 128], BF16),
        ("cat", [1, 3], F32), ("bc3", [128, 3], F32),
        ("run128", [128, 1], F32), ("wf128", [128, 1], F32),
        ("amb", [128, 2], F32), ("wab", [128, 2], F32),
        ("s1", [128, 2], F32), ("s2", [128, 2], F32),
        ("mstat", [128, 2], F32), ("qstat", [128, 2], F32),
        ("mm2", [128, 2], F32), ("varst", [128, 2], F32),
        ("rstd", [128, 2], F32), ("rinv", [128, 2], F32),
        ("bc3r", [128, 3], F32), ("prow1", [1, 128], F32),
        ("prow2", [1, 128], F32), ("lmax", [1, 1], F32),
        ("wmax", [1, 1], F32), ("gmax", [1, 1], F32),
        ("pmq", [128, max(NPS, 1)], F32), ("negm", [128, 1], F32),
        ("bmk", [128, 2], F32), ("den0", [128, 2], F32),
        ("den1", [128, 2], F32), ("dent", [128, 2], F32),
        ("recdall", [128, NE], F32),
        ("bias_rep", [128, E], BF16), ("wfb", [128, E], BF16),
        ("wsb", [128, 1], F32),
        ("wsbr", [128, 1], F32), ("oMx", [128, NT], F32),
        ("omn", [128, NT], F32), ("osm", [128, NT], F32),
        ("osq", [128, NT], F32), ("om", [128, NT], F32),
        ("oq", [128, NT], F32), ("omm", [128, NT], F32),
        ("ovar", [128, NT], F32), ("orstd", [128, NT], F32),
        ("orinv", [128, NT], F32), ("od1", [128, NT], F32),
        ("odm", [128, NT], F32), ("opx", [128, 1], F32),
    ]:
        names_small.append(sb(n, shp, dt_))
    sb("xnT", [128, NE * S], BF16)
    sb("ot_res", [128, NE * S], BF16)
    for i in range(2):
        sb(f"xbuf{i}", [128, E], F32)
    for i in range(2):
        sb(f"xnbuf{i}", [128, E], F32)
        sb(f"wq8st{i}", [128, E], FP8)
    sb("K_bf", [128, NT * E + (0 if S >= 2048 else 12 * E)], BF16)
    sb("VT", [128, NE * S], BF16)
    for i in range(2):
        sb(f"wobf{i}", [128, NE * 512], BF16)
    for i in range(6):
        sb(f"obuf{i}", [128, E], BF16)
    sb("brow", [1, E], BF16)
    sb("Wk_bf", [128, NE * E], BF16)
    for i in range(2):
        sb(f"wqbf{i}", [128, NE * QW], BF16)
    sb("q_pnl", [128, NT * QW], BF16)
    for i in range(2):
        sb(f"A_bf{i}", [128, S], BF16)
        sb(f"at_st{i}", [128, S], BF16)
        sb(f"atb{i}", [128, S], BF16)
    for i in range(4):
        sb(f"wvbf{i}", [128, NE * 128], BF16)
    for i in range(2):
        sb(f"otst{i}", [128, S], BF16)
    sb("xq8", [128, NE * S], FP8)
    for i in range(2):
        sb(f"wq8r{i}", [128, NE * 512], FP8)
        sb(f"fbf{i}", [128, 512], BF16)
        sb(f"f2t{i}", [128, 512], BF16)
        sb(f"xst{i}", [128, 512], F32)
        sb(f"outst{i}", [128, 512], F32)

    end0 = place(names_small, 0)
    R1 = (end0 + 1023) & ~1023
    e1 = max(place(["xnT"], R1), place(["ot_res"], R1))
    R2 = (e1 + 1023) & ~1023
    e2 = max(place([f"xbuf{i}" for i in range(2)]
                   + [f"xnbuf{i}" for i in range(2)]
                   + [f"wq8st{i}" for i in range(2)], R2),
             place(["K_bf"], R2), place(["VT"], R2),
             place(["wobf0", "wobf1"], R2))
    CHB = E * 2
    AT_END = ((addr_of["wq8st1"] + E - R2) + 63) & ~63
    FREEK = min(NT, max((NT * CHB - AT_END) // CHB, 0))
    e2 = max(e2, R2 + AT_END + FREEK * CHB)

    def kaddr(c):
        if c < FREEK:
            return AT_END + CHB * c
        return CHB * (c - FREEK)

    def K_ap(c, lo, hi):
        base = kaddr(c) // 2
        return K_bf[:, base + lo:base + hi]
    R3 = (e2 + 1023) & ~1023
    e3 = max(
        place(["Wk_bf"], R3),
        place(["wqbf0", "wqbf1", "q_pnl", "A_bf0", "A_bf1", "at_st0", "at_st1"], R3),
        place(["wvbf0", "wvbf1", "wvbf2", "wvbf3"], R3),
        place(["otst0", "otst1"], R3 + 8 * KIB),
        place(["atb0", "atb1"], R3),
        place(["xq8"], R3 + 16 * KIB),
        place(["wq8r0", "wq8r1"], R3 + 48 * KIB),
        place(["obuf0", "obuf1", "obuf2", "obuf3", "obuf4", "obuf5"], R3 + 40 * KIB),
        place(["brow"], (R2 + 32 * KIB) if S >= 2048 else (R3 + 120 * KIB)),
        place(["fbf0", "fbf1", "f2t0", "f2t1", "xst0", "xst1", "outst0", "outst1"], R3),
    )
    total = (e3 + 63) & ~63
    assert total <= 206 * KIB, f"sbuf over budget: {total}"
    backing = nc.alloc_sbuf_tensor("sbuf_backing", [128, total], mybir.dt.uint8)
    base = nc.lookup_mloc(backing).addr
    T = {}
    for n, (shp, dt_) in specs.items():
        T[n] = nc.alloc_sbuf_tensor_at(n, shp, dt_, offset=base + addr_of[n])

    G = T
    ident_f, ident_b, ones_f, ones_b = G["ident_f"], G["ident_b"], G["ones_f"], G["ones_b"]
    cat, bc3, run128, wf128 = G["cat"], G["bc3"], G["run128"], G["wf128"]
    amb, wab, s1, s2 = G["amb"], G["wab"], G["s1"], G["s2"]
    mstat, qstat, mm2, varst = G["mstat"], G["qstat"], G["mm2"], G["varst"]
    rstd, rinv, bc3r = G["rstd"], G["rinv"], G["bc3r"]
    prow1, prow2, lmax, wmax, gmax = G["prow1"], G["prow2"], G["lmax"], G["wmax"], G["gmax"]
    pmq, negm, bmk = G["pmq"], G["negm"], G["bmk"]
    den0, den1, dent, recdall = G["den0"], G["den1"], G["dent"], G["recdall"]
    brow, bias_rep, wfb, wsb, wsbr = G["brow"], G["bias_rep"], G["wfb"], G["wsb"], G["wsbr"]
    oMx, omn, osm, osq = G["oMx"], G["omn"], G["osm"], G["osq"]
    om, oq, omm, ovar = G["om"], G["oq"], G["omm"], G["ovar"]
    orstd, orinv, od1, odm, opx = G["orstd"], G["orinv"], G["od1"], G["odm"], G["opx"]

    xnT, ot_res, K_bf, VT = T["xnT"], T["ot_res"], T["K_bf"], T["VT"]
    Wk_bf, q_pnl, xq8 = T["Wk_bf"], T["q_pnl"], T["xq8"]
    xbuf = [T[f"xbuf{i}"] for i in range(2)]
    obuf = [T[f"obuf{i}"] for i in range(6)] + [T["bias_rep"], T["wfb"]]
    otst = [T[f"otst{i}"] for i in range(2)]
    NOB = 8
    xnbuf = [T[f"xnbuf{i}"] for i in range(2)]
    wq8st = [T[f"wq8st{i}"] for i in range(2)]
    wobf = [T[f"wobf{i}"] for i in range(2)]
    wqbf = [T[f"wqbf{i}"] for i in range(2)]
    A_bf = [T[f"A_bf{i}"] for i in range(2)]
    at_st = [T[f"at_st{i}"] for i in range(2)]
    wvbf = [T[f"wvbf{i}"] for i in range(4)]
    atb = [T[f"atb{i}"] for i in range(2)]
    wq8r = [T[f"wq8r{i}"] for i in range(2)]
    fbf = [T[f"fbf{i}"] for i in range(2)]
    f2t = [T[f"f2t{i}"] for i in range(2)]
    xst = [T[f"xst{i}"] for i in range(2)]
    outst = [T[f"outst{i}"] for i in range(2)]

    import contextlib
    est = contextlib.ExitStack()
    ps_big = est.enter_context(nc.psum_tensor("ps_big", [128, min(S, 2048)], F32))
    pb4 = est.enter_context(nc.psum_tensor("pb4", [128, 512], F32))
    pb5 = est.enter_context(nc.psum_tensor("pb5", [128, 512], F32))
    pb6 = est.enter_context(nc.psum_tensor("pb6", [128, 512], F32))
    pbt = est.enter_context(nc.psum_tensor("pbt", [128, 1024], BF16))
    pbank = [pb4, pb5, pb6]
    BCX0 = min(S, 2048) - 512
    K0 = min(10, max(NE - 2, 2)) & ~1
    pbank6 = [pb4[:], pb5[:], pb6[:], ps_big[:, 0:512], ps_big[:, 512:1024],
              ps_big[:, 1024:1536]] if S >= 2048 else [pb4[:], pb5[:], pb6[:]]
    NBK = len(pbank6)
    pbank7 = pbank6 + [ps_big[:, BCX0:BCX0 + 512]]
    NBK7 = len(pbank7)
    bb = ([ps_big[:, 512 * k:512 * (k + 1)] for k in range(4)] if S >= 2048
          else [ps_big[:, 0:512]])
    NBB = len(bb)

    finals = {}
    sem = {}

    def mksem(name, n_slots, final_counts):
        for s_ in range(n_slots):
            key = f"{name}{s_}" if n_slots > 1 else name
            fc = final_counts[s_] if isinstance(final_counts, list) else final_counts
            finals[key] = fc
            sem[key] = est.enter_context(nc.semaphore(key))

    def slotkey(name, slot, n_slots):
        return f"{name}{slot}" if n_slots > 1 else name

    def ring_finals(n_items, n_slots, inc=16):
        return [inc * len([i for i in range(n_items) if i % n_slots == s_])
                for s_ in range(n_slots)]

    mksem("d_x", 2, ring_finals(NT, 2))
    mksem("d_wfb", 1, 16 * 2 * NE)
    mksem("c_wab", 1, NE)
    mksem("d_p1", 1, 16)
    mksem("d_p2", 1, 16)
    mksem("d_p3", 1, 16)
    mksem("d_p4", 1, 16)
    mksem("d_bias", 1, 16)
    NC1O = min(24, NTO)
    slot_of_t, ord_of_t, prev_of_t = [], [], []
    _sc = [0] * 8
    _last = [None] * 8
    for _t in range(NTO):
        _s = 6 + (_t & 1) if _t < NC1O else (_t - NC1O) % 6
        slot_of_t.append(_s)
        ord_of_t.append(_sc[_s])
        prev_of_t.append(_last[_s])
        _sc[_s] += 1
        _last[_s] = _t
    mksem("d_oth", 8, [16 * _sc[_s2] for _s2 in range(8)])
    mksem("d_wq8w", 2, ring_finals(NE, 2))
    for j in range(max(NE // 4, 1)):
        mksem(f"d_wk{j}", 1, 16)
    mksem("d_wq", 2, ring_finals(NQP, 2))
    mksem("d_atw", 2, ring_finals(NE, 2))
    mksem("d_atr", 2, ring_finals(NE, 2))
    mksem("d_wv", 4, ring_finals(NE, 4))
    mksem("d_otw", 2, ring_finals(K0, 2))
    mksem("d_otrb", 1, 16 * (K0 // 2))
    mksem("d_wq8r", 2, ring_finals(NPE, 2))
    mksem("d_wo", 2, ring_finals(NPE, 2))
    mksem("d_xe", 2, ring_finals(U, 2))
    mksem("d_out", 2, ring_finals(U, 2))
    for name, fc in [
        ("c_idv", 4), ("c_idp", 2), ("c_sacc", NT), ("c_scp", NT),
        ("c_pp", 4 * NT),
        ("c_rstd", NT), ("c_ri", NT), ("c_bc3r", 1), ("c_xn", NT), ("c_trA", GA * NT), ("c_evA", GA * NT),
        ("c_am", NT), ("c_mg", NT), ("c_dsc", 12),
        ("c_bcW", 1), ("c_bcX", 1), ("c_wq8", NE),
        ("c_oe", 9 * max(NB - 1, 1)), ("c_othv", NTO), ("c_otha", NTO),
        ("c_ocp", NTO),
        ("c_ovar", max(NB - 1, 1)),
        ("c_orst", max(NB - 1, 1)), ("c_othmg", max(NB - 1, 1)),
        ("c_mmB", NT * NPE), ("c_evB", NT * NPE), ("c_mmQ", NQP * NT),
        ("c_qev", NQP * NT), ("c_mmSq", NPS * NE), ("c_pmx", NPS * NE),
        ("c_ngm", NE), ("c_bmk", NE), ("c_den", NE), ("c_rcd", NE),
        ("c_exph", 2 * NE), ("c_trAT", GK * NE), ("c_evAT", GK * NE),
        ("c_mmV", NE * NPS), ("c_evV", NE * NPS), ("c_mmO", NE * NPS),
        ("c_evO", NE * NPS), ("c_q8x", NE),
        ("c_bcb", 1), ("c_bcb2", 1), ("c_mmEat", U), ("c_mmEff", U),
        ("c_fb", U), ("c_f2", U), ("c_o1", U), ("c_o2", U),
    ]:
        mksem(name, 1, fc)
    waits = []

    def W(eng, name, val):
        if val <= 0:
            return
        waits.append((name, val))
        eng.wait_ge(sem[name], val)

    def I(inst, name, n=1):
        if n > 0:
            inst.then_inc(sem[name], n)
        return inst

    def ck(t, j, w):
        return t[:, j * w:(j + 1) * w]

    oth_ld = [0]
    oth_dv = [0]
    oth_ac = [0]
    oth_oe = [0]
    wq8r_issued = [False]

    def emit_oth_load(eng, n):
        for _ in range(n):
            t = oth_ld[0]
            if t >= NTO:
                return
            oth_ld[0] += 1
            s = slot_of_t[t]
            pv = prev_of_t[t]
            if pv is not None:
                W(eng, "c_othv", pv + 1)
                W(eng, "c_otha", pv + 1)
            if ord_of_t[t] == 0:
                if s == 6:
                    W(eng, "d_wq8w0", finals["d_wq8w0"])
                    W(eng, "d_wq8w1", finals["d_wq8w1"])
                elif s == 7:
                    W(eng, "c_wq8", NE)
            if t == NC1O and NC1O < NTO:
                W(eng, "d_atw0", finals["d_atw0"])
                W(eng, "d_atw1", finals["d_atw1"])
                W(eng, "c_trAT", GK * NE)
                W(eng, "c_mmSq", NPS * NE)
            I(eng.dma_start(out=obuf[s][:], in_=xo_t[t]), f"d_oth{s}", 16)

    def emit_oth_dve(vector, n):
        invn_ = 1.0 / E
        for _ in range(n):
            t = oth_dv[0]
            if t >= NTO:
                return
            oth_dv[0] += 1
            b, lt = divmod(t, NT)
            s = slot_of_t[t]
            W(vector, f"d_oth{s}", 16 * (ord_of_t[t] + 1))
            vector.tensor_reduce(out=oMx[:, lt:lt + 1], in_=obuf[s][:],
                                 axis=AX.X, op=ALU.max)
            I(vector.tensor_reduce(out=omn[:, lt:lt + 1], in_=obuf[s][:],
                                   axis=AX.X, op=ALU.min), "c_othv")
            if lt == NT - 1:
                base = oth_oe[0]
                W(vector, "c_otha", (b + 1) * NT)
                W(vector, "c_othv", (b + 1) * NT)
                I(vector.tensor_scalar(out=om[:], in0=osm[:], scalar1=invn_,
                                       scalar2=None, op0=ALU.mult), "c_oe")
                I(vector.tensor_scalar(out=oq[:], in0=osq[:], scalar1=invn_,
                                       scalar2=1e-5, op0=ALU.mult,
                                       op1=ALU.add), "c_oe")
                W(vector, "c_oe", base + 1)
                I(vector.tensor_tensor(out=omm[:], in0=om[:], in1=om[:],
                                       op=ALU.mult), "c_oe")
                W(vector, "c_oe", base + 3)
                I(vector.tensor_tensor(out=ovar[:], in0=oq[:], in1=omm[:],
                                       op=ALU.subtract), "c_ovar")
                W(vector, "c_orst", b + 1)
                I(vector.reciprocal(out=orinv[:], in_=orstd[:]), "c_oe")
                I(vector.tensor_tensor(out=od1[:], in0=oMx[:], in1=om[:],
                                       op=ALU.subtract), "c_oe")
                I(vector.tensor_tensor(out=odm[:], in0=om[:], in1=omn[:],
                                       op=ALU.subtract), "c_oe")
                W(vector, "c_oe", base + 6)
                I(vector.tensor_tensor(out=odm[:], in0=od1[:], in1=odm[:],
                                       op=ALU.max), "c_oe")
                W(vector, "c_oe", base + 7)
                I(vector.tensor_tensor(out=odm[:], in0=odm[:], in1=orinv[:],
                                       op=ALU.mult), "c_oe")
                W(vector, "c_oe", base + 8)
                I(vector.tensor_reduce(out=opx[:], in_=odm[:], axis=AX.X,
                                       op=ALU.max), "c_oe")
                W(vector, "c_oe", base + 9)
                I(vector.tensor_tensor(out=run128[:], in0=run128[:], in1=opx[:],
                                       op=ALU.max), "c_othmg")
                oth_oe[0] = base + 9

    def emit_oth_act(scalar, n):
        for _ in range(n):
            t = oth_ac[0]
            if t >= NTO:
                return
            oth_ac[0] += 1
            b, lt = divmod(t, NT)
            s = slot_of_t[t]
            W(scalar, f"d_oth{s}", 16 * (ord_of_t[t] + 1))
            W(scalar, "c_othv", t + 1)
            I(scalar.activation(out=obuf[s][:], in_=obuf[s][:], func=AF.Copy,
                                accum_out=osm[:, lt:lt + 1]), "c_ocp")
            W(scalar, "c_ocp", t + 1)
            I(scalar.activation(out=obuf[s][:], in_=obuf[s][:],
                                func=AF.Square, accum_out=osq[:, lt:lt + 1]),
              "c_otha")
            if lt == NT - 1:
                W(scalar, "c_ovar", b + 1)
                I(scalar.activation(out=orstd[:], in_=ovar[:], func=AF.Sqrt),
                  "c_orst")

    with nc.Block() as block:

        @block.sync
        def _(sync):
            for i in range(NT):
                if i >= 2:
                    W(sync, "c_xn", i - 1)
                    W(sync, "c_sacc", i - 1)
                I(sync.dma_start(out=xbuf[i % 2][:], in_=x_t[i]), f"d_x{i % 2}", 16)
                if i < NE:
                    if i >= 1:
                        W(sync, "c_wab", i)
                    I(sync.dma_start(out=wfb[:], in_=wft_c[i]), "d_wfb", 16)
            for k in range(NE):
                W(sync, "c_evAT", GK * (k + 1))
                I(sync.dma_start(out=at_hbm[k], in_=at_st[k % 2][:]), f"d_atw{k % 2}", 16)
            for k in range(NE):
                if k == 0:
                    W(sync, "c_mmV", NE * NPS)
                if k >= 2:
                    W(sync, "c_mmO", (k - 1) * NPS)
                W(sync, f"d_atw{k % 2}", 16 * (k // 2 + 1))
                I(sync.dma_start(out=atb[k % 2][:], in_=at_hbm[k]), f"d_atr{k % 2}", 16)
                if 1 <= k <= K0:
                    W(sync, "c_evO", k * NPS)
                    I(sync.dma_start(out=ot_hbm[k - 1], in_=otst[(k - 1) % 2][:]),
                      f"d_otw{(k - 1) % 2}", 16)
            W(sync, "c_mmO", NE * NPS)
            for g in range(NPE):
                for s_ in range(NT):
                    u = g * NT + s_
                    if u >= 2:
                        W(sync, "c_o1", u - 1)
                    I(sync.dma_start(out=xst[u % 2][:],
                                     in_=x_t[s_][:, 512 * g:512 * (g + 1)]),
                      f"d_xe{u % 2}", 16)
                    if u >= 1:
                        W(sync, "c_o2", u)
                        up = u - 1
                        I(sync.dma_start(
                            out=out_t[up % NT][:, 512 * (up // NT):512 * (up // NT + 1)],
                            in_=outst[up % 2][:]), f"d_out{up % 2}", 16)
            W(sync, "c_o2", U)
            up = U - 1
            I(sync.dma_start(
                out=out_t[up % NT][:, 512 * (up // NT):512 * (up // NT + 1)],
                in_=outst[up % 2][:]), f"d_out{up % 2}", 16)
            sync.wait_ge(sem["d_out0"], finals["d_out0"])
            sync.wait_ge(sem["d_out1"], finals["d_out1"])

        @block.scalar
        def _(scalar):
            for i in range(NT):
                W(scalar, f"d_x{i % 2}", 16 * (i // 2 + 1))
                if i >= 2:
                    W(scalar, "c_trA", GA * (i - 1))
                    W(scalar, "c_am", i - 1)
                I(scalar.activation(out=xnbuf[i % 2][:], in_=xbuf[i % 2][:],
                                    func=AF.Copy,
                                    accum_out=s1[:, i % 2:i % 2 + 1]), "c_scp")
                W(scalar, "c_scp", i + 1)
                I(scalar.activation(out=xnbuf[i % 2][:], in_=xbuf[i % 2][:],
                                    func=AF.Square,
                                    accum_out=s2[:, i % 2:i % 2 + 1]), "c_sacc")
                W(scalar, "c_pp", 4 * (i + 1))
                I(scalar.activation(out=rstd[:, i % 2:i % 2 + 1],
                                    in_=varst[:, i % 2:i % 2 + 1],
                                    func=AF.Sqrt), "c_rstd")
                if i >= 1:
                    for g in range(GA):
                        idx = GA * (i - 1) + g
                        W(scalar, "c_trA", idx + 1)
                        src = pb4 if idx % 2 == 0 else pb5
                        I(scalar.activation(
                            out=xnT[:].rearrange("p (c s) -> p c s", c=NE)[:, g * TPG:(g + 1) * TPG, 128 * (i - 1):128 * i],
                            in_=src[:].rearrange("p (c s) -> p c s", c=TPG),
                            func=AF.Copy), "c_evA")
            for g in range(GA):
                idx = GA * (NT - 1) + g
                W(scalar, "c_trA", idx + 1)
                src = pb4 if idx % 2 == 0 else pb5
                I(scalar.activation(
                    out=xnT[:].rearrange("p (c s) -> p c s", c=NE)[:, g * TPG:(g + 1) * TPG, 128 * (NT - 1):128 * NT],
                    in_=src[:].rearrange("p (c s) -> p c s", c=TPG),
                    func=AF.Copy), "c_evA")
            W(scalar, "c_wab", NE)
            for j in range(NE):
                if j >= 1:
                    W(scalar, "c_wq8", j)
                I(scalar.dma_start(out=wfb[:], in_=wft_c[j]), "d_wfb", 16)
            for p in range(NQP):
                for kl in range(QB):
                    k = p * QB + kl
                    W(scalar, "c_bmk", k + 1)
                    W(scalar, "c_mmSq", NPS * (k + 1))
                    if k >= 2:
                        W(scalar, "c_trAT", GK * (k - 1))
                        W(scalar, "c_den", k - 1)
                    I(scalar.activation(out=A_bf[k % 2][:, 0:SH], in_=ps_big[:, 0:SH],
                                        func=AF.Exp, scale=0.125,
                                        bias=bmk[:, k % 2:k % 2 + 1],
                                        accum_out=den0[:, k % 2:k % 2 + 1]), "c_exph")
                    I(scalar.activation(out=A_bf[k % 2][:, SH:S], in_=ps_big[:, SH:S],
                                        func=AF.Exp, scale=0.125,
                                        bias=bmk[:, k % 2:k % 2 + 1],
                                        accum_out=den1[:, k % 2:k % 2 + 1]), "c_exph")
                    for g2 in range(GK):
                        tcnt = min(8, TK - 8 * g2)
                        idxAT = GK * k + g2
                        W(scalar, "c_trAT", idxAT + 1)
                        if k >= 2 and g2 == 0:
                            W(scalar, f"d_atw{k % 2}", 16 * ((k - 2) // 2 + 1))
                        I(scalar.activation(
                            out=at_st[k % 2][:, 1024 * g2:1024 * g2 + 128 * tcnt],
                            in_=pbt[:, 0:128 * tcnt], func=AF.Copy), "c_evAT")
                    emit_oth_act(scalar, 2 if k < 8 else 1)
            for _f in range(NE):
                emit_oth_act(scalar, 3 if _f < NE // 2 else 4)
            for k in range(NE):
                if k < 8:
                    emit_oth_act(scalar, 4)
                for sp in range(NPS):
                    idx = k * NPS + sp
                    W(scalar, "c_mmO", idx + 1)
                    if k == 0 and sp == 0:
                        W(scalar, "c_mmV", NE * NPS)
                        W(scalar, "d_atw0", finals["d_atw0"])
                        W(scalar, "d_atw1", finals["d_atw1"])
                    if sp == 0:
                        W(scalar, "c_rcd", k + 1)
                        if k < K0:
                            if k >= 2:
                                W(scalar, f"d_otw{k % 2}", 16 * ((k - 2) // 2 + 1))
                        else:
                            W(scalar, "c_q8x", k + 1)
                    bank = pbank6[idx % NBK]
                    dst = (otst[k % 2][:, 512 * sp:512 * (sp + 1)] if k < K0 else
                           ot_res[:, S * k + 512 * sp:S * k + 512 * (sp + 1)])
                    I(scalar.activation(out=dst, in_=bank, func=AF.Copy,
                                        scale=recdall[:, k:k + 1]), "c_evO")
                if oth_ac[0] >= NTO and not wq8r_issued[0]:
                    wq8r_issued[0] = True
                    W(scalar, "d_atw0", finals["d_atw0"])
                    W(scalar, "d_atw1", finals["d_atw1"])
                    W(scalar, "d_wq8w0", finals["d_wq8w0"])
                    W(scalar, "d_wq8w1", finals["d_wq8w1"])
                    W(scalar, "c_mmB", NT * NPE)
                    I(scalar.dma_start(out=wq8r[0][:].rearrange("p (c f) -> p c f", c=NE),
                                       in_=wq8_hbm[:, :, 0:512].rearrange("c p f -> p c f")),
                      "d_wq8r0", 16)
                    if NPE > 1:
                        I(scalar.dma_start(out=wq8r[1][:].rearrange("p (c f) -> p c f", c=NE),
                                           in_=wq8_hbm[:, :, 512:1024].rearrange("c p f -> p c f")),
                          "d_wq8r1", 16)
            emit_oth_act(scalar, NTO)
            if not wq8r_issued[0]:
                wq8r_issued[0] = True
                W(scalar, "d_wq8w0", finals["d_wq8w0"])
                W(scalar, "d_wq8w1", finals["d_wq8w1"])
                W(scalar, "c_mmB", NT * NPE)
                I(scalar.dma_start(out=wq8r[0][:].rearrange("p (c f) -> p c f", c=NE),
                                   in_=wq8_hbm[:, :, 0:512].rearrange("c p f -> p c f")),
                  "d_wq8r0", 16)
                if NPE > 1:
                    I(scalar.dma_start(out=wq8r[1][:].rearrange("p (c f) -> p c f", c=NE),
                                       in_=wq8_hbm[:, :, 512:1024].rearrange("c p f -> p c f")),
                      "d_wq8r1", 16)
            for m in range(K0 // 2):
                W(scalar, "c_q8x", 2 * m + 2)
                W(scalar, "d_otw0", 16 * (m + 1))
                W(scalar, "d_otw1", 16 * (m + 1))
                I(scalar.dma_start(
                    out=ot_res[:, S * 2 * m:S * (2 * m + 2)].rearrange("p (k s) -> p k s", k=2),
                    in_=ot_hbm[2 * m:2 * m + 2].rearrange("k p s -> p k s")), "d_otrb", 16)
            for g in range(NPE):
                if g == 0:
                    W(scalar, "c_dsc", 11)
                for s_ in range(NT):
                    u = g * NT + s_
                    W(scalar, "c_mmEff", u + 1)
                    if u >= 2:
                        W(scalar, "c_f2", u - 1)
                    pf = pb6[:] if s_ % 2 == 0 else ps_big[:, 0:512]
                    I(scalar.activation(out=fbf[u % 2][:], in_=pf, func=AF.Copy,
                                        scale=bc3[:, 2:3]), "c_fb")
                gq = g + 2
                if gq < NPE:
                    W(scalar, "c_mmEff", (gq - 1) * NT)
                    I(scalar.dma_start(
                        out=wq8r[gq % 2][:].rearrange("p (c f) -> p c f", c=NE),
                        in_=wq8_hbm[:, :, 512 * gq:512 * (gq + 1)].rearrange("c p f -> p c f")),
                      f"d_wq8r{gq % 2}", 16)

        @block.vector
        def _(vector):
            invn = 1.0 / E
            I(vector.memset(ident_f[:], 0.0), "c_idv")
            I(vector.memset(ident_b[:], 0.0), "c_idv")
            I(vector.memset(ones_f[:], 1.0), "c_idv")
            I(vector.memset(ones_b[:], 1.0), "c_idv")
            for i in range(NT):
                if i < NE:
                    j = i
                    W(vector, "d_wfb", 16 * (j + 1))
                    I(vector.reduce_max(out=wab[:, j % 2:j % 2 + 1], in_=wfb[:],
                                        axis=AX.X, apply_absolute_value=True),
                      "c_wab")
                    W(vector, "c_wab", j + 1)
                    if j == 0:
                        iw = vector.tensor_copy(out=wf128[:], in_=wab[:, 0:1])
                    else:
                        iw = vector.tensor_tensor(out=wf128[:], in0=wf128[:],
                                                  in1=wab[:, j % 2:j % 2 + 1],
                                                  op=ALU.max)
                    if j == NE - 1:
                        iw.then_inc(sem["c_dsc"], 3)
                W(vector, "c_sacc", i + 1)
                if i >= 2:
                    W(vector, "c_trA", GA * (i - 1))
                    W(vector, "c_am", i - 1)
                I(vector.tensor_scalar(out=mstat[:, i % 2:i % 2 + 1],
                                       in0=s1[:, i % 2:i % 2 + 1], scalar1=invn,
                                       scalar2=None, op0=ALU.mult), "c_pp")
                I(vector.tensor_scalar(out=qstat[:, i % 2:i % 2 + 1],
                                       in0=s2[:, i % 2:i % 2 + 1], scalar1=invn,
                                       scalar2=1e-5, op0=ALU.mult,
                                       op1=ALU.add), "c_pp")
                W(vector, "c_pp", 4 * i + 2)
                I(vector.tensor_tensor(out=mm2[:, i % 2:i % 2 + 1],
                                       in0=mstat[:, i % 2:i % 2 + 1],
                                       in1=mstat[:, i % 2:i % 2 + 1],
                                       op=ALU.mult), "c_pp")
                W(vector, "c_pp", 4 * i + 3)
                I(vector.tensor_tensor(out=varst[:, i % 2:i % 2 + 1],
                                       in0=qstat[:, i % 2:i % 2 + 1],
                                       in1=mm2[:, i % 2:i % 2 + 1],
                                       op=ALU.subtract), "c_pp")
                W(vector, "c_rstd", i + 1)
                I(vector.reciprocal(out=rinv[:, i % 2:i % 2 + 1],
                                    in_=rstd[:, i % 2:i % 2 + 1]), "c_ri")
                W(vector, "c_ri", i + 1)
                I(vector.tensor_scalar(out=xnbuf[i % 2][:], in0=xbuf[i % 2][:],
                                       scalar1=mstat[:, i % 2:i % 2 + 1],
                                       scalar2=rinv[:, i % 2:i % 2 + 1],
                                       op0=ALU.subtract, op1=ALU.mult), "c_xn")
                W(vector, "c_xn", i + 1)
                I(vector.reduce_max(out=amb[:, i % 2:i % 2 + 1],
                                    in_=xnbuf[i % 2][:], axis=AX.X,
                                    apply_absolute_value=True), "c_am")
                W(vector, "c_am", i + 1)
                if i == 0:
                    I(vector.tensor_copy(out=run128[:], in_=amb[:, 0:1]), "c_mg")
                else:
                    I(vector.tensor_tensor(out=run128[:], in0=run128[:],
                                           in1=amb[:, i % 2:i % 2 + 1],
                                           op=ALU.max), "c_mg")
                if 0 <= i - 2 < min(FREEK, NT - 2):
                    sbv = i - 2
                    for pnl in range(NPE):
                        idx = sbv * NPE + pnl
                        W(vector, "c_mmB", idx + 1)
                        I(vector.tensor_copy(
                            out=K_ap(sbv, 512 * pnl, 512 * (pnl + 1)),
                            in_=bb[idx % NBB]), "c_evB")
            def emit_div(vector, j):
                if j == 0:
                    W(vector, "c_dsc", 7)
                W(vector, "d_wfb", 16 * (NE + j + 1))
                if j >= 2:
                    W(vector, f"d_wq8w{j % 2}", 16 * ((j - 2) // 2 + 1))
                I(vector.tensor_scalar(out=wq8st[j % 2][:], in0=wfb[:],
                                       scalar1=wsbr[:], scalar2=None,
                                       op0=ALU.mult), "c_wq8")
            W(vector, "d_p4", 16)
            I(vector.reduce_max(out=wmax[:], in_=prow2[:], axis=AX.X,
                                apply_absolute_value=True), "c_dsc")
            W(vector, "c_dsc", 4)
            I(vector.tensor_scalar_max(out=cat[:, 1:2], in0=wmax[:],
                                       scalar1=1e-6), "c_dsc")
            W(vector, "c_bcW", 1)
            I(vector.tensor_copy(out=wsb[:], in_=pb6[:, 0:1]), "c_dsc")
            W(vector, "c_dsc", 6)
            I(vector.reciprocal(out=wsbr[:], in_=wsb[:]), "c_dsc")
            done_early = max(min(FREEK, NT - 2), 0)
            for sb_ in range(NT):
                if sb_ < done_early:
                    continue
                if sb_ == FREEK:
                    W(vector, "c_xn", NT)
                    W(vector, "c_sacc", NT)
                    W(vector, "c_trA", GA * NT)
                    W(vector, "c_mg", NT)
                if sb_ >= FREEK and kaddr(sb_) == ((addr_of["wq8st0"] - R2) // CHB) * CHB:
                    W(vector, "d_wq8w0", finals["d_wq8w0"])
                    W(vector, "d_wq8w1", finals["d_wq8w1"])
                for pnl in range(NPE):
                    idx = sb_ * NPE + pnl
                    W(vector, "c_mmB", idx + 1)
                    I(vector.tensor_copy(
                        out=K_ap(sb_, 512 * pnl, 512 * (pnl + 1)),
                        in_=bb[idx % NBB]), "c_evB")
                if NT >= 14:
                    climit = FREEK + (addr_of["wq8st0"] - R2) // CHB
                    nproc = max(climit - done_early, 1)
                    if done_early <= sb_ < climit:
                        k0 = (sb_ - done_early) * NE // nproc
                        k1 = (sb_ - done_early + 1) * NE // nproc
                        for j in range(k0, k1):
                            emit_div(vector, j)
            if NT < 14:
                for j in range(NE):
                    emit_div(vector, j)
            for p in range(NQP):
                for sb_ in range(NT):
                    gi = p * NT + sb_
                    W(vector, "c_mmQ", gi + 1)
                    src = pbank[gi % 3]
                    I(vector.tensor_copy(out=q_pnl[:, QW * sb_:QW * (sb_ + 1)],
                                         in_=src[:, 0:QW]), "c_qev")
                for kl in range(QB):
                    k = p * QB + kl
                    for qq in range(NPS):
                        W(vector, "c_mmSq", NPS * k + qq + 1)
                        I(vector.reduce_max(out=pmq[:, qq:qq + 1],
                                            in_=ps_big[:, 512 * qq:512 * (qq + 1)],
                                            axis=AX.X), "c_pmx")
                    W(vector, "c_pmx", NPS * (k + 1))
                    I(vector.tensor_reduce(out=negm[:], in_=pmq[:], axis=AX.X,
                                           op=ALU.max, negate=True), "c_ngm")
                    W(vector, "c_ngm", k + 1)
                    I(vector.tensor_scalar(out=bmk[:, k % 2:k % 2 + 1], in0=negm[:],
                                           scalar1=0.125, scalar2=None,
                                           op0=ALU.mult), "c_bmk")
                    W(vector, "c_exph", 2 * (k + 1))
                    I(vector.tensor_tensor(out=dent[:, k % 2:k % 2 + 1],
                                           in0=den0[:, k % 2:k % 2 + 1],
                                           in1=den1[:, k % 2:k % 2 + 1],
                                           op=ALU.add), "c_den")
                    W(vector, "c_den", k + 1)
                    I(vector.reciprocal(out=recdall[:, k:k + 1],
                                        in_=dent[:, k % 2:k % 2 + 1]), "c_rcd")
                    emit_oth_dve(vector, 2 if k < 8 else 1)
            for f in range(NE):
                for sp in range(NPS):
                    idx = f * NPS + sp
                    W(vector, "c_mmV", idx + 1)
                    bank = pbank7[idx % NBK7]
                    I(vector.tensor_copy(
                        out=VT[:, S * f + 512 * sp:S * f + 512 * (sp + 1)],
                        in_=bank), "c_evV")
                emit_oth_dve(vector, 3 if f < NE // 2 else 4)
            emit_oth_dve(vector, NTO)
            W(vector, "d_p2", 16)
            I(vector.reduce_max(out=gmax[:], in_=prow1[:], axis=AX.X,
                                apply_absolute_value=True), "c_dsc")
            W(vector, "c_dsc", 8)
            I(vector.tensor_scalar_max(out=cat[:, 0:1], in0=gmax[:], scalar1=1e-6),
              "c_dsc")
            W(vector, "c_dsc", 9)
            I(vector.tensor_tensor(out=cat[:, 2:3], in0=cat[:, 0:1],
                                   in1=cat[:, 1:2], op=ALU.mult), "c_dsc")
            W(vector, "c_bcX", 1)
            I(vector.tensor_copy(out=bc3[:], in_=ps_big[:, BCX0:BCX0 + 3]), "c_dsc")
            W(vector, "c_dsc", 11)
            I(vector.reciprocal(out=bc3r[:], in_=bc3[:]), "c_dsc")
            W(vector, "c_dsc", 12)
            W(vector, "d_atw0", finals["d_atw0"])
            W(vector, "d_atw1", finals["d_atw1"])
            W(vector, "c_trAT", GK * NE)
            W(vector, "c_mmSq", NPS * NE)
            W(vector, "c_otha", NTO)
            for c in range(NE):
                I(vector.tensor_scalar(
                    out=xq8[:, S * c:S * (c + 1)],
                    in0=xnT[:, S * c:S * (c + 1)],
                    scalar1=bc3r[:, 0:1], scalar2=None, op0=ALU.mult), "c_q8x")
            for t in range(NPE):
                W(vector, "c_bcb", 1)
                I(vector.tensor_copy(out=bias_rep[:, 512 * t:512 * (t + 1)],
                                     in_=ps_big[:, 512 * t:512 * (t + 1)]),
                  "c_bcb2", 1 if t == NPE - 1 else 0)
            for g in range(NPE):
                for s_ in range(NT):
                    u = g * NT + s_
                    W(vector, "c_mmEat", u + 1)
                    W(vector, f"d_xe{u % 2}", 16 * (u // 2 + 1))
                    if u >= 2:
                        W(vector, f"d_out{u % 2}", 16 * ((u - 2) // 2 + 1))
                    if u < 2:
                        W(vector, "d_otw0", finals["d_otw0"])
                        W(vector, "d_otw1", finals["d_otw1"])
                    pa = pb4 if s_ % 2 == 0 else pb5
                    I(vector.tensor_tensor(out=outst[u % 2][:], in0=pa[:],
                                           in1=xst[u % 2][:], op=ALU.add), "c_o1")
                    W(vector, "c_fb", u + 1)
                    if u == 0:
                        W(vector, "c_bcb2", 1)
                    if u >= 2:
                        W(vector, "c_o2", u - 1)
                    I(vector.tensor_tensor(out=f2t[u % 2][:], in0=fbf[u % 2][:],
                                           in1=bias_rep[:, 512 * g:512 * (g + 1)],
                                           op=ALU.add), "c_f2")
                    W(vector, "c_f2", u + 1)
                    W(vector, "c_o1", u + 1)
                    I(vector.tensor_tensor(out=outst[u % 2][:], in0=outst[u % 2][:],
                                           in1=f2t[u % 2][:], op=ALU.add), "c_o2")

        @block.tensor
        def _(tensor):
            W(tensor, "c_idp", 1)
            CPGk = NE // max(NE // 4, 1)

            def emit_b_group(sb_):
                for pnl in range(NPE):
                    idx = sb_ * NPE + pnl
                    if idx >= NBB:
                        W(tensor, "c_evB", idx - NBB + 1)
                    dst = bb[idx % NBB]
                    for j in range(NE):
                        if sb_ == 0 and pnl == 0 and j % CPGk == 0:
                            W(tensor, f"d_wk{j // CPGk}", 16)
                        I(tensor.matmul(
                            dst,
                            lhsT=xnT[:, S * j + 128 * sb_:S * j + 128 * (sb_ + 1)],
                            rhs=Wk_bf[:, E * j + 512 * pnl:E * j + 512 * (pnl + 1)],
                            start=(j == 0), stop=(j == NE - 1)),
                          "c_mmB", 1 if j == NE - 1 else 0)

            for i in range(NT):
                W(tensor, "c_xn", i + 1)
                for g in range(GA):
                    idx = GA * i + g
                    if idx >= 2:
                        W(tensor, "c_evA", idx - 1)
                    dst = pb4 if idx % 2 == 0 else pb5
                    for t in range(TPG):
                        j = g * TPG + t
                        I(tensor.transpose(dst[:, 128 * t:128 * (t + 1)],
                                           in_=xnbuf[i % 2][:, 128 * j:128 * (j + 1)],
                                           identity=ident_f[:]),
                          "c_trA", 1 if t == TPG - 1 else 0)
                if 1 <= i <= min(FREEK, NT - 2) + 1:
                    W(tensor, "c_evA", GA * i)
                    emit_b_group(i - 1)
            W(tensor, "c_evA", GA * NT)
            W(tensor, "c_dsc", 5)
            W(tensor, "c_idv", 3)
            I(tensor.matmul(pb6[:, 0:1], lhsT=ones_f[:], rhs=cat[:, 1:2],
                            start=True, stop=True), "c_bcW")
            for sbr_ in range(min(FREEK, NT - 2) + 1, NT):
                emit_b_group(sbr_)
            W(tensor, "c_evB", NT * NPE)
            W(tensor, "c_dsc", 6)
            W(tensor, "c_idp", 2)
            for p in range(NQP):
                W(tensor, f"d_wq{p % 2}", 16 * (p // 2 + 1))
                for sb_ in range(NT):
                    gi = p * NT + sb_
                    if gi >= 3:
                        W(tensor, "c_qev", gi - 2)
                    dst = pbank[gi % 3]
                    for j in range(NE):
                        I(tensor.matmul(
                            dst[:, 0:QW],
                            lhsT=xnT[:, S * j + 128 * sb_:S * j + 128 * (sb_ + 1)],
                            rhs=wqbf[p % 2][:, QW * j:QW * (j + 1)],
                            start=(j == 0), stop=(j == NE - 1)),
                          "c_mmQ", 1 if j == NE - 1 else 0)
                for kl in range(QB):
                    k = p * QB + kl
                    for qq in range(NPS):
                        if k >= 1:
                            half = 1 if (NPS > 1 and qq < NPS // 2) else 2
                            W(tensor, "c_exph", 2 * (k - 1) + half)
                        for s_ch in range(NT):
                            if kl == 0 and qq == 0:
                                W(tensor, "c_qev", p * NT + s_ch + 1)
                            I(tensor.matmul(
                                ps_big[:, 512 * qq:512 * (qq + 1)],
                                lhsT=q_pnl[:, QW * s_ch + 128 * kl:QW * s_ch + 128 * (kl + 1)],
                                rhs=K_ap(s_ch, 512 * qq, 512 * (qq + 1)),
                                start=(s_ch == 0), stop=(s_ch == NT - 1)),
                              "c_mmSq", 1 if s_ch == NT - 1 else 0)
                    for g2 in range(GK):
                        tcnt = min(8, TK - 8 * g2)
                        idxAT = GK * k + g2
                        need = 2 * k + (2 if GK == 1 else g2 + 1)
                        W(tensor, "c_exph", need)
                        W(tensor, "c_evAT", idxAT)
                        for t in range(tcnt):
                            j = g2 * 8 + t
                            I(tensor.transpose(pbt[:, 128 * t:128 * (t + 1)],
                                               in_=A_bf[k % 2][:, 128 * j:128 * (j + 1)],
                                               identity=ident_b[:]),
                              "c_trAT", 1 if t == tcnt - 1 else 0)
            W(tensor, "c_qev", NQP * NT)
            W(tensor, "c_dsc", 6)
            W(tensor, "c_exph", 2 * NE)
            W(tensor, "c_rcd", NE)
            for f in range(NE):
                W(tensor, f"d_wv{f % 4}", 16 * (f // 4 + 1))
                for sp in range(NPS):
                    idx = f * NPS + sp
                    if idx >= NBK7:
                        W(tensor, "c_evV", idx - NBK7 + 1)
                    bank = pbank7[idx % NBK7]
                    for j in range(NE):
                        I(tensor.matmul(
                            bank, lhsT=wvbf[f % 4][:, 128 * j:128 * (j + 1)],
                            rhs=xnT[:, S * j + 512 * sp:S * j + 512 * (sp + 1)],
                            start=(j == 0), stop=(j == NE - 1)),
                          "c_mmV", 1 if j == NE - 1 else 0)
            W(tensor, "c_evV", NE * NPS)
            for k in range(NE):
                if k == K0:
                    W(tensor, "c_dsc", 10)
                    W(tensor, "c_exph", 2 * NE)
                    I(tensor.matmul(ps_big[:, BCX0:BCX0 + 3], lhsT=ones_f[:], rhs=cat[:],
                                    start=True, stop=True), "c_bcX")
                W(tensor, f"d_atr{k % 2}", 16 * (k // 2 + 1))
                for sp in range(NPS):
                    idx = k * NPS + sp
                    if idx >= NBK:
                        W(tensor, "c_evO", idx - NBK + 1)
                    bank = pbank6[idx % NBK]
                    for c in range(NE):
                        I(tensor.matmul(
                            bank, lhsT=atb[k % 2][:, 128 * c:128 * (c + 1)],
                            rhs=VT[:, S * c + 512 * sp:S * c + 512 * (sp + 1)],
                            start=(c == 0), stop=(c == NE - 1)),
                          "c_mmO", 1 if c == NE - 1 else 0)
            W(tensor, "c_idv", 4)
            W(tensor, "d_bias", 16)
            W(tensor, "c_evO", NE * NPS)
            W(tensor, "c_dsc", 11)
            for t in range(NPE):
                I(tensor.matmul(ps_big[:, 512 * t:512 * (t + 1)], lhsT=ones_b[:],
                                rhs=brow[:, 512 * t:512 * (t + 1)], start=True,
                                stop=True), "c_bcb", 1 if t == NPE - 1 else 0)
            W(tensor, "c_q8x", NE)
            W(tensor, "d_otrb", 16 * (K0 // 2))
            for g in range(NPE):
                W(tensor, f"d_wo{g % 2}", 16 * (g // 2 + 1))
                W(tensor, f"d_wq8r{g % 2}", 16 * (g // 2 + 1))
                for s_ in range(NT):
                    u = g * NT + s_
                    pa = pb4 if s_ % 2 == 0 else pb5
                    if u >= 2:
                        W(tensor, "c_o1", u - 1)
                    for c in range(NE):
                        I(tensor.matmul(
                            pa[:],
                            lhsT=ot_res[:, S * c + 128 * s_:S * c + 128 * (s_ + 1)],
                            rhs=wobf[g % 2][:, 512 * c:512 * (c + 1)],
                            start=(c == 0), stop=(c == NE - 1)),
                          "c_mmEat", 1 if c == NE - 1 else 0)
                    pf = pb6[:] if s_ % 2 == 0 else ps_big[:, 0:512]
                    if u >= 2:
                        W(tensor, "c_fb", u - 1)
                    elif u == 0:
                        W(tensor, "c_bcb2", 1)
                    for d in range(NE // 2):
                        I(tensor.matmul(
                            pf,
                            lhsT=xq8[:].rearrange("p (c s) -> p c s", c=NE)[:, 2 * d:2 * d + 2, 128 * s_:128 * (s_ + 1)],
                            rhs=wq8r[g % 2][:].rearrange("p (c f) -> p c f", c=NE)[:, 2 * d:2 * d + 2, :],
                            start=(d == 0), stop=(d == NE // 2 - 1),
                            perf_mode=mybir.MatmulPerfMode.DoubleRow),
                          "c_mmEff", 1 if d == NE // 2 - 1 else 0)

        @block.gpsimd
        def _(gpsimd):
            W(gpsimd, "c_idv", 2)
            I(gpsimd.affine_select(out=ident_f[:], in_=ident_f[:],
                                   compare_op=ALU.not_equal, fill=1.0, base=0,
                                   pattern=[[-1, 128]], channel_multiplier=1), "c_idp")
            I(gpsimd.affine_select(out=ident_b[:], in_=ident_b[:],
                                   compare_op=ALU.not_equal, fill=1.0, base=0,
                                   pattern=[[-1, 128]], channel_multiplier=1), "c_idp")
            WKG = max(NE // 4, 1)
            CPG = NE // WKG
            for m in range(WKG):
                I(gpsimd.dma_start(
                    out=Wk_bf[:, CPG * m * E:CPG * (m + 1) * E].rearrange("p (c f) -> p c f", c=CPG),
                    in_=wk_c[CPG * m:CPG * (m + 1)].rearrange("c p f -> p c f")),
                  f"d_wk{m}", 16)
            W(gpsimd, "c_dsc", 3)
            I(gpsimd.dma_start(out=pscr2[:, None], in_=wf128[:, 0:1]), "d_p3", 16)
            W(gpsimd, "d_p3", 16)
            I(gpsimd.dma_start(out=prow2[:], in_=pscr2[None, :]), "d_p4", 16)
            for j in range(NE):
                W(gpsimd, "c_wq8", j + 1)
                I(gpsimd.dma_start(out=wq8_hbm[j], in_=wq8st[j % 2][:]), f"d_wq8w{j % 2}", 16)
            W(gpsimd, "c_mmB", NT * NPE)
            for p in range(min(2, NQP)):
                I(gpsimd.dma_start(out=wqbf[p % 2][:].rearrange("p (c f) -> p c f", c=NE),
                                   in_=wq_c[:, :, QW * p:QW * (p + 1)]), f"d_wq{p % 2}", 16)
            for p in range(NQP):
                pq = p + 2
                if pq < NQP:
                    W(gpsimd, "c_mmQ", (pq - 1) * NT)
                    I(gpsimd.dma_start(out=wqbf[pq % 2][:].rearrange("p (c f) -> p c f", c=NE),
                                       in_=wq_c[:, :, QW * pq:QW * (pq + 1)]), f"d_wq{pq % 2}", 16)
                emit_oth_load(gpsimd, 8 if p < 2 else 4)
            W(gpsimd, "c_mmQ", NQP * NT)
            for f in range(min(4, NE)):
                I(gpsimd.dma_start(out=wvbf[f % 4][:].rearrange("p (c f) -> p c f", c=NE),
                                   in_=wv_c[:, :, 128 * f:128 * (f + 1)]), f"d_wv{f % 4}", 16)
            for f in range(NE):
                fq = f + 4
                if fq < NE:
                    W(gpsimd, "c_mmV", (fq - 3) * NPS)
                    I(gpsimd.dma_start(out=wvbf[fq % 4][:].rearrange("p (c f) -> p c f", c=NE),
                                       in_=wv_c[:, :, 128 * fq:128 * (fq + 1)]), f"d_wv{fq % 4}", 16)
                emit_oth_load(gpsimd, 3 if f < NE // 2 else 4)
            emit_oth_load(gpsimd, NTO)
            W(gpsimd, "c_mg", NT)
            W(gpsimd, "c_othmg", max(NB - 1, 1))
            I(gpsimd.dma_start(out=pscr1[:, None], in_=run128[:, 0:1]), "d_p1", 16)
            W(gpsimd, "d_p1", 16)
            I(gpsimd.dma_start(out=prow1[:], in_=pscr1[None, :]), "d_p2", 16)
            W(gpsimd, "c_mmO", NE * NPS)
            I(gpsimd.dma_start(out=brow[:], in_=bias_in[None, :]), "d_bias", 16)
            for g in range(min(2, NPE)):
                I(gpsimd.dma_start(out=wobf[g % 2][:].rearrange("p (c f) -> p c f", c=NE),
                                   in_=wo_c[:, :, 512 * g:512 * (g + 1)]), f"d_wo{g % 2}", 16)
            for gq in range(2, NPE):
                W(gpsimd, "c_mmEat", (gq - 1) * NT)
                I(gpsimd.dma_start(
                    out=wobf[gq % 2][:].rearrange("p (c f) -> p c f", c=NE),
                    in_=wo_c[:, :, 512 * gq:512 * (gq + 1)]), f"d_wo{gq % 2}", 16)

    for name, val in waits:
        assert val <= finals[name], f"wait {name}>={val} exceeds final {finals[name]}"
    return nc


def kernel(L0_attn_x, L0_attn_Wq, L0_attn_Wk, L0_attn_Wv, L0_attn_Wo,
           L0_ffn_weight, L0_ffn_bias):
    x = np.asarray(L0_attn_x)
    B, S, E = x.shape

    def f32(a):
        return np.ascontiguousarray(np.asarray(a), dtype=np.float32)

    def bf16(a):
        a = np.asarray(a)
        if a.dtype == np.uint16:
            a = a.view(ml_dtypes.bfloat16)
        assert a.dtype == ml_dtypes.bfloat16
        return a

    wq, wk, wv, wo = f32(L0_attn_Wq), f32(L0_attn_Wk), f32(L0_attn_Wv), f32(L0_attn_Wo)
    wft = np.ascontiguousarray(bf16(L0_ffn_weight).T)
    bias = np.ascontiguousarray(bf16(L0_ffn_bias))

    nc = build_program(S, E, n_cores=B)
    xs_f32 = [np.ascontiguousarray(x[c], dtype=np.float32) for c in range(B)]
    in_maps = [{"x": xs_f32[c],
                "xo": np.concatenate([xs_f32[d] for d in range(B) if d != c], axis=0),
                "wq": wq, "wk": wk, "wv": wv, "wo": wo,
                "wft": wft, "bias": bias} for c in range(B)]
    res = run_bass_kernel_spmd(nc, in_maps, core_ids=list(range(B)))
    out = np.stack([np.asarray(res.results[c]["out"]) for c in range(B)], axis=0)
    return out.astype(np.float32, copy=False)


# version 29
# speedup vs baseline: 1.0001x; 1.0001x over previous
import numpy as np
import ml_dtypes
import concourse.bass as bass
import concourse.mybir as mybir
from concourse.bass_utils import run_bass_kernel_spmd

F32 = mybir.dt.float32
BF16 = mybir.dt.bfloat16
FP8 = mybir.dt.float8e4
ALU = mybir.AluOpType
AF = mybir.ActivationFunctionType
AX = mybir.AxisListType
KIB = 1024


def build_program(S=2048, E=2048, n_cores=8):
    assert S == E
    NT = S // 128
    NE = E // 128
    GA = max(NE // 4, 1)
    TPG = NE // GA
    NPS = max(S // 512, 1)
    NPE = max(E // 512, 1)
    QW = 256
    NQP = E // QW
    QB = QW // 128
    U = NT * NPE
    SH = S // 2
    TK = S // 128
    GK = (TK + 7) // 8
    NB = n_cores
    NTO = (NB - 1) * NT

    nc = bass.Bass("TRN2", target_bir_lowering=False)

    x_in = nc.dram_tensor("x", [S, E], F32, kind="ExternalInput").ap()
    xo_in = nc.dram_tensor("xo", [NTO * 128, E], F32, kind="ExternalInput").ap()
    wq_in = nc.dram_tensor("wq", [E, E], F32, kind="ExternalInput").ap()
    wk_in = nc.dram_tensor("wk", [E, E], F32, kind="ExternalInput").ap()
    wv_in = nc.dram_tensor("wv", [E, E], F32, kind="ExternalInput").ap()
    wo_in = nc.dram_tensor("wo", [E, E], F32, kind="ExternalInput").ap()
    wft_in = nc.dram_tensor("wft", [E, E], BF16, kind="ExternalInput").ap()
    bias_in = nc.dram_tensor("bias", [E], BF16, kind="ExternalInput").ap()
    out_hbm = nc.dram_tensor("out", [S, E], F32, kind="ExternalOutput").ap()

    xn_hbm = nc.dram_tensor("xn_scr", [S, E], F32, kind="Internal").ap()
    at_hbm = nc.dram_tensor("at_scr", [NE, 128, S], BF16, kind="Internal").ap()
    ot_hbm = nc.dram_tensor("ot_scr", [NE, 128, S], BF16, kind="Internal").ap()
    wq8_hbm = nc.dram_tensor("wq8_scr", [NE, 128, E], FP8, kind="Internal").ap()
    pscr1 = nc.dram_tensor("pscr1", [128], F32, kind="Internal").ap()
    pscr2 = nc.dram_tensor("pscr2", [128], F32, kind="Internal").ap()

    x_t = x_in.rearrange("(t p) e -> t p e", p=128)
    xo_t = xo_in.rearrange("(t p) e -> t p e", p=128)
    xn_t = xn_hbm.rearrange("(t p) e -> t p e", p=128)
    out_t = out_hbm.rearrange("(t p) e -> t p e", p=128)
    wq_c = wq_in.rearrange("(c p) f -> p c f", p=128)
    wk_c = wk_in.rearrange("(c p) f -> c p f", p=128)
    wv_c = wv_in.rearrange("(c p) f -> p c f", p=128)
    wo_c = wo_in.rearrange("(c p) f -> p c f", p=128)
    wft_c = wft_in.rearrange("(c p) f -> c p f", p=128)

    specs = {}

    def sb(name, shape, dtype):
        specs[name] = (shape, dtype)
        return name

    addr_of = {}

    def place(names, base):
        off = base
        for n in names:
            shape, dtype = specs[n]
            per = 1
            for d in shape[1:]:
                per *= d
            per *= mybir.dt.size(dtype)
            off = (off + 63) & ~63
            addr_of[n] = off
            off += per
        return off

    names_small = []
    for n, shp, dt_ in [
        ("ident_f", [128, 128], F32), ("ident_b", [128, 128], BF16),
        ("ones_f", [1, 128], F32), ("ones_b", [1, 128], BF16),
        ("cat", [1, 3], F32), ("bc3", [128, 3], F32),
        ("run128", [128, 1], F32), ("wf128", [128, 1], F32),
        ("amb", [128, 2], F32), ("wab", [128, 2], F32),
        ("s1", [128, 2], F32), ("s2", [128, 2], F32),
        ("mstat", [128, 2], F32), ("qstat", [128, 2], F32),
        ("mm2", [128, 2], F32), ("varst", [128, 2], F32),
        ("rstd", [128, 2], F32), ("rinv", [128, 2], F32),
        ("bc3r", [128, 3], F32), ("prow1", [1, 128], F32),
        ("prow2", [1, 128], F32), ("lmax", [1, 1], F32),
        ("wmax", [1, 1], F32), ("gmax", [1, 1], F32),
        ("pmq", [128, max(NPS, 1)], F32), ("negm", [128, 1], F32),
        ("bmk", [128, 2], F32), ("den0", [128, 2], F32),
        ("den1", [128, 2], F32), ("dent", [128, 2], F32),
        ("recdall", [128, NE], F32),
        ("bias_rep", [128, E], BF16), ("wfb", [128, E], BF16),
        ("wsb", [128, 1], F32),
        ("wsbr", [128, 1], F32), ("oMx", [128, NT], F32),
        ("omn", [128, NT], F32), ("osm", [128, NT], F32),
        ("osq", [128, NT], F32), ("om", [128, NT], F32),
        ("oq", [128, NT], F32), ("omm", [128, NT], F32),
        ("ovar", [128, NT], F32), ("orstd", [128, NT], F32),
        ("orinv", [128, NT], F32), ("od1", [128, NT], F32),
        ("odm", [128, NT], F32), ("opx", [128, 1], F32),
    ]:
        names_small.append(sb(n, shp, dt_))
    sb("xnT", [128, NE * S], BF16)
    sb("ot_res", [128, NE * S], BF16)
    for i in range(2):
        sb(f"xbuf{i}", [128, E], F32)
    for i in range(2):
        sb(f"xnbuf{i}", [128, E], F32)
        sb(f"wq8st{i}", [128, E], FP8)
    sb("K_bf", [128, NT * E + (0 if S >= 2048 else 12 * E)], BF16)
    sb("VT", [128, NE * S], BF16)
    for i in range(2):
        sb(f"wobf{i}", [128, NE * 512], BF16)
    for i in range(6):
        sb(f"obuf{i}", [128, E], BF16)
    sb("brow", [1, E], BF16)
    sb("Wk_bf", [128, NE * E], BF16)
    for i in range(2):
        sb(f"wqbf{i}", [128, NE * QW], BF16)
    sb("q_pnl", [128, NT * QW], BF16)
    for i in range(2):
        sb(f"A_bf{i}", [128, S], BF16)
        sb(f"at_st{i}", [128, S], BF16)
        sb(f"atb{i}", [128, S], BF16)
    for i in range(4):
        sb(f"wvbf{i}", [128, NE * 128], BF16)
    for i in range(2):
        sb(f"otst{i}", [128, S], BF16)
    sb("xq8", [128, NE * S], FP8)
    for i in range(2):
        sb(f"wq8r{i}", [128, NE * 512], FP8)
        sb(f"fbf{i}", [128, 512], BF16)
        sb(f"f2t{i}", [128, 512], BF16)
        sb(f"xst{i}", [128, 512], F32)
        sb(f"outst{i}", [128, 512], F32)

    end0 = place(names_small, 0)
    R1 = (end0 + 1023) & ~1023
    e1 = max(place(["xnT"], R1), place(["ot_res"], R1))
    R2 = (e1 + 1023) & ~1023
    e2 = max(place([f"xbuf{i}" for i in range(2)]
                   + [f"xnbuf{i}" for i in range(2)]
                   + [f"wq8st{i}" for i in range(2)], R2),
             place(["K_bf"], R2), place(["VT"], R2),
             place(["wobf0", "wobf1"], R2))
    CHB = E * 2
    AT_END = ((addr_of["wq8st1"] + E - R2) + 63) & ~63
    FREEK = min(NT, max((NT * CHB - AT_END) // CHB, 0))
    e2 = max(e2, R2 + AT_END + FREEK * CHB)

    def kaddr(c):
        if c < FREEK:
            return AT_END + CHB * c
        return CHB * (c - FREEK)

    def K_ap(c, lo, hi):
        base = kaddr(c) // 2
        return K_bf[:, base + lo:base + hi]
    R3 = (e2 + 1023) & ~1023
    e3 = max(
        place(["Wk_bf"], R3),
        place(["wqbf0", "wqbf1", "q_pnl", "A_bf0", "A_bf1", "at_st0", "at_st1"], R3),
        place(["wvbf0", "wvbf1", "wvbf2", "wvbf3"], R3),
        place(["otst0", "otst1"], R3 + 8 * KIB),
        place(["atb0", "atb1"], R3),
        place(["xq8"], R3 + 16 * KIB),
        place(["wq8r0", "wq8r1"], R3 + 48 * KIB),
        place(["obuf0", "obuf1", "obuf2", "obuf3", "obuf4", "obuf5"], R3 + 40 * KIB),
        place(["brow"], (R2 + 32 * KIB) if S >= 2048 else (R3 + 120 * KIB)),
        place(["fbf0", "fbf1", "f2t0", "f2t1", "xst0", "xst1", "outst0", "outst1"], R3),
    )
    total = (e3 + 63) & ~63
    assert total <= 206 * KIB, f"sbuf over budget: {total}"
    backing = nc.alloc_sbuf_tensor("sbuf_backing", [128, total], mybir.dt.uint8)
    base = nc.lookup_mloc(backing).addr
    T = {}
    for n, (shp, dt_) in specs.items():
        T[n] = nc.alloc_sbuf_tensor_at(n, shp, dt_, offset=base + addr_of[n])

    G = T
    ident_f, ident_b, ones_f, ones_b = G["ident_f"], G["ident_b"], G["ones_f"], G["ones_b"]
    cat, bc3, run128, wf128 = G["cat"], G["bc3"], G["run128"], G["wf128"]
    amb, wab, s1, s2 = G["amb"], G["wab"], G["s1"], G["s2"]
    mstat, qstat, mm2, varst = G["mstat"], G["qstat"], G["mm2"], G["varst"]
    rstd, rinv, bc3r = G["rstd"], G["rinv"], G["bc3r"]
    prow1, prow2, lmax, wmax, gmax = G["prow1"], G["prow2"], G["lmax"], G["wmax"], G["gmax"]
    pmq, negm, bmk = G["pmq"], G["negm"], G["bmk"]
    den0, den1, dent, recdall = G["den0"], G["den1"], G["dent"], G["recdall"]
    brow, bias_rep, wfb, wsb, wsbr = G["brow"], G["bias_rep"], G["wfb"], G["wsb"], G["wsbr"]
    oMx, omn, osm, osq = G["oMx"], G["omn"], G["osm"], G["osq"]
    om, oq, omm, ovar = G["om"], G["oq"], G["omm"], G["ovar"]
    orstd, orinv, od1, odm, opx = G["orstd"], G["orinv"], G["od1"], G["odm"], G["opx"]

    xnT, ot_res, K_bf, VT = T["xnT"], T["ot_res"], T["K_bf"], T["VT"]
    Wk_bf, q_pnl, xq8 = T["Wk_bf"], T["q_pnl"], T["xq8"]
    xbuf = [T[f"xbuf{i}"] for i in range(2)]
    obuf = [T[f"obuf{i}"] for i in range(6)] + [T["bias_rep"], T["wfb"]]
    otst = [T[f"otst{i}"] for i in range(2)]
    NOB = 8
    xnbuf = [T[f"xnbuf{i}"] for i in range(2)]
    wq8st = [T[f"wq8st{i}"] for i in range(2)]
    wobf = [T[f"wobf{i}"] for i in range(2)]
    wqbf = [T[f"wqbf{i}"] for i in range(2)]
    A_bf = [T[f"A_bf{i}"] for i in range(2)]
    at_st = [T[f"at_st{i}"] for i in range(2)]
    wvbf = [T[f"wvbf{i}"] for i in range(4)]
    atb = [T[f"atb{i}"] for i in range(2)]
    wq8r = [T[f"wq8r{i}"] for i in range(2)]
    fbf = [T[f"fbf{i}"] for i in range(2)]
    f2t = [T[f"f2t{i}"] for i in range(2)]
    xst = [T[f"xst{i}"] for i in range(2)]
    outst = [T[f"outst{i}"] for i in range(2)]

    import contextlib
    est = contextlib.ExitStack()
    ps_big = est.enter_context(nc.psum_tensor("ps_big", [128, min(S, 2048)], F32))
    pb4 = est.enter_context(nc.psum_tensor("pb4", [128, 512], F32))
    pb5 = est.enter_context(nc.psum_tensor("pb5", [128, 512], F32))
    pb6 = est.enter_context(nc.psum_tensor("pb6", [128, 512], F32))
    pbt = est.enter_context(nc.psum_tensor("pbt", [128, 1024], BF16))
    pbank = [pb4, pb5, pb6]
    BCX0 = min(S, 2048) - 512
    K0 = min(6, max(NE - 2, 2)) & ~1
    pbank6 = [pb4[:], pb5[:], pb6[:], ps_big[:, 0:512], ps_big[:, 512:1024],
              ps_big[:, 1024:1536]] if S >= 2048 else [pb4[:], pb5[:], pb6[:]]
    NBK = len(pbank6)
    pbank7 = pbank6 + [ps_big[:, BCX0:BCX0 + 512]]
    NBK7 = len(pbank7)
    bb = ([ps_big[:, 512 * k:512 * (k + 1)] for k in range(4)] if S >= 2048
          else [ps_big[:, 0:512]])
    NBB = len(bb)

    finals = {}
    sem = {}

    def mksem(name, n_slots, final_counts):
        for s_ in range(n_slots):
            key = f"{name}{s_}" if n_slots > 1 else name
            fc = final_counts[s_] if isinstance(final_counts, list) else final_counts
            finals[key] = fc
            sem[key] = est.enter_context(nc.semaphore(key))

    def slotkey(name, slot, n_slots):
        return f"{name}{slot}" if n_slots > 1 else name

    def ring_finals(n_items, n_slots, inc=16):
        return [inc * len([i for i in range(n_items) if i % n_slots == s_])
                for s_ in range(n_slots)]

    mksem("d_x", 2, ring_finals(NT, 2))
    mksem("d_wfb", 1, 16 * 2 * NE)
    mksem("c_wab", 1, NE)
    mksem("d_p1", 1, 16)
    mksem("d_p2", 1, 16)
    mksem("d_p3", 1, 16)
    mksem("d_p4", 1, 16)
    mksem("d_bias", 1, 16)
    mksem("d_oth", 8, ring_finals(NTO, 8))
    mksem("d_wq8w", 2, ring_finals(NE, 2))
    for j in range(max(NE // 4, 1)):
        mksem(f"d_wk{j}", 1, 16)
    mksem("d_wq", 2, ring_finals(NQP, 2))
    mksem("d_atw", 2, ring_finals(NE, 2))
    mksem("d_atr", 2, ring_finals(NE, 2))
    mksem("d_wv", 4, ring_finals(NE, 4))
    mksem("d_otw", 2, ring_finals(K0, 2))
    mksem("d_otrb", 1, 16 * (K0 // 2))
    mksem("d_wq8r", 2, ring_finals(NPE, 2))
    mksem("d_wo", 2, ring_finals(NPE, 2))
    mksem("d_xe", 2, ring_finals(U, 2))
    mksem("d_out", 2, ring_finals(U, 2))
    for name, fc in [
        ("c_idv", 4), ("c_idp", 2), ("c_sacc", NT), ("c_scp", NT),
        ("c_pp", 4 * NT),
        ("c_rstd", NT), ("c_ri", NT), ("c_bc3r", 1), ("c_xn", NT), ("c_trA", GA * NT), ("c_evA", GA * NT),
        ("c_am", NT), ("c_mg", NT), ("c_dsc", 12),
        ("c_bcW", 1), ("c_bcX", 1), ("c_wq8", NE),
        ("c_oe", 9 * max(NB - 1, 1)), ("c_othv", NTO), ("c_otha", NTO),
        ("c_ocp", NTO),
        ("c_ovar", max(NB - 1, 1)),
        ("c_orst", max(NB - 1, 1)), ("c_othmg", max(NB - 1, 1)),
        ("c_mmB", NT * NPE), ("c_evB", NT * NPE), ("c_mmQ", NQP * NT),
        ("c_qev", NQP * NT), ("c_mmSq", NPS * NE), ("c_pmx", NPS * NE),
        ("c_ngm", NE), ("c_bmk", NE), ("c_den", NE), ("c_rcd", NE),
        ("c_exph", 2 * NE), ("c_trAT", GK * NE), ("c_evAT", GK * NE),
        ("c_mmV", NE * NPS), ("c_evV", NE * NPS), ("c_mmO", NE * NPS),
        ("c_evO", NE * NPS), ("c_q8x", NE),
        ("c_bcb", 1), ("c_bcb2", 1), ("c_mmEat", U), ("c_mmEff", U),
        ("c_fb", U), ("c_f2", U), ("c_o1", U), ("c_o2", U),
    ]:
        mksem(name, 1, fc)
    waits = []

    def W(eng, name, val):
        if val <= 0:
            return
        waits.append((name, val))
        eng.wait_ge(sem[name], val)

    def I(inst, name, n=1):
        if n > 0:
            inst.then_inc(sem[name], n)
        return inst

    def ck(t, j, w):
        return t[:, j * w:(j + 1) * w]

    oth_ld = [0]
    oth_dv = [0]
    oth_ac = [0]
    oth_oe = [0]
    wq8r_issued = [False]

    def emit_oth_load(eng, n):
        for _ in range(n):
            t = oth_ld[0]
            if t >= NTO:
                return
            oth_ld[0] += 1
            if t >= NOB:
                W(eng, "c_othv", t - NOB + 1)
                W(eng, "c_otha", t - NOB + 1)
            if t % NOB == 7 and t // NOB == 0:
                W(eng, "c_wq8", NE)
            I(eng.dma_start(out=obuf[t % NOB][:], in_=xo_t[t]), f"d_oth{t % NOB}", 16)

    def emit_oth_dve(vector, n):
        invn_ = 1.0 / E
        for _ in range(n):
            t = oth_dv[0]
            if t >= NTO:
                return
            oth_dv[0] += 1
            b, lt = divmod(t, NT)
            W(vector, f"d_oth{t % NOB}", 16 * (t // NOB + 1))
            vector.tensor_reduce(out=oMx[:, lt:lt + 1], in_=obuf[t % NOB][:],
                                 axis=AX.X, op=ALU.max)
            I(vector.tensor_reduce(out=omn[:, lt:lt + 1], in_=obuf[t % NOB][:],
                                   axis=AX.X, op=ALU.min), "c_othv")
            if lt == NT - 1:
                base = oth_oe[0]
                W(vector, "c_otha", (b + 1) * NT)
                W(vector, "c_othv", (b + 1) * NT)
                I(vector.tensor_scalar(out=om[:], in0=osm[:], scalar1=invn_,
                                       scalar2=None, op0=ALU.mult), "c_oe")
                I(vector.tensor_scalar(out=oq[:], in0=osq[:], scalar1=invn_,
                                       scalar2=1e-5, op0=ALU.mult,
                                       op1=ALU.add), "c_oe")
                W(vector, "c_oe", base + 1)
                I(vector.tensor_tensor(out=omm[:], in0=om[:], in1=om[:],
                                       op=ALU.mult), "c_oe")
                W(vector, "c_oe", base + 3)
                I(vector.tensor_tensor(out=ovar[:], in0=oq[:], in1=omm[:],
                                       op=ALU.subtract), "c_ovar")
                W(vector, "c_orst", b + 1)
                I(vector.reciprocal(out=orinv[:], in_=orstd[:]), "c_oe")
                I(vector.tensor_tensor(out=od1[:], in0=oMx[:], in1=om[:],
                                       op=ALU.subtract), "c_oe")
                I(vector.tensor_tensor(out=odm[:], in0=om[:], in1=omn[:],
                                       op=ALU.subtract), "c_oe")
                W(vector, "c_oe", base + 6)
                I(vector.tensor_tensor(out=odm[:], in0=od1[:], in1=odm[:],
                                       op=ALU.max), "c_oe")
                W(vector, "c_oe", base + 7)
                I(vector.tensor_tensor(out=odm[:], in0=odm[:], in1=orinv[:],
                                       op=ALU.mult), "c_oe")
                W(vector, "c_oe", base + 8)
                I(vector.tensor_reduce(out=opx[:], in_=odm[:], axis=AX.X,
                                       op=ALU.max), "c_oe")
                W(vector, "c_oe", base + 9)
                I(vector.tensor_tensor(out=run128[:], in0=run128[:], in1=opx[:],
                                       op=ALU.max), "c_othmg")
                oth_oe[0] = base + 9

    def emit_oth_act(scalar, n):
        for _ in range(n):
            t = oth_ac[0]
            if t >= NTO:
                return
            oth_ac[0] += 1
            b, lt = divmod(t, NT)
            W(scalar, f"d_oth{t % NOB}", 16 * (t // NOB + 1))
            W(scalar, "c_othv", t + 1)
            I(scalar.activation(out=obuf[t % NOB][:], in_=obuf[t % NOB][:], func=AF.Copy,
                                accum_out=osm[:, lt:lt + 1]), "c_ocp")
            W(scalar, "c_ocp", t + 1)
            I(scalar.activation(out=obuf[t % NOB][:], in_=obuf[t % NOB][:],
                                func=AF.Square, accum_out=osq[:, lt:lt + 1]),
              "c_otha")
            if lt == NT - 1:
                W(scalar, "c_ovar", b + 1)
                I(scalar.activation(out=orstd[:], in_=ovar[:], func=AF.Sqrt),
                  "c_orst")

    with nc.Block() as block:

        @block.sync
        def _(sync):
            for i in range(NT):
                if i >= 2:
                    W(sync, "c_xn", i - 1)
                    W(sync, "c_sacc", i - 1)
                I(sync.dma_start(out=xbuf[i % 2][:], in_=x_t[i]), f"d_x{i % 2}", 16)
                if i < NE:
                    if i >= 1:
                        W(sync, "c_wab", i)
                    I(sync.dma_start(out=wfb[:], in_=wft_c[i]), "d_wfb", 16)
            for k in range(NE):
                W(sync, "c_evAT", GK * (k + 1))
                I(sync.dma_start(out=at_hbm[k], in_=at_st[k % 2][:]), f"d_atw{k % 2}", 16)
            for k in range(NE):
                if k == 0:
                    W(sync, "c_mmV", NE * NPS)
                if k >= 2:
                    W(sync, "c_mmO", (k - 1) * NPS)
                W(sync, f"d_atw{k % 2}", 16 * (k // 2 + 1))
                I(sync.dma_start(out=atb[k % 2][:], in_=at_hbm[k]), f"d_atr{k % 2}", 16)
                if 1 <= k <= K0:
                    W(sync, "c_evO", k * NPS)
                    I(sync.dma_start(out=ot_hbm[k - 1], in_=otst[(k - 1) % 2][:]),
                      f"d_otw{(k - 1) % 2}", 16)
            W(sync, "c_mmO", NE * NPS)
            for g in range(NPE):
                for s_ in range(NT):
                    u = g * NT + s_
                    if u >= 2:
                        W(sync, "c_o1", u - 1)
                    I(sync.dma_start(out=xst[u % 2][:],
                                     in_=x_t[s_][:, 512 * g:512 * (g + 1)]),
                      f"d_xe{u % 2}", 16)
                    if u >= 1:
                        W(sync, "c_o2", u)
                        up = u - 1
                        I(sync.dma_start(
                            out=out_t[up % NT][:, 512 * (up // NT):512 * (up // NT + 1)],
                            in_=outst[up % 2][:]), f"d_out{up % 2}", 16)
            W(sync, "c_o2", U)
            up = U - 1
            I(sync.dma_start(
                out=out_t[up % NT][:, 512 * (up // NT):512 * (up // NT + 1)],
                in_=outst[up % 2][:]), f"d_out{up % 2}", 16)
            sync.wait_ge(sem["d_out0"], finals["d_out0"])
            sync.wait_ge(sem["d_out1"], finals["d_out1"])

        @block.scalar
        def _(scalar):
            for i in range(NT):
                W(scalar, f"d_x{i % 2}", 16 * (i // 2 + 1))
                if i >= 2:
                    W(scalar, "c_trA", GA * (i - 1))
                    W(scalar, "c_am", i - 1)
                I(scalar.activation(out=xnbuf[i % 2][:], in_=xbuf[i % 2][:],
                                    func=AF.Copy,
                                    accum_out=s1[:, i % 2:i % 2 + 1]), "c_scp")
                W(scalar, "c_scp", i + 1)
                I(scalar.activation(out=xnbuf[i % 2][:], in_=xbuf[i % 2][:],
                                    func=AF.Square,
                                    accum_out=s2[:, i % 2:i % 2 + 1]), "c_sacc")
                W(scalar, "c_pp", 4 * (i + 1))
                I(scalar.activation(out=rstd[:, i % 2:i % 2 + 1],
                                    in_=varst[:, i % 2:i % 2 + 1],
                                    func=AF.Sqrt), "c_rstd")
                if i >= 1:
                    for g in range(GA):
                        idx = GA * (i - 1) + g
                        W(scalar, "c_trA", idx + 1)
                        src = pb4 if idx % 2 == 0 else pb5
                        I(scalar.activation(
                            out=xnT[:].rearrange("p (c s) -> p c s", c=NE)[:, g * TPG:(g + 1) * TPG, 128 * (i - 1):128 * i],
                            in_=src[:].rearrange("p (c s) -> p c s", c=TPG),
                            func=AF.Copy), "c_evA")
            for g in range(GA):
                idx = GA * (NT - 1) + g
                W(scalar, "c_trA", idx + 1)
                src = pb4 if idx % 2 == 0 else pb5
                I(scalar.activation(
                    out=xnT[:].rearrange("p (c s) -> p c s", c=NE)[:, g * TPG:(g + 1) * TPG, 128 * (NT - 1):128 * NT],
                    in_=src[:].rearrange("p (c s) -> p c s", c=TPG),
                    func=AF.Copy), "c_evA")
            W(scalar, "c_wab", NE)
            for j in range(NE):
                if j >= 1:
                    W(scalar, "c_wq8", j)
                I(scalar.dma_start(out=wfb[:], in_=wft_c[j]), "d_wfb", 16)
            for p in range(NQP):
                for kl in range(QB):
                    k = p * QB + kl
                    W(scalar, "c_bmk", k + 1)
                    W(scalar, "c_mmSq", NPS * (k + 1))
                    if k >= 2:
                        W(scalar, "c_trAT", GK * (k - 1))
                        W(scalar, "c_den", k - 1)
                    I(scalar.activation(out=A_bf[k % 2][:, 0:SH], in_=ps_big[:, 0:SH],
                                        func=AF.Exp, scale=0.125,
                                        bias=bmk[:, k % 2:k % 2 + 1],
                                        accum_out=den0[:, k % 2:k % 2 + 1]), "c_exph")
                    I(scalar.activation(out=A_bf[k % 2][:, SH:S], in_=ps_big[:, SH:S],
                                        func=AF.Exp, scale=0.125,
                                        bias=bmk[:, k % 2:k % 2 + 1],
                                        accum_out=den1[:, k % 2:k % 2 + 1]), "c_exph")
                    for g2 in range(GK):
                        tcnt = min(8, TK - 8 * g2)
                        idxAT = GK * k + g2
                        W(scalar, "c_trAT", idxAT + 1)
                        if k >= 2 and g2 == 0:
                            W(scalar, f"d_atw{k % 2}", 16 * ((k - 2) // 2 + 1))
                        I(scalar.activation(
                            out=at_st[k % 2][:, 1024 * g2:1024 * g2 + 128 * tcnt],
                            in_=pbt[:, 0:128 * tcnt], func=AF.Copy), "c_evAT")
                    emit_oth_act(scalar, 3)
            for _f in range(NE):
                emit_oth_act(scalar, 3 if _f < NE // 2 else 4)
            for k in range(NE):
                if k < 2:
                    emit_oth_act(scalar, 4)
                for sp in range(NPS):
                    idx = k * NPS + sp
                    W(scalar, "c_mmO", idx + 1)
                    if k == 0 and sp == 0:
                        W(scalar, "c_mmV", NE * NPS)
                        W(scalar, "d_atw0", finals["d_atw0"])
                        W(scalar, "d_atw1", finals["d_atw1"])
                    if sp == 0:
                        W(scalar, "c_rcd", k + 1)
                        if k < K0:
                            if k >= 2:
                                W(scalar, f"d_otw{k % 2}", 16 * ((k - 2) // 2 + 1))
                        else:
                            W(scalar, "c_q8x", k + 1)
                    bank = pbank6[idx % NBK]
                    dst = (otst[k % 2][:, 512 * sp:512 * (sp + 1)] if k < K0 else
                           ot_res[:, S * k + 512 * sp:S * k + 512 * (sp + 1)])
                    I(scalar.activation(out=dst, in_=bank, func=AF.Copy,
                                        scale=recdall[:, k:k + 1]), "c_evO")
                if oth_ac[0] >= NTO and not wq8r_issued[0]:
                    wq8r_issued[0] = True
                    W(scalar, "d_wq8w0", finals["d_wq8w0"])
                    W(scalar, "d_wq8w1", finals["d_wq8w1"])
                    W(scalar, "c_mmB", NT * NPE)
                    I(scalar.dma_start(out=wq8r[0][:].rearrange("p (c f) -> p c f", c=NE),
                                       in_=wq8_hbm[:, :, 0:512].rearrange("c p f -> p c f")),
                      "d_wq8r0", 16)
                    if NPE > 1:
                        I(scalar.dma_start(out=wq8r[1][:].rearrange("p (c f) -> p c f", c=NE),
                                           in_=wq8_hbm[:, :, 512:1024].rearrange("c p f -> p c f")),
                          "d_wq8r1", 16)
            emit_oth_act(scalar, NTO)
            if not wq8r_issued[0]:
                wq8r_issued[0] = True
                W(scalar, "d_wq8w0", finals["d_wq8w0"])
                W(scalar, "d_wq8w1", finals["d_wq8w1"])
                W(scalar, "c_mmB", NT * NPE)
                I(scalar.dma_start(out=wq8r[0][:].rearrange("p (c f) -> p c f", c=NE),
                                   in_=wq8_hbm[:, :, 0:512].rearrange("c p f -> p c f")),
                  "d_wq8r0", 16)
                if NPE > 1:
                    I(scalar.dma_start(out=wq8r[1][:].rearrange("p (c f) -> p c f", c=NE),
                                       in_=wq8_hbm[:, :, 512:1024].rearrange("c p f -> p c f")),
                      "d_wq8r1", 16)
            for m in range(K0 // 2):
                W(scalar, "c_q8x", 2 * m + 2)
                W(scalar, "d_otw0", 16 * (m + 1))
                W(scalar, "d_otw1", 16 * (m + 1))
                I(scalar.dma_start(
                    out=ot_res[:, S * 2 * m:S * (2 * m + 2)].rearrange("p (k s) -> p k s", k=2),
                    in_=ot_hbm[2 * m:2 * m + 2].rearrange("k p s -> p k s")), "d_otrb", 16)
            for g in range(NPE):
                if g == 0:
                    W(scalar, "c_dsc", 11)
                for s_ in range(NT):
                    u = g * NT + s_
                    W(scalar, "c_mmEff", u + 1)
                    if u >= 2:
                        W(scalar, "c_f2", u - 1)
                    pf = pb6[:] if s_ % 2 == 0 else ps_big[:, 0:512]
                    I(scalar.activation(out=fbf[u % 2][:], in_=pf, func=AF.Copy,
                                        scale=bc3[:, 2:3]), "c_fb")
                gq = g + 2
                if gq < NPE:
                    W(scalar, "c_mmEff", (gq - 1) * NT)
                    I(scalar.dma_start(
                        out=wq8r[gq % 2][:].rearrange("p (c f) -> p c f", c=NE),
                        in_=wq8_hbm[:, :, 512 * gq:512 * (gq + 1)].rearrange("c p f -> p c f")),
                      f"d_wq8r{gq % 2}", 16)

        @block.vector
        def _(vector):
            invn = 1.0 / E
            I(vector.memset(ident_f[:], 0.0), "c_idv")
            I(vector.memset(ident_b[:], 0.0), "c_idv")
            I(vector.memset(ones_f[:], 1.0), "c_idv")
            I(vector.memset(ones_b[:], 1.0), "c_idv")
            for i in range(NT):
                if i < NE:
                    j = i
                    W(vector, "d_wfb", 16 * (j + 1))
                    I(vector.reduce_max(out=wab[:, j % 2:j % 2 + 1], in_=wfb[:],
                                        axis=AX.X, apply_absolute_value=True),
                      "c_wab")
                    W(vector, "c_wab", j + 1)
                    if j == 0:
                        iw = vector.tensor_copy(out=wf128[:], in_=wab[:, 0:1])
                    else:
                        iw = vector.tensor_tensor(out=wf128[:], in0=wf128[:],
                                                  in1=wab[:, j % 2:j % 2 + 1],
                                                  op=ALU.max)
                    if j == NE - 1:
                        iw.then_inc(sem["c_dsc"], 3)
                W(vector, "c_sacc", i + 1)
                if i >= 2:
                    W(vector, "c_trA", GA * (i - 1))
                    W(vector, "c_am", i - 1)
                I(vector.tensor_scalar(out=mstat[:, i % 2:i % 2 + 1],
                                       in0=s1[:, i % 2:i % 2 + 1], scalar1=invn,
                                       scalar2=None, op0=ALU.mult), "c_pp")
                I(vector.tensor_scalar(out=qstat[:, i % 2:i % 2 + 1],
                                       in0=s2[:, i % 2:i % 2 + 1], scalar1=invn,
                                       scalar2=1e-5, op0=ALU.mult,
                                       op1=ALU.add), "c_pp")
                W(vector, "c_pp", 4 * i + 2)
                I(vector.tensor_tensor(out=mm2[:, i % 2:i % 2 + 1],
                                       in0=mstat[:, i % 2:i % 2 + 1],
                                       in1=mstat[:, i % 2:i % 2 + 1],
                                       op=ALU.mult), "c_pp")
                W(vector, "c_pp", 4 * i + 3)
                I(vector.tensor_tensor(out=varst[:, i % 2:i % 2 + 1],
                                       in0=qstat[:, i % 2:i % 2 + 1],
                                       in1=mm2[:, i % 2:i % 2 + 1],
                                       op=ALU.subtract), "c_pp")
                W(vector, "c_rstd", i + 1)
                I(vector.reciprocal(out=rinv[:, i % 2:i % 2 + 1],
                                    in_=rstd[:, i % 2:i % 2 + 1]), "c_ri")
                W(vector, "c_ri", i + 1)
                I(vector.tensor_scalar(out=xnbuf[i % 2][:], in0=xbuf[i % 2][:],
                                       scalar1=mstat[:, i % 2:i % 2 + 1],
                                       scalar2=rinv[:, i % 2:i % 2 + 1],
                                       op0=ALU.subtract, op1=ALU.mult), "c_xn")
                W(vector, "c_xn", i + 1)
                I(vector.reduce_max(out=amb[:, i % 2:i % 2 + 1],
                                    in_=xnbuf[i % 2][:], axis=AX.X,
                                    apply_absolute_value=True), "c_am")
                W(vector, "c_am", i + 1)
                if i == 0:
                    I(vector.tensor_copy(out=run128[:], in_=amb[:, 0:1]), "c_mg")
                else:
                    I(vector.tensor_tensor(out=run128[:], in0=run128[:],
                                           in1=amb[:, i % 2:i % 2 + 1],
                                           op=ALU.max), "c_mg")
                if 0 <= i - 2 < min(FREEK, NT - 2):
                    sbv = i - 2
                    for pnl in range(NPE):
                        idx = sbv * NPE + pnl
                        W(vector, "c_mmB", idx + 1)
                        I(vector.tensor_copy(
                            out=K_ap(sbv, 512 * pnl, 512 * (pnl + 1)),
                            in_=bb[idx % NBB]), "c_evB")
            def emit_div(vector, j):
                if j == 0:
                    W(vector, "c_dsc", 7)
                W(vector, "d_wfb", 16 * (NE + j + 1))
                if j >= 2:
                    W(vector, f"d_wq8w{j % 2}", 16 * ((j - 2) // 2 + 1))
                I(vector.tensor_scalar(out=wq8st[j % 2][:], in0=wfb[:],
                                       scalar1=wsbr[:], scalar2=None,
                                       op0=ALU.mult), "c_wq8")
            W(vector, "d_p4", 16)
            I(vector.reduce_max(out=wmax[:], in_=prow2[:], axis=AX.X,
                                apply_absolute_value=True), "c_dsc")
            W(vector, "c_dsc", 4)
            I(vector.tensor_scalar_max(out=cat[:, 1:2], in0=wmax[:],
                                       scalar1=1e-6), "c_dsc")
            W(vector, "c_bcW", 1)
            I(vector.tensor_copy(out=wsb[:], in_=pb6[:, 0:1]), "c_dsc")
            W(vector, "c_dsc", 6)
            I(vector.reciprocal(out=wsbr[:], in_=wsb[:]), "c_dsc")
            done_early = max(min(FREEK, NT - 2), 0)
            for sb_ in range(NT):
                if sb_ < done_early:
                    continue
                if sb_ == FREEK:
                    W(vector, "c_xn", NT)
                    W(vector, "c_sacc", NT)
                    W(vector, "c_trA", GA * NT)
                    W(vector, "c_mg", NT)
                if sb_ >= FREEK and kaddr(sb_) == ((addr_of["wq8st0"] - R2) // CHB) * CHB:
                    W(vector, "d_wq8w0", finals["d_wq8w0"])
                    W(vector, "d_wq8w1", finals["d_wq8w1"])
                for pnl in range(NPE):
                    idx = sb_ * NPE + pnl
                    W(vector, "c_mmB", idx + 1)
                    I(vector.tensor_copy(
                        out=K_ap(sb_, 512 * pnl, 512 * (pnl + 1)),
                        in_=bb[idx % NBB]), "c_evB")
                if NT >= 14:
                    climit = FREEK + (addr_of["wq8st0"] - R2) // CHB
                    nproc = max(climit - done_early, 1)
                    if done_early <= sb_ < climit:
                        k0 = (sb_ - done_early) * NE // nproc
                        k1 = (sb_ - done_early + 1) * NE // nproc
                        for j in range(k0, k1):
                            emit_div(vector, j)
            if NT < 14:
                for j in range(NE):
                    emit_div(vector, j)
            for p in range(NQP):
                for sb_ in range(NT):
                    gi = p * NT + sb_
                    W(vector, "c_mmQ", gi + 1)
                    src = pbank[gi % 3]
                    I(vector.tensor_copy(out=q_pnl[:, QW * sb_:QW * (sb_ + 1)],
                                         in_=src[:, 0:QW]), "c_qev")
                for kl in range(QB):
                    k = p * QB + kl
                    for qq in range(NPS):
                        W(vector, "c_mmSq", NPS * k + qq + 1)
                        I(vector.reduce_max(out=pmq[:, qq:qq + 1],
                                            in_=ps_big[:, 512 * qq:512 * (qq + 1)],
                                            axis=AX.X), "c_pmx")
                    W(vector, "c_pmx", NPS * (k + 1))
                    I(vector.tensor_reduce(out=negm[:], in_=pmq[:], axis=AX.X,
                                           op=ALU.max, negate=True), "c_ngm")
                    W(vector, "c_ngm", k + 1)
                    I(vector.tensor_scalar(out=bmk[:, k % 2:k % 2 + 1], in0=negm[:],
                                           scalar1=0.125, scalar2=None,
                                           op0=ALU.mult), "c_bmk")
                    W(vector, "c_exph", 2 * (k + 1))
                    I(vector.tensor_tensor(out=dent[:, k % 2:k % 2 + 1],
                                           in0=den0[:, k % 2:k % 2 + 1],
                                           in1=den1[:, k % 2:k % 2 + 1],
                                           op=ALU.add), "c_den")
                    W(vector, "c_den", k + 1)
                    I(vector.reciprocal(out=recdall[:, k:k + 1],
                                        in_=dent[:, k % 2:k % 2 + 1]), "c_rcd")
                    emit_oth_dve(vector, 3)
            for f in range(NE):
                for sp in range(NPS):
                    idx = f * NPS + sp
                    W(vector, "c_mmV", idx + 1)
                    bank = pbank7[idx % NBK7]
                    I(vector.tensor_copy(
                        out=VT[:, S * f + 512 * sp:S * f + 512 * (sp + 1)],
                        in_=bank), "c_evV")
                emit_oth_dve(vector, 3 if f < NE // 2 else 4)
            emit_oth_dve(vector, NTO)
            W(vector, "d_p2", 16)
            I(vector.reduce_max(out=gmax[:], in_=prow1[:], axis=AX.X,
                                apply_absolute_value=True), "c_dsc")
            W(vector, "c_dsc", 8)
            I(vector.tensor_scalar_max(out=cat[:, 0:1], in0=gmax[:], scalar1=1e-6),
              "c_dsc")
            W(vector, "c_dsc", 9)
            I(vector.tensor_tensor(out=cat[:, 2:3], in0=cat[:, 0:1],
                                   in1=cat[:, 1:2], op=ALU.mult), "c_dsc")
            W(vector, "c_bcX", 1)
            I(vector.tensor_copy(out=bc3[:], in_=ps_big[:, BCX0:BCX0 + 3]), "c_dsc")
            W(vector, "c_dsc", 11)
            I(vector.reciprocal(out=bc3r[:], in_=bc3[:]), "c_dsc")
            W(vector, "c_dsc", 12)
            W(vector, "d_atw0", finals["d_atw0"])
            W(vector, "d_atw1", finals["d_atw1"])
            W(vector, "c_trAT", GK * NE)
            W(vector, "c_mmSq", NPS * NE)
            W(vector, "c_otha", NTO)
            for c in range(NE):
                I(vector.tensor_scalar(
                    out=xq8[:, S * c:S * (c + 1)],
                    in0=xnT[:, S * c:S * (c + 1)],
                    scalar1=bc3r[:, 0:1], scalar2=None, op0=ALU.mult), "c_q8x")
            for t in range(NPE):
                W(vector, "c_bcb", 1)
                I(vector.tensor_copy(out=bias_rep[:, 512 * t:512 * (t + 1)],
                                     in_=ps_big[:, 512 * t:512 * (t + 1)]),
                  "c_bcb2", 1 if t == NPE - 1 else 0)
            for g in range(NPE):
                for s_ in range(NT):
                    u = g * NT + s_
                    W(vector, "c_mmEat", u + 1)
                    W(vector, f"d_xe{u % 2}", 16 * (u // 2 + 1))
                    if u >= 2:
                        W(vector, f"d_out{u % 2}", 16 * ((u - 2) // 2 + 1))
                    if u < 2:
                        W(vector, "d_otw0", finals["d_otw0"])
                        W(vector, "d_otw1", finals["d_otw1"])
                    pa = pb4 if s_ % 2 == 0 else pb5
                    I(vector.tensor_tensor(out=outst[u % 2][:], in0=pa[:],
                                           in1=xst[u % 2][:], op=ALU.add), "c_o1")
                    W(vector, "c_fb", u + 1)
                    if u == 0:
                        W(vector, "c_bcb2", 1)
                    if u >= 2:
                        W(vector, "c_o2", u - 1)
                    I(vector.tensor_tensor(out=f2t[u % 2][:], in0=fbf[u % 2][:],
                                           in1=bias_rep[:, 512 * g:512 * (g + 1)],
                                           op=ALU.add), "c_f2")
                    W(vector, "c_f2", u + 1)
                    W(vector, "c_o1", u + 1)
                    I(vector.tensor_tensor(out=outst[u % 2][:], in0=outst[u % 2][:],
                                           in1=f2t[u % 2][:], op=ALU.add), "c_o2")

        @block.tensor
        def _(tensor):
            W(tensor, "c_idp", 1)
            CPGk = NE // max(NE // 4, 1)

            def emit_b_group(sb_):
                for pnl in range(NPE):
                    idx = sb_ * NPE + pnl
                    if idx >= NBB:
                        W(tensor, "c_evB", idx - NBB + 1)
                    dst = bb[idx % NBB]
                    for j in range(NE):
                        if sb_ == 0 and pnl == 0 and j % CPGk == 0:
                            W(tensor, f"d_wk{j // CPGk}", 16)
                        I(tensor.matmul(
                            dst,
                            lhsT=xnT[:, S * j + 128 * sb_:S * j + 128 * (sb_ + 1)],
                            rhs=Wk_bf[:, E * j + 512 * pnl:E * j + 512 * (pnl + 1)],
                            start=(j == 0), stop=(j == NE - 1)),
                          "c_mmB", 1 if j == NE - 1 else 0)

            for i in range(NT):
                W(tensor, "c_xn", i + 1)
                for g in range(GA):
                    idx = GA * i + g
                    if idx >= 2:
                        W(tensor, "c_evA", idx - 1)
                    dst = pb4 if idx % 2 == 0 else pb5
                    for t in range(TPG):
                        j = g * TPG + t
                        I(tensor.transpose(dst[:, 128 * t:128 * (t + 1)],
                                           in_=xnbuf[i % 2][:, 128 * j:128 * (j + 1)],
                                           identity=ident_f[:]),
                          "c_trA", 1 if t == TPG - 1 else 0)
                if 1 <= i <= min(FREEK, NT - 2) + 1:
                    W(tensor, "c_evA", GA * i)
                    emit_b_group(i - 1)
            W(tensor, "c_evA", GA * NT)
            W(tensor, "c_dsc", 5)
            W(tensor, "c_idv", 3)
            I(tensor.matmul(pb6[:, 0:1], lhsT=ones_f[:], rhs=cat[:, 1:2],
                            start=True, stop=True), "c_bcW")
            for sbr_ in range(min(FREEK, NT - 2) + 1, NT):
                emit_b_group(sbr_)
            W(tensor, "c_evB", NT * NPE)
            W(tensor, "c_dsc", 6)
            W(tensor, "c_idp", 2)
            for p in range(NQP):
                W(tensor, f"d_wq{p % 2}", 16 * (p // 2 + 1))
                for sb_ in range(NT):
                    gi = p * NT + sb_
                    if gi >= 3:
                        W(tensor, "c_qev", gi - 2)
                    dst = pbank[gi % 3]
                    for j in range(NE):
                        I(tensor.matmul(
                            dst[:, 0:QW],
                            lhsT=xnT[:, S * j + 128 * sb_:S * j + 128 * (sb_ + 1)],
                            rhs=wqbf[p % 2][:, QW * j:QW * (j + 1)],
                            start=(j == 0), stop=(j == NE - 1)),
                          "c_mmQ", 1 if j == NE - 1 else 0)
                for kl in range(QB):
                    k = p * QB + kl
                    for qq in range(NPS):
                        if k >= 1:
                            half = 1 if (NPS > 1 and qq < NPS // 2) else 2
                            W(tensor, "c_exph", 2 * (k - 1) + half)
                        for s_ch in range(NT):
                            if kl == 0 and qq == 0:
                                W(tensor, "c_qev", p * NT + s_ch + 1)
                            I(tensor.matmul(
                                ps_big[:, 512 * qq:512 * (qq + 1)],
                                lhsT=q_pnl[:, QW * s_ch + 128 * kl:QW * s_ch + 128 * (kl + 1)],
                                rhs=K_ap(s_ch, 512 * qq, 512 * (qq + 1)),
                                start=(s_ch == 0), stop=(s_ch == NT - 1)),
                              "c_mmSq", 1 if s_ch == NT - 1 else 0)
                    for g2 in range(GK):
                        tcnt = min(8, TK - 8 * g2)
                        idxAT = GK * k + g2
                        need = 2 * k + (2 if GK == 1 else g2 + 1)
                        W(tensor, "c_exph", need)
                        W(tensor, "c_evAT", idxAT)
                        for t in range(tcnt):
                            j = g2 * 8 + t
                            I(tensor.transpose(pbt[:, 128 * t:128 * (t + 1)],
                                               in_=A_bf[k % 2][:, 128 * j:128 * (j + 1)],
                                               identity=ident_b[:]),
                              "c_trAT", 1 if t == tcnt - 1 else 0)
            W(tensor, "c_qev", NQP * NT)
            W(tensor, "c_dsc", 6)
            W(tensor, "c_exph", 2 * NE)
            W(tensor, "c_rcd", NE)
            for f in range(NE):
                W(tensor, f"d_wv{f % 4}", 16 * (f // 4 + 1))
                for sp in range(NPS):
                    idx = f * NPS + sp
                    if idx >= NBK7:
                        W(tensor, "c_evV", idx - NBK7 + 1)
                    bank = pbank7[idx % NBK7]
                    for j in range(NE):
                        I(tensor.matmul(
                            bank, lhsT=wvbf[f % 4][:, 128 * j:128 * (j + 1)],
                            rhs=xnT[:, S * j + 512 * sp:S * j + 512 * (sp + 1)],
                            start=(j == 0), stop=(j == NE - 1)),
                          "c_mmV", 1 if j == NE - 1 else 0)
            W(tensor, "c_evV", NE * NPS)
            for k in range(NE):
                if k == K0:
                    W(tensor, "c_dsc", 10)
                    W(tensor, "c_exph", 2 * NE)
                    I(tensor.matmul(ps_big[:, BCX0:BCX0 + 3], lhsT=ones_f[:], rhs=cat[:],
                                    start=True, stop=True), "c_bcX")
                W(tensor, f"d_atr{k % 2}", 16 * (k // 2 + 1))
                for sp in range(NPS):
                    idx = k * NPS + sp
                    if idx >= NBK:
                        W(tensor, "c_evO", idx - NBK + 1)
                    bank = pbank6[idx % NBK]
                    for c in range(NE):
                        I(tensor.matmul(
                            bank, lhsT=atb[k % 2][:, 128 * c:128 * (c + 1)],
                            rhs=VT[:, S * c + 512 * sp:S * c + 512 * (sp + 1)],
                            start=(c == 0), stop=(c == NE - 1)),
                          "c_mmO", 1 if c == NE - 1 else 0)
            W(tensor, "c_idv", 4)
            W(tensor, "d_bias", 16)
            W(tensor, "c_evO", NE * NPS)
            W(tensor, "c_dsc", 11)
            for t in range(NPE):
                I(tensor.matmul(ps_big[:, 512 * t:512 * (t + 1)], lhsT=ones_b[:],
                                rhs=brow[:, 512 * t:512 * (t + 1)], start=True,
                                stop=True), "c_bcb", 1 if t == NPE - 1 else 0)
            W(tensor, "c_q8x", NE)
            W(tensor, "d_otrb", 16 * (K0 // 2))
            for g in range(NPE):
                W(tensor, f"d_wo{g % 2}", 16 * (g // 2 + 1))
                W(tensor, f"d_wq8r{g % 2}", 16 * (g // 2 + 1))
                for s_ in range(NT):
                    u = g * NT + s_
                    pa = pb4 if s_ % 2 == 0 else pb5
                    if u >= 2:
                        W(tensor, "c_o1", u - 1)
                    for c in range(NE):
                        I(tensor.matmul(
                            pa[:],
                            lhsT=ot_res[:, S * c + 128 * s_:S * c + 128 * (s_ + 1)],
                            rhs=wobf[g % 2][:, 512 * c:512 * (c + 1)],
                            start=(c == 0), stop=(c == NE - 1)),
                          "c_mmEat", 1 if c == NE - 1 else 0)
                    pf = pb6[:] if s_ % 2 == 0 else ps_big[:, 0:512]
                    if u >= 2:
                        W(tensor, "c_fb", u - 1)
                    elif u == 0:
                        W(tensor, "c_bcb2", 1)
                    for d in range(NE // 2):
                        I(tensor.matmul(
                            pf,
                            lhsT=xq8[:].rearrange("p (c s) -> p c s", c=NE)[:, 2 * d:2 * d + 2, 128 * s_:128 * (s_ + 1)],
                            rhs=wq8r[g % 2][:].rearrange("p (c f) -> p c f", c=NE)[:, 2 * d:2 * d + 2, :],
                            start=(d == 0), stop=(d == NE // 2 - 1),
                            perf_mode=mybir.MatmulPerfMode.DoubleRow),
                          "c_mmEff", 1 if d == NE // 2 - 1 else 0)

        @block.gpsimd
        def _(gpsimd):
            W(gpsimd, "c_idv", 2)
            I(gpsimd.affine_select(out=ident_f[:], in_=ident_f[:],
                                   compare_op=ALU.not_equal, fill=1.0, base=0,
                                   pattern=[[-1, 128]], channel_multiplier=1), "c_idp")
            I(gpsimd.affine_select(out=ident_b[:], in_=ident_b[:],
                                   compare_op=ALU.not_equal, fill=1.0, base=0,
                                   pattern=[[-1, 128]], channel_multiplier=1), "c_idp")
            WKG = max(NE // 4, 1)
            CPG = NE // WKG
            for m in range(WKG):
                I(gpsimd.dma_start(
                    out=Wk_bf[:, CPG * m * E:CPG * (m + 1) * E].rearrange("p (c f) -> p c f", c=CPG),
                    in_=wk_c[CPG * m:CPG * (m + 1)].rearrange("c p f -> p c f")),
                  f"d_wk{m}", 16)
            W(gpsimd, "c_dsc", 3)
            I(gpsimd.dma_start(out=pscr2[:, None], in_=wf128[:, 0:1]), "d_p3", 16)
            W(gpsimd, "d_p3", 16)
            I(gpsimd.dma_start(out=prow2[:], in_=pscr2[None, :]), "d_p4", 16)
            for j in range(NE):
                W(gpsimd, "c_wq8", j + 1)
                I(gpsimd.dma_start(out=wq8_hbm[j], in_=wq8st[j % 2][:]), f"d_wq8w{j % 2}", 16)
            W(gpsimd, "c_mmB", NT * NPE)
            for p in range(min(2, NQP)):
                I(gpsimd.dma_start(out=wqbf[p % 2][:].rearrange("p (c f) -> p c f", c=NE),
                                   in_=wq_c[:, :, QW * p:QW * (p + 1)]), f"d_wq{p % 2}", 16)
            for p in range(NQP):
                pq = p + 2
                if pq < NQP:
                    W(gpsimd, "c_mmQ", (pq - 1) * NT)
                    I(gpsimd.dma_start(out=wqbf[pq % 2][:].rearrange("p (c f) -> p c f", c=NE),
                                       in_=wq_c[:, :, QW * pq:QW * (pq + 1)]), f"d_wq{pq % 2}", 16)
                emit_oth_load(gpsimd, 3 * QB)
            W(gpsimd, "c_mmQ", NQP * NT)
            for f in range(min(4, NE)):
                I(gpsimd.dma_start(out=wvbf[f % 4][:].rearrange("p (c f) -> p c f", c=NE),
                                   in_=wv_c[:, :, 128 * f:128 * (f + 1)]), f"d_wv{f % 4}", 16)
            for f in range(NE):
                fq = f + 4
                if fq < NE:
                    W(gpsimd, "c_mmV", (fq - 3) * NPS)
                    I(gpsimd.dma_start(out=wvbf[fq % 4][:].rearrange("p (c f) -> p c f", c=NE),
                                       in_=wv_c[:, :, 128 * fq:128 * (fq + 1)]), f"d_wv{fq % 4}", 16)
                emit_oth_load(gpsimd, 3 if f < NE // 2 else 4)
            emit_oth_load(gpsimd, NTO)
            W(gpsimd, "c_mg", NT)
            W(gpsimd, "c_othmg", max(NB - 1, 1))
            I(gpsimd.dma_start(out=pscr1[:, None], in_=run128[:, 0:1]), "d_p1", 16)
            W(gpsimd, "d_p1", 16)
            I(gpsimd.dma_start(out=prow1[:], in_=pscr1[None, :]), "d_p2", 16)
            W(gpsimd, "c_mmO", NE * NPS)
            I(gpsimd.dma_start(out=brow[:], in_=bias_in[None, :]), "d_bias", 16)
            for g in range(min(2, NPE)):
                I(gpsimd.dma_start(out=wobf[g % 2][:].rearrange("p (c f) -> p c f", c=NE),
                                   in_=wo_c[:, :, 512 * g:512 * (g + 1)]), f"d_wo{g % 2}", 16)
            for gq in range(2, NPE):
                W(gpsimd, "c_mmEat", (gq - 1) * NT)
                I(gpsimd.dma_start(
                    out=wobf[gq % 2][:].rearrange("p (c f) -> p c f", c=NE),
                    in_=wo_c[:, :, 512 * gq:512 * (gq + 1)]), f"d_wo{gq % 2}", 16)

    for name, val in waits:
        assert val <= finals[name], f"wait {name}>={val} exceeds final {finals[name]}"
    return nc


def kernel(L0_attn_x, L0_attn_Wq, L0_attn_Wk, L0_attn_Wv, L0_attn_Wo,
           L0_ffn_weight, L0_ffn_bias):
    x = np.asarray(L0_attn_x)
    B, S, E = x.shape

    def f32(a):
        return np.ascontiguousarray(np.asarray(a), dtype=np.float32)

    def bf16(a):
        a = np.asarray(a)
        if a.dtype == np.uint16:
            a = a.view(ml_dtypes.bfloat16)
        assert a.dtype == ml_dtypes.bfloat16
        return a

    wq, wk, wv, wo = f32(L0_attn_Wq), f32(L0_attn_Wk), f32(L0_attn_Wv), f32(L0_attn_Wo)
    wft = np.ascontiguousarray(bf16(L0_ffn_weight).T)
    bias = np.ascontiguousarray(bf16(L0_ffn_bias))

    nc = build_program(S, E, n_cores=B)
    xs_f32 = [np.ascontiguousarray(x[c], dtype=np.float32) for c in range(B)]
    in_maps = [{"x": xs_f32[c],
                "xo": np.concatenate([xs_f32[d] for d in range(B) if d != c], axis=0),
                "wq": wq, "wk": wk, "wv": wv, "wo": wo,
                "wft": wft, "bias": bias} for c in range(B)]
    res = run_bass_kernel_spmd(nc, in_maps, core_ids=list(range(B)))
    out = np.stack([np.asarray(res.results[c]["out"]) for c in range(B)], axis=0)
    return out.astype(np.float32, copy=False)


# version 30 (final)
# speedup vs baseline: 1.0241x; 1.0239x over previous
import numpy as np
import ml_dtypes
import concourse.bass as bass
import concourse.mybir as mybir
from concourse.bass_utils import run_bass_kernel_spmd

F32 = mybir.dt.float32
BF16 = mybir.dt.bfloat16
FP8 = mybir.dt.float8e4
ALU = mybir.AluOpType
AF = mybir.ActivationFunctionType
AX = mybir.AxisListType
KIB = 1024


def build_program(S=2048, E=2048, n_cores=8):
    assert S == E
    NT = S // 128
    NE = E // 128
    GA = max(NE // 4, 1)
    TPG = NE // GA
    NPS = max(S // 512, 1)
    NPE = max(E // 512, 1)
    QW = 256
    NQP = E // QW
    QB = QW // 128
    U = NT * NPE
    SH = S // 2
    TK = S // 128
    GK = (TK + 7) // 8
    NB = n_cores
    NTO = (NB - 1) * NT

    nc = bass.Bass("TRN2", target_bir_lowering=False)

    x_in = nc.dram_tensor("x", [S, E], F32, kind="ExternalInput").ap()
    xo_in = nc.dram_tensor("xo", [NTO * 128, E], F32, kind="ExternalInput").ap()
    wq_in = nc.dram_tensor("wq", [E, E], F32, kind="ExternalInput").ap()
    wk_in = nc.dram_tensor("wk", [E, E], F32, kind="ExternalInput").ap()
    wv_in = nc.dram_tensor("wv", [E, E], F32, kind="ExternalInput").ap()
    wo_in = nc.dram_tensor("wo", [E, E], F32, kind="ExternalInput").ap()
    wft_in = nc.dram_tensor("wft", [E, E], BF16, kind="ExternalInput").ap()
    bias_in = nc.dram_tensor("bias", [E], BF16, kind="ExternalInput").ap()
    out_hbm = nc.dram_tensor("out", [S, E], F32, kind="ExternalOutput").ap()

    xn_hbm = nc.dram_tensor("xn_scr", [S, E], F32, kind="Internal").ap()
    at_hbm = nc.dram_tensor("at_scr", [NE, 128, S], BF16, kind="Internal").ap()
    ot_hbm = nc.dram_tensor("ot_scr", [NE, 128, S], BF16, kind="Internal").ap()
    wq8_hbm = nc.dram_tensor("wq8_scr", [NE, 128, E], FP8, kind="Internal").ap()
    pscr1 = nc.dram_tensor("pscr1", [128], F32, kind="Internal").ap()
    pscr2 = nc.dram_tensor("pscr2", [128], F32, kind="Internal").ap()

    x_t = x_in.rearrange("(t p) e -> t p e", p=128)
    xo_t = xo_in.rearrange("(t p) e -> t p e", p=128)
    xn_t = xn_hbm.rearrange("(t p) e -> t p e", p=128)
    out_t = out_hbm.rearrange("(t p) e -> t p e", p=128)
    wq_c = wq_in.rearrange("(c p) f -> p c f", p=128)
    wk_c = wk_in.rearrange("(c p) f -> c p f", p=128)
    wv_c = wv_in.rearrange("(c p) f -> p c f", p=128)
    wo_c = wo_in.rearrange("(c p) f -> p c f", p=128)
    wft_c = wft_in.rearrange("(c p) f -> c p f", p=128)

    specs = {}

    def sb(name, shape, dtype):
        specs[name] = (shape, dtype)
        return name

    addr_of = {}

    def place(names, base):
        off = base
        for n in names:
            shape, dtype = specs[n]
            per = 1
            for d in shape[1:]:
                per *= d
            per *= mybir.dt.size(dtype)
            off = (off + 63) & ~63
            addr_of[n] = off
            off += per
        return off

    names_small = []
    for n, shp, dt_ in [
        ("ident_f", [128, 128], F32), ("ident_b", [128, 128], BF16),
        ("ones_f", [1, 128], F32), ("ones_b", [1, 128], BF16),
        ("cat", [1, 3], F32), ("bc3", [128, 3], F32),
        ("run128", [128, 1], F32), ("wf128", [128, 1], F32),
        ("amb", [128, 2], F32), ("wab", [128, 2], F32),
        ("s1", [128, 2], F32), ("s2", [128, 2], F32),
        ("mstat", [128, 2], F32), ("qstat", [128, 2], F32),
        ("mm2", [128, 2], F32), ("varst", [128, 2], F32),
        ("rstd", [128, 2], F32), ("rinv", [128, 2], F32),
        ("bc3r", [128, 3], F32), ("prow1", [1, 128], F32),
        ("prow2", [1, 128], F32), ("lmax", [1, 1], F32),
        ("wmax", [1, 1], F32), ("gmax", [1, 1], F32),
        ("pmq", [128, max(NPS, 1)], F32), ("negm", [128, 1], F32),
        ("bmk", [128, 2], F32), ("den0", [128, 2], F32),
        ("den1", [128, 2], F32), ("dent", [128, 2], F32),
        ("recdall", [128, NE], F32),
        ("bias_rep", [128, E], BF16), ("wfb", [128, E], BF16),
        ("wsb", [128, 1], F32),
        ("wsbr", [128, 1], F32), ("oMx", [128, NT], F32),
        ("omn", [128, NT], F32), ("osm", [128, NT], F32),
        ("osq", [128, NT], F32), ("om", [128, NT], F32),
        ("oq", [128, NT], F32), ("omm", [128, NT], F32),
        ("ovar", [128, NT], F32), ("orstd", [128, NT], F32),
        ("orinv", [128, NT], F32), ("od1", [128, NT], F32),
        ("odm", [128, NT], F32), ("opx", [128, 1], F32),
    ]:
        names_small.append(sb(n, shp, dt_))
    sb("xnT", [128, NE * S], BF16)
    sb("ot_res", [128, NE * S], BF16)
    for i in range(2):
        sb(f"xbuf{i}", [128, E], F32)
    for i in range(2):
        sb(f"xnbuf{i}", [128, E], F32)
        sb(f"wq8st{i}", [128, E], FP8)
    sb("K_bf", [128, NT * E + (0 if S >= 2048 else 12 * E)], BF16)
    sb("VT", [128, NE * S], BF16)
    for i in range(2):
        sb(f"wobf{i}", [128, NE * 512], BF16)
    for i in range(6):
        sb(f"obuf{i}", [128, E], BF16)
    sb("brow", [1, E], BF16)
    sb("Wk_bf", [128, NE * E], BF16)
    for i in range(2):
        sb(f"wqbf{i}", [128, NE * QW], BF16)
    sb("q_pnl", [128, NT * QW], BF16)
    for i in range(2):
        sb(f"A_bf{i}", [128, S], BF16)
        sb(f"at_st{i}", [128, S], BF16)
        sb(f"atb{i}", [128, S], BF16)
    for i in range(4):
        sb(f"wvbf{i}", [128, NE * 128], BF16)
    for i in range(2):
        sb(f"otst{i}", [128, S], BF16)
    sb("xq8", [128, NE * S], FP8)
    for i in range(2):
        sb(f"wq8r{i}", [128, NE * 512], FP8)
        sb(f"fbf{i}", [128, 512], BF16)
        sb(f"f2t{i}", [128, 512], BF16)
        sb(f"xst{i}", [128, 512], F32)
        sb(f"outst{i}", [128, 512], F32)

    end0 = place(names_small, 0)
    R1 = (end0 + 1023) & ~1023
    e1 = max(place(["xnT"], R1), place(["ot_res"], R1))
    R2 = (e1 + 1023) & ~1023
    e2 = max(place([f"xbuf{i}" for i in range(2)]
                   + [f"xnbuf{i}" for i in range(2)]
                   + [f"wq8st{i}" for i in range(2)], R2),
             place(["K_bf"], R2), place(["VT"], R2),
             place(["wobf0", "wobf1"], R2))
    CHB = E * 2
    AT_END = ((addr_of["wq8st1"] + E - R2) + 63) & ~63
    FREEK = min(NT, max((NT * CHB - AT_END) // CHB, 0))
    e2 = max(e2, R2 + AT_END + FREEK * CHB)

    def kaddr(c):
        if c < FREEK:
            return AT_END + CHB * c
        return CHB * (c - FREEK)

    def K_ap(c, lo, hi):
        base = kaddr(c) // 2
        return K_bf[:, base + lo:base + hi]
    R3 = (e2 + 1023) & ~1023
    e3 = max(
        place(["Wk_bf"], R3),
        place(["wqbf0", "wqbf1", "q_pnl", "A_bf0", "A_bf1", "at_st0", "at_st1"], R3),
        place(["wvbf0", "wvbf1", "wvbf2", "wvbf3"], R3),
        place(["otst0", "otst1"], R3 + 8 * KIB),
        place(["atb0", "atb1"], R3),
        place(["xq8"], R3 + 16 * KIB),
        place(["wq8r0", "wq8r1"], R3 + 48 * KIB),
        place(["obuf0", "obuf1", "obuf2", "obuf3", "obuf4", "obuf5"], R3 + 40 * KIB),
        place(["brow"], (R2 + 32 * KIB) if S >= 2048 else (R3 + 120 * KIB)),
        place(["fbf0", "fbf1", "f2t0", "f2t1", "xst0", "xst1", "outst0", "outst1"], R3),
    )
    total = (e3 + 63) & ~63
    assert total <= 206 * KIB, f"sbuf over budget: {total}"
    backing = nc.alloc_sbuf_tensor("sbuf_backing", [128, total], mybir.dt.uint8)
    base = nc.lookup_mloc(backing).addr
    T = {}
    for n, (shp, dt_) in specs.items():
        T[n] = nc.alloc_sbuf_tensor_at(n, shp, dt_, offset=base + addr_of[n])

    G = T
    ident_f, ident_b, ones_f, ones_b = G["ident_f"], G["ident_b"], G["ones_f"], G["ones_b"]
    cat, bc3, run128, wf128 = G["cat"], G["bc3"], G["run128"], G["wf128"]
    amb, wab, s1, s2 = G["amb"], G["wab"], G["s1"], G["s2"]
    mstat, qstat, mm2, varst = G["mstat"], G["qstat"], G["mm2"], G["varst"]
    rstd, rinv, bc3r = G["rstd"], G["rinv"], G["bc3r"]
    prow1, prow2, lmax, wmax, gmax = G["prow1"], G["prow2"], G["lmax"], G["wmax"], G["gmax"]
    pmq, negm, bmk = G["pmq"], G["negm"], G["bmk"]
    den0, den1, dent, recdall = G["den0"], G["den1"], G["dent"], G["recdall"]
    brow, bias_rep, wfb, wsb, wsbr = G["brow"], G["bias_rep"], G["wfb"], G["wsb"], G["wsbr"]
    oMx, omn, osm, osq = G["oMx"], G["omn"], G["osm"], G["osq"]
    om, oq, omm, ovar = G["om"], G["oq"], G["omm"], G["ovar"]
    orstd, orinv, od1, odm, opx = G["orstd"], G["orinv"], G["od1"], G["odm"], G["opx"]

    xnT, ot_res, K_bf, VT = T["xnT"], T["ot_res"], T["K_bf"], T["VT"]
    Wk_bf, q_pnl, xq8 = T["Wk_bf"], T["q_pnl"], T["xq8"]
    xbuf = [T[f"xbuf{i}"] for i in range(2)]
    obuf = [T[f"obuf{i}"] for i in range(6)] + [T["bias_rep"], T["wfb"]]
    otst = [T[f"otst{i}"] for i in range(2)]
    NOB = 8
    xnbuf = [T[f"xnbuf{i}"] for i in range(2)]
    wq8st = [T[f"wq8st{i}"] for i in range(2)]
    wobf = [T[f"wobf{i}"] for i in range(2)]
    wqbf = [T[f"wqbf{i}"] for i in range(2)]
    A_bf = [T[f"A_bf{i}"] for i in range(2)]
    at_st = [T[f"at_st{i}"] for i in range(2)]
    wvbf = [T[f"wvbf{i}"] for i in range(4)]
    atb = [T[f"atb{i}"] for i in range(2)]
    wq8r = [T[f"wq8r{i}"] for i in range(2)]
    fbf = [T[f"fbf{i}"] for i in range(2)]
    f2t = [T[f"f2t{i}"] for i in range(2)]
    xst = [T[f"xst{i}"] for i in range(2)]
    outst = [T[f"outst{i}"] for i in range(2)]

    import contextlib
    est = contextlib.ExitStack()
    ps_big = est.enter_context(nc.psum_tensor("ps_big", [128, min(S, 2048)], F32))
    pb4 = est.enter_context(nc.psum_tensor("pb4", [128, 512], F32))
    pb5 = est.enter_context(nc.psum_tensor("pb5", [128, 512], F32))
    pb6 = est.enter_context(nc.psum_tensor("pb6", [128, 512], F32))
    pbt = est.enter_context(nc.psum_tensor("pbt", [128, 1024], BF16))
    pbank = [pb4, pb5, pb6]
    BCX0 = min(S, 2048) - 512
    K0 = min(6, max(NE - 2, 2)) & ~1
    pbank6 = [pb4[:], pb5[:], pb6[:], ps_big[:, 0:512], ps_big[:, 512:1024],
              ps_big[:, 1024:1536]] if S >= 2048 else [pb4[:], pb5[:], pb6[:]]
    NBK = len(pbank6)
    pbank7 = pbank6 + [ps_big[:, BCX0:BCX0 + 512]]
    NBK7 = len(pbank7)
    bb = ([ps_big[:, 512 * k:512 * (k + 1)] for k in range(4)] if S >= 2048
          else [ps_big[:, 0:512]])
    NBB = len(bb)

    finals = {}
    sem = {}

    def mksem(name, n_slots, final_counts):
        for s_ in range(n_slots):
            key = f"{name}{s_}" if n_slots > 1 else name
            fc = final_counts[s_] if isinstance(final_counts, list) else final_counts
            finals[key] = fc
            sem[key] = est.enter_context(nc.semaphore(key))

    def slotkey(name, slot, n_slots):
        return f"{name}{slot}" if n_slots > 1 else name

    def ring_finals(n_items, n_slots, inc=16):
        return [inc * len([i for i in range(n_items) if i % n_slots == s_])
                for s_ in range(n_slots)]

    mksem("d_x", 2, ring_finals(NT, 2))
    mksem("d_wfb", 1, 16 * 2 * NE)
    mksem("c_wab", 1, NE)
    mksem("d_p1", 1, 16)
    mksem("d_p2", 1, 16)
    mksem("d_p3", 1, 16)
    mksem("d_p4", 1, 16)
    mksem("d_bias", 1, 16)
    mksem("d_oth", 8, ring_finals(NTO, 8))
    mksem("d_wq8w", 2, ring_finals(NE, 2))
    for j in range(max(NE // 2, 1)):
        mksem(f"d_wk{j}", 1, 16)
    mksem("d_wq", 2, ring_finals(NQP, 2))
    mksem("d_atw", 2, ring_finals(NE, 2))
    mksem("d_atr", 2, ring_finals(NE, 2))
    mksem("d_wv", 4, ring_finals(NE, 4))
    mksem("d_otw", 2, ring_finals(K0, 2))
    mksem("d_otrb", 1, 16 * (K0 // 2))
    mksem("d_wq8r", 2, ring_finals(NPE, 2))
    mksem("d_wo", 2, ring_finals(NPE, 2))
    mksem("d_xe", 2, ring_finals(U, 2))
    mksem("d_out", 2, ring_finals(U, 2))
    for name, fc in [
        ("c_idv", 4), ("c_idp", 2), ("c_sacc", NT), ("c_scp", NT),
        ("c_pp", 4 * NT),
        ("c_rstd", NT), ("c_ri", NT), ("c_bc3r", 1), ("c_xn", NT), ("c_trA", GA * NT), ("c_evA", GA * NT),
        ("c_am", NT), ("c_mg", NT), ("c_dsc", 12),
        ("c_bcW", 1), ("c_bcX", 1), ("c_wq8", NE),
        ("c_oe", 9 * max(NB - 1, 1)), ("c_othv", NTO), ("c_otha", NTO),
        ("c_ocp", NTO),
        ("c_ovar", max(NB - 1, 1)),
        ("c_orst", max(NB - 1, 1)), ("c_othmg", max(NB - 1, 1)),
        ("c_mmB", NT * NPE), ("c_evB", NT * NPE), ("c_mmQ", NQP * NT),
        ("c_qev", NQP * NT), ("c_mmSq", NPS * NE), ("c_pmx", NPS * NE),
        ("c_ngm", NE), ("c_bmk", NE), ("c_den", NE), ("c_rcd", NE),
        ("c_exph", 2 * NE), ("c_trAT", GK * NE), ("c_evAT", GK * NE),
        ("c_mmV", NE * NPS), ("c_evV", NE * NPS), ("c_mmO", NE * NPS),
        ("c_evO", NE * NPS), ("c_q8x", NE),
        ("c_bcb", 1), ("c_bcb2", 1), ("c_mmEat", U), ("c_mmEff", U),
        ("c_fb", U), ("c_f2", U), ("c_o1", U), ("c_o2", U),
    ]:
        mksem(name, 1, fc)
    waits = []

    def W(eng, name, val):
        if val <= 0:
            return
        waits.append((name, val))
        eng.wait_ge(sem[name], val)

    def I(inst, name, n=1):
        if n > 0:
            inst.then_inc(sem[name], n)
        return inst

    def ck(t, j, w):
        return t[:, j * w:(j + 1) * w]

    oth_ld = [0]
    oth_dv = [0]
    oth_ac = [0]
    oth_oe = [0]
    wq8r_issued = [False]

    def emit_oth_load(eng, n):
        for _ in range(n):
            t = oth_ld[0]
            if t >= NTO:
                return
            oth_ld[0] += 1
            if t >= NOB:
                W(eng, "c_othv", t - NOB + 1)
                W(eng, "c_otha", t - NOB + 1)
            if t % NOB == 7 and t // NOB == 0:
                W(eng, "c_wq8", NE)
            I(eng.dma_start(out=obuf[t % NOB][:], in_=xo_t[t]), f"d_oth{t % NOB}", 16)

    def emit_oth_dve(vector, n):
        invn_ = 1.0 / E
        for _ in range(n):
            t = oth_dv[0]
            if t >= NTO:
                return
            oth_dv[0] += 1
            b, lt = divmod(t, NT)
            W(vector, f"d_oth{t % NOB}", 16 * (t // NOB + 1))
            vector.tensor_reduce(out=oMx[:, lt:lt + 1], in_=obuf[t % NOB][:],
                                 axis=AX.X, op=ALU.max)
            I(vector.tensor_reduce(out=omn[:, lt:lt + 1], in_=obuf[t % NOB][:],
                                   axis=AX.X, op=ALU.min), "c_othv")
            if lt == NT - 1:
                base = oth_oe[0]
                W(vector, "c_otha", (b + 1) * NT)
                W(vector, "c_othv", (b + 1) * NT)
                I(vector.tensor_scalar(out=om[:], in0=osm[:], scalar1=invn_,
                                       scalar2=None, op0=ALU.mult), "c_oe")
                I(vector.tensor_scalar(out=oq[:], in0=osq[:], scalar1=invn_,
                                       scalar2=1e-5, op0=ALU.mult,
                                       op1=ALU.add), "c_oe")
                W(vector, "c_oe", base + 1)
                I(vector.tensor_tensor(out=omm[:], in0=om[:], in1=om[:],
                                       op=ALU.mult), "c_oe")
                W(vector, "c_oe", base + 3)
                I(vector.tensor_tensor(out=ovar[:], in0=oq[:], in1=omm[:],
                                       op=ALU.subtract), "c_ovar")
                W(vector, "c_orst", b + 1)
                I(vector.reciprocal(out=orinv[:], in_=orstd[:]), "c_oe")
                I(vector.tensor_tensor(out=od1[:], in0=oMx[:], in1=om[:],
                                       op=ALU.subtract), "c_oe")
                I(vector.tensor_tensor(out=odm[:], in0=om[:], in1=omn[:],
                                       op=ALU.subtract), "c_oe")
                W(vector, "c_oe", base + 6)
                I(vector.tensor_tensor(out=odm[:], in0=od1[:], in1=odm[:],
                                       op=ALU.max), "c_oe")
                W(vector, "c_oe", base + 7)
                I(vector.tensor_tensor(out=odm[:], in0=odm[:], in1=orinv[:],
                                       op=ALU.mult), "c_oe")
                W(vector, "c_oe", base + 8)
                I(vector.tensor_reduce(out=opx[:], in_=odm[:], axis=AX.X,
                                       op=ALU.max), "c_oe")
                W(vector, "c_oe", base + 9)
                I(vector.tensor_tensor(out=run128[:], in0=run128[:], in1=opx[:],
                                       op=ALU.max), "c_othmg")
                oth_oe[0] = base + 9

    def emit_oth_act(scalar, n):
        for _ in range(n):
            t = oth_ac[0]
            if t >= NTO:
                return
            oth_ac[0] += 1
            b, lt = divmod(t, NT)
            W(scalar, f"d_oth{t % NOB}", 16 * (t // NOB + 1))
            W(scalar, "c_othv", t + 1)
            I(scalar.activation(out=obuf[t % NOB][:], in_=obuf[t % NOB][:], func=AF.Copy,
                                accum_out=osm[:, lt:lt + 1]), "c_ocp")
            W(scalar, "c_ocp", t + 1)
            I(scalar.activation(out=obuf[t % NOB][:], in_=obuf[t % NOB][:],
                                func=AF.Square, accum_out=osq[:, lt:lt + 1]),
              "c_otha")
            if lt == NT - 1:
                W(scalar, "c_ovar", b + 1)
                I(scalar.activation(out=orstd[:], in_=ovar[:], func=AF.Sqrt),
                  "c_orst")

    with nc.Block() as block:

        @block.sync
        def _(sync):
            for i in range(NT):
                if i >= 2:
                    W(sync, "c_xn", i - 1)
                    W(sync, "c_sacc", i - 1)
                I(sync.dma_start(out=xbuf[i % 2][:], in_=x_t[i]), f"d_x{i % 2}", 16)
                if i < NE:
                    if i >= 1:
                        W(sync, "c_wab", i)
                    I(sync.dma_start(out=wfb[:], in_=wft_c[i]), "d_wfb", 16)
            for k in range(NE):
                W(sync, "c_evAT", GK * (k + 1))
                I(sync.dma_start(out=at_hbm[k], in_=at_st[k % 2][:]), f"d_atw{k % 2}", 16)
            for k in range(NE):
                if k == 0:
                    W(sync, "c_mmV", NE * NPS)
                if k >= 2:
                    W(sync, "c_mmO", (k - 1) * NPS)
                W(sync, f"d_atw{k % 2}", 16 * (k // 2 + 1))
                I(sync.dma_start(out=atb[k % 2][:], in_=at_hbm[k]), f"d_atr{k % 2}", 16)
                if 1 <= k <= K0:
                    W(sync, "c_evO", k * NPS)
                    I(sync.dma_start(out=ot_hbm[k - 1], in_=otst[(k - 1) % 2][:]),
                      f"d_otw{(k - 1) % 2}", 16)
            W(sync, "c_mmO", NE * NPS)
            for g in range(NPE):
                for s_ in range(NT):
                    u = g * NT + s_
                    if u >= 2:
                        W(sync, "c_o1", u - 1)
                    I(sync.dma_start(out=xst[u % 2][:],
                                     in_=x_t[s_][:, 512 * g:512 * (g + 1)]),
                      f"d_xe{u % 2}", 16)
                    if u >= 1:
                        W(sync, "c_o2", u)
                        up = u - 1
                        I(sync.dma_start(
                            out=out_t[up % NT][:, 512 * (up // NT):512 * (up // NT + 1)],
                            in_=outst[up % 2][:]), f"d_out{up % 2}", 16)
            W(sync, "c_o2", U)
            up = U - 1
            I(sync.dma_start(
                out=out_t[up % NT][:, 512 * (up // NT):512 * (up // NT + 1)],
                in_=outst[up % 2][:]), f"d_out{up % 2}", 16)
            sync.wait_ge(sem["d_out0"], finals["d_out0"])
            sync.wait_ge(sem["d_out1"], finals["d_out1"])

        @block.scalar
        def _(scalar):
            for i in range(NT):
                W(scalar, f"d_x{i % 2}", 16 * (i // 2 + 1))
                if i >= 2:
                    W(scalar, "c_trA", GA * (i - 1))
                    W(scalar, "c_am", i - 1)
                I(scalar.activation(out=xnbuf[i % 2][:], in_=xbuf[i % 2][:],
                                    func=AF.Copy,
                                    accum_out=s1[:, i % 2:i % 2 + 1]), "c_scp")
                W(scalar, "c_scp", i + 1)
                I(scalar.activation(out=xnbuf[i % 2][:], in_=xbuf[i % 2][:],
                                    func=AF.Square,
                                    accum_out=s2[:, i % 2:i % 2 + 1]), "c_sacc")
                W(scalar, "c_pp", 4 * (i + 1))
                I(scalar.activation(out=rstd[:, i % 2:i % 2 + 1],
                                    in_=varst[:, i % 2:i % 2 + 1],
                                    func=AF.Sqrt), "c_rstd")
                if i >= 1:
                    for g in range(GA):
                        idx = GA * (i - 1) + g
                        W(scalar, "c_trA", idx + 1)
                        src = pb4 if idx % 2 == 0 else pb5
                        I(scalar.activation(
                            out=xnT[:].rearrange("p (c s) -> p c s", c=NE)[:, g * TPG:(g + 1) * TPG, 128 * (i - 1):128 * i],
                            in_=src[:].rearrange("p (c s) -> p c s", c=TPG),
                            func=AF.Copy), "c_evA")
            for g in range(GA):
                idx = GA * (NT - 1) + g
                W(scalar, "c_trA", idx + 1)
                src = pb4 if idx % 2 == 0 else pb5
                I(scalar.activation(
                    out=xnT[:].rearrange("p (c s) -> p c s", c=NE)[:, g * TPG:(g + 1) * TPG, 128 * (NT - 1):128 * NT],
                    in_=src[:].rearrange("p (c s) -> p c s", c=TPG),
                    func=AF.Copy), "c_evA")
            W(scalar, "c_wab", NE)
            for j in range(NE):
                if j >= 1:
                    W(scalar, "c_wq8", j)
                I(scalar.dma_start(out=wfb[:], in_=wft_c[j]), "d_wfb", 16)
            for p in range(NQP):
                for kl in range(QB):
                    k = p * QB + kl
                    W(scalar, "c_bmk", k + 1)
                    W(scalar, "c_mmSq", NPS * (k + 1))
                    if k >= 2:
                        W(scalar, "c_trAT", GK * (k - 1))
                        W(scalar, "c_den", k - 1)
                    I(scalar.activation(out=A_bf[k % 2][:, 0:SH], in_=ps_big[:, 0:SH],
                                        func=AF.Exp, scale=0.125,
                                        bias=bmk[:, k % 2:k % 2 + 1],
                                        accum_out=den0[:, k % 2:k % 2 + 1]), "c_exph")
                    I(scalar.activation(out=A_bf[k % 2][:, SH:S], in_=ps_big[:, SH:S],
                                        func=AF.Exp, scale=0.125,
                                        bias=bmk[:, k % 2:k % 2 + 1],
                                        accum_out=den1[:, k % 2:k % 2 + 1]), "c_exph")
                    for g2 in range(GK):
                        tcnt = min(8, TK - 8 * g2)
                        idxAT = GK * k + g2
                        W(scalar, "c_trAT", idxAT + 1)
                        if k >= 2 and g2 == 0:
                            W(scalar, f"d_atw{k % 2}", 16 * ((k - 2) // 2 + 1))
                        I(scalar.activation(
                            out=at_st[k % 2][:, 1024 * g2:1024 * g2 + 128 * tcnt],
                            in_=pbt[:, 0:128 * tcnt], func=AF.Copy), "c_evAT")
                    emit_oth_act(scalar, 3)
            for _f in range(NE):
                emit_oth_act(scalar, 3)
            for k in range(NE):
                if k < 4:
                    emit_oth_act(scalar, 4)
                for sp in range(NPS):
                    idx = k * NPS + sp
                    W(scalar, "c_mmO", idx + 1)
                    if k == 0 and sp == 0:
                        W(scalar, "c_mmV", NE * NPS)
                        W(scalar, "d_atw0", finals["d_atw0"])
                        W(scalar, "d_atw1", finals["d_atw1"])
                    if sp == 0:
                        W(scalar, "c_rcd", k + 1)
                        if k < K0:
                            if k >= 2:
                                W(scalar, f"d_otw{k % 2}", 16 * ((k - 2) // 2 + 1))
                        else:
                            W(scalar, "c_q8x", k + 1)
                    bank = pbank6[idx % NBK]
                    dst = (otst[k % 2][:, 512 * sp:512 * (sp + 1)] if k < K0 else
                           ot_res[:, S * k + 512 * sp:S * k + 512 * (sp + 1)])
                    I(scalar.activation(out=dst, in_=bank, func=AF.Copy,
                                        scale=recdall[:, k:k + 1]), "c_evO")
                if oth_ac[0] >= NTO and not wq8r_issued[0]:
                    wq8r_issued[0] = True
                    W(scalar, "d_wq8w0", finals["d_wq8w0"])
                    W(scalar, "d_wq8w1", finals["d_wq8w1"])
                    W(scalar, "c_mmB", NT * NPE)
                    I(scalar.dma_start(out=wq8r[0][:].rearrange("p (c f) -> p c f", c=NE),
                                       in_=wq8_hbm[:, :, 0:512].rearrange("c p f -> p c f")),
                      "d_wq8r0", 16)
                    if NPE > 1:
                        I(scalar.dma_start(out=wq8r[1][:].rearrange("p (c f) -> p c f", c=NE),
                                           in_=wq8_hbm[:, :, 512:1024].rearrange("c p f -> p c f")),
                          "d_wq8r1", 16)
            emit_oth_act(scalar, NTO)
            if not wq8r_issued[0]:
                wq8r_issued[0] = True
                W(scalar, "d_wq8w0", finals["d_wq8w0"])
                W(scalar, "d_wq8w1", finals["d_wq8w1"])
                W(scalar, "c_mmB", NT * NPE)
                I(scalar.dma_start(out=wq8r[0][:].rearrange("p (c f) -> p c f", c=NE),
                                   in_=wq8_hbm[:, :, 0:512].rearrange("c p f -> p c f")),
                  "d_wq8r0", 16)
                if NPE > 1:
                    I(scalar.dma_start(out=wq8r[1][:].rearrange("p (c f) -> p c f", c=NE),
                                       in_=wq8_hbm[:, :, 512:1024].rearrange("c p f -> p c f")),
                      "d_wq8r1", 16)
            for m in range(K0 // 2):
                W(scalar, "c_q8x", 2 * m + 2)
                W(scalar, "d_otw0", 16 * (m + 1))
                W(scalar, "d_otw1", 16 * (m + 1))
                I(scalar.dma_start(
                    out=ot_res[:, S * 2 * m:S * (2 * m + 2)].rearrange("p (k s) -> p k s", k=2),
                    in_=ot_hbm[2 * m:2 * m + 2].rearrange("k p s -> p k s")), "d_otrb", 16)
            for g in range(NPE):
                if g == 0:
                    W(scalar, "c_dsc", 11)
                for s_ in range(NT):
                    u = g * NT + s_
                    W(scalar, "c_mmEff", u + 1)
                    if u >= 2:
                        W(scalar, "c_f2", u - 1)
                    pf = pb6[:] if s_ % 2 == 0 else ps_big[:, 0:512]
                    I(scalar.activation(out=fbf[u % 2][:], in_=pf, func=AF.Copy,
                                        scale=bc3[:, 2:3]), "c_fb")
                gq = g + 2
                if gq < NPE:
                    W(scalar, "c_mmEff", (gq - 1) * NT)
                    I(scalar.dma_start(
                        out=wq8r[gq % 2][:].rearrange("p (c f) -> p c f", c=NE),
                        in_=wq8_hbm[:, :, 512 * gq:512 * (gq + 1)].rearrange("c p f -> p c f")),
                      f"d_wq8r{gq % 2}", 16)

        @block.vector
        def _(vector):
            invn = 1.0 / E
            I(vector.memset(ident_f[:], 0.0), "c_idv")
            I(vector.memset(ident_b[:], 0.0), "c_idv")
            I(vector.memset(ones_f[:], 1.0), "c_idv")
            I(vector.memset(ones_b[:], 1.0), "c_idv")
            for i in range(NT):
                if i < NE:
                    j = i
                    W(vector, "d_wfb", 16 * (j + 1))
                    I(vector.reduce_max(out=wab[:, j % 2:j % 2 + 1], in_=wfb[:],
                                        axis=AX.X, apply_absolute_value=True),
                      "c_wab")
                    W(vector, "c_wab", j + 1)
                    if j == 0:
                        iw = vector.tensor_copy(out=wf128[:], in_=wab[:, 0:1])
                    else:
                        iw = vector.tensor_tensor(out=wf128[:], in0=wf128[:],
                                                  in1=wab[:, j % 2:j % 2 + 1],
                                                  op=ALU.max)
                    if j == NE - 1:
                        iw.then_inc(sem["c_dsc"], 3)
                W(vector, "c_sacc", i + 1)
                if i >= 2:
                    W(vector, "c_trA", GA * (i - 1))
                    W(vector, "c_am", i - 1)
                I(vector.tensor_scalar(out=mstat[:, i % 2:i % 2 + 1],
                                       in0=s1[:, i % 2:i % 2 + 1], scalar1=invn,
                                       scalar2=None, op0=ALU.mult), "c_pp")
                I(vector.tensor_scalar(out=qstat[:, i % 2:i % 2 + 1],
                                       in0=s2[:, i % 2:i % 2 + 1], scalar1=invn,
                                       scalar2=1e-5, op0=ALU.mult,
                                       op1=ALU.add), "c_pp")
                W(vector, "c_pp", 4 * i + 2)
                I(vector.tensor_tensor(out=mm2[:, i % 2:i % 2 + 1],
                                       in0=mstat[:, i % 2:i % 2 + 1],
                                       in1=mstat[:, i % 2:i % 2 + 1],
                                       op=ALU.mult), "c_pp")
                W(vector, "c_pp", 4 * i + 3)
                I(vector.tensor_tensor(out=varst[:, i % 2:i % 2 + 1],
                                       in0=qstat[:, i % 2:i % 2 + 1],
                                       in1=mm2[:, i % 2:i % 2 + 1],
                                       op=ALU.subtract), "c_pp")
                W(vector, "c_rstd", i + 1)
                I(vector.reciprocal(out=rinv[:, i % 2:i % 2 + 1],
                                    in_=rstd[:, i % 2:i % 2 + 1]), "c_ri")
                W(vector, "c_ri", i + 1)
                I(vector.tensor_scalar(out=xnbuf[i % 2][:], in0=xbuf[i % 2][:],
                                       scalar1=mstat[:, i % 2:i % 2 + 1],
                                       scalar2=rinv[:, i % 2:i % 2 + 1],
                                       op0=ALU.subtract, op1=ALU.mult), "c_xn")
                W(vector, "c_xn", i + 1)
                I(vector.reduce_max(out=amb[:, i % 2:i % 2 + 1],
                                    in_=xnbuf[i % 2][:], axis=AX.X,
                                    apply_absolute_value=True), "c_am")
                W(vector, "c_am", i + 1)
                if i == 0:
                    I(vector.tensor_copy(out=run128[:], in_=amb[:, 0:1]), "c_mg")
                else:
                    I(vector.tensor_tensor(out=run128[:], in0=run128[:],
                                           in1=amb[:, i % 2:i % 2 + 1],
                                           op=ALU.max), "c_mg")
                if 0 <= i - 2 < min(FREEK, NT - 2):
                    sbv = i - 2
                    for pnl in range(NPE):
                        idx = sbv * NPE + pnl
                        W(vector, "c_mmB", idx + 1)
                        I(vector.tensor_copy(
                            out=K_ap(sbv, 512 * pnl, 512 * (pnl + 1)),
                            in_=bb[idx % NBB]), "c_evB")
            def emit_div(vector, j):
                if j == 0:
                    W(vector, "c_dsc", 7)
                W(vector, "d_wfb", 16 * (NE + j + 1))
                if j >= 2:
                    W(vector, f"d_wq8w{j % 2}", 16 * ((j - 2) // 2 + 1))
                I(vector.tensor_scalar(out=wq8st[j % 2][:], in0=wfb[:],
                                       scalar1=wsbr[:], scalar2=None,
                                       op0=ALU.mult), "c_wq8")
            W(vector, "d_p4", 16)
            I(vector.reduce_max(out=wmax[:], in_=prow2[:], axis=AX.X,
                                apply_absolute_value=True), "c_dsc")
            W(vector, "c_dsc", 4)
            I(vector.tensor_scalar_max(out=cat[:, 1:2], in0=wmax[:],
                                       scalar1=1e-6), "c_dsc")
            W(vector, "c_bcW", 1)
            I(vector.tensor_copy(out=wsb[:], in_=pb6[:, 0:1]), "c_dsc")
            W(vector, "c_dsc", 6)
            I(vector.reciprocal(out=wsbr[:], in_=wsb[:]), "c_dsc")
            done_early = max(min(FREEK, NT - 2), 0)
            for sb_ in range(NT):
                if sb_ < done_early:
                    continue
                if sb_ == FREEK:
                    W(vector, "c_xn", NT)
                    W(vector, "c_sacc", NT)
                    W(vector, "c_trA", GA * NT)
                    W(vector, "c_mg", NT)
                if sb_ >= FREEK and kaddr(sb_) == ((addr_of["wq8st0"] - R2) // CHB) * CHB:
                    W(vector, "d_wq8w0", finals["d_wq8w0"])
                    W(vector, "d_wq8w1", finals["d_wq8w1"])
                for pnl in range(NPE):
                    idx = sb_ * NPE + pnl
                    W(vector, "c_mmB", idx + 1)
                    I(vector.tensor_copy(
                        out=K_ap(sb_, 512 * pnl, 512 * (pnl + 1)),
                        in_=bb[idx % NBB]), "c_evB")
                if NT >= 14:
                    climit = FREEK + (addr_of["wq8st0"] - R2) // CHB
                    nproc = max(climit - done_early, 1)
                    if done_early <= sb_ < climit:
                        k0 = (sb_ - done_early) * NE // nproc
                        k1 = (sb_ - done_early + 1) * NE // nproc
                        for j in range(k0, k1):
                            emit_div(vector, j)
            if NT < 14:
                for j in range(NE):
                    emit_div(vector, j)
            for p in range(NQP):
                for sb_ in range(NT):
                    gi = p * NT + sb_
                    W(vector, "c_mmQ", gi + 1)
                    src = pbank[gi % 3]
                    I(vector.tensor_copy(out=q_pnl[:, QW * sb_:QW * (sb_ + 1)],
                                         in_=src[:, 0:QW]), "c_qev")
                for kl in range(QB):
                    k = p * QB + kl
                    for qq in range(NPS):
                        W(vector, "c_mmSq", NPS * k + qq + 1)
                        I(vector.reduce_max(out=pmq[:, qq:qq + 1],
                                            in_=ps_big[:, 512 * qq:512 * (qq + 1)],
                                            axis=AX.X), "c_pmx")
                    W(vector, "c_pmx", NPS * (k + 1))
                    I(vector.tensor_reduce(out=negm[:], in_=pmq[:], axis=AX.X,
                                           op=ALU.max, negate=True), "c_ngm")
                    W(vector, "c_ngm", k + 1)
                    I(vector.tensor_scalar(out=bmk[:, k % 2:k % 2 + 1], in0=negm[:],
                                           scalar1=0.125, scalar2=None,
                                           op0=ALU.mult), "c_bmk")
                    W(vector, "c_exph", 2 * (k + 1))
                    I(vector.tensor_tensor(out=dent[:, k % 2:k % 2 + 1],
                                           in0=den0[:, k % 2:k % 2 + 1],
                                           in1=den1[:, k % 2:k % 2 + 1],
                                           op=ALU.add), "c_den")
                    W(vector, "c_den", k + 1)
                    I(vector.reciprocal(out=recdall[:, k:k + 1],
                                        in_=dent[:, k % 2:k % 2 + 1]), "c_rcd")
                    emit_oth_dve(vector, 3)
            for f in range(NE):
                for sp in range(NPS):
                    idx = f * NPS + sp
                    W(vector, "c_mmV", idx + 1)
                    bank = pbank7[idx % NBK7]
                    I(vector.tensor_copy(
                        out=VT[:, S * f + 512 * sp:S * f + 512 * (sp + 1)],
                        in_=bank), "c_evV")
                emit_oth_dve(vector, 3)
            emit_oth_dve(vector, NTO)
            W(vector, "d_p2", 16)
            I(vector.reduce_max(out=gmax[:], in_=prow1[:], axis=AX.X,
                                apply_absolute_value=True), "c_dsc")
            W(vector, "c_dsc", 8)
            I(vector.tensor_scalar_max(out=cat[:, 0:1], in0=gmax[:], scalar1=1e-6),
              "c_dsc")
            W(vector, "c_dsc", 9)
            I(vector.tensor_tensor(out=cat[:, 2:3], in0=cat[:, 0:1],
                                   in1=cat[:, 1:2], op=ALU.mult), "c_dsc")
            W(vector, "c_bcX", 1)
            I(vector.tensor_copy(out=bc3[:], in_=ps_big[:, BCX0:BCX0 + 3]), "c_dsc")
            W(vector, "c_dsc", 11)
            I(vector.reciprocal(out=bc3r[:], in_=bc3[:]), "c_dsc")
            W(vector, "c_dsc", 12)
            W(vector, "d_atw0", finals["d_atw0"])
            W(vector, "d_atw1", finals["d_atw1"])
            W(vector, "c_trAT", GK * NE)
            W(vector, "c_mmSq", NPS * NE)
            W(vector, "c_otha", NTO)
            for c in range(NE):
                I(vector.tensor_scalar(
                    out=xq8[:, S * c:S * (c + 1)],
                    in0=xnT[:, S * c:S * (c + 1)],
                    scalar1=bc3r[:, 0:1], scalar2=None, op0=ALU.mult), "c_q8x")
            for t in range(NPE):
                W(vector, "c_bcb", 1)
                I(vector.tensor_copy(out=bias_rep[:, 512 * t:512 * (t + 1)],
                                     in_=ps_big[:, 512 * t:512 * (t + 1)]),
                  "c_bcb2", 1 if t == NPE - 1 else 0)
            for g in range(NPE):
                for s_ in range(NT):
                    u = g * NT + s_
                    W(vector, "c_mmEat", u + 1)
                    W(vector, f"d_xe{u % 2}", 16 * (u // 2 + 1))
                    if u >= 2:
                        W(vector, f"d_out{u % 2}", 16 * ((u - 2) // 2 + 1))
                    if u < 2:
                        W(vector, "d_otw0", finals["d_otw0"])
                        W(vector, "d_otw1", finals["d_otw1"])
                    pa = pb4 if s_ % 2 == 0 else pb5
                    I(vector.tensor_tensor(out=outst[u % 2][:], in0=pa[:],
                                           in1=xst[u % 2][:], op=ALU.add), "c_o1")
                    W(vector, "c_fb", u + 1)
                    if u == 0:
                        W(vector, "c_bcb2", 1)
                    if u >= 2:
                        W(vector, "c_o2", u - 1)
                    I(vector.tensor_tensor(out=f2t[u % 2][:], in0=fbf[u % 2][:],
                                           in1=bias_rep[:, 512 * g:512 * (g + 1)],
                                           op=ALU.add), "c_f2")
                    W(vector, "c_f2", u + 1)
                    W(vector, "c_o1", u + 1)
                    I(vector.tensor_tensor(out=outst[u % 2][:], in0=outst[u % 2][:],
                                           in1=f2t[u % 2][:], op=ALU.add), "c_o2")

        @block.tensor
        def _(tensor):
            W(tensor, "c_idp", 1)
            CPGk = NE // max(NE // 2, 1)

            def emit_b_group(sb_):
                for pnl in range(NPE):
                    idx = sb_ * NPE + pnl
                    if idx >= NBB:
                        W(tensor, "c_evB", idx - NBB + 1)
                    dst = bb[idx % NBB]
                    for j in range(NE):
                        if sb_ == 0 and pnl == 0 and j % CPGk == 0:
                            W(tensor, f"d_wk{j // CPGk}", 16)
                        I(tensor.matmul(
                            dst,
                            lhsT=xnT[:, S * j + 128 * sb_:S * j + 128 * (sb_ + 1)],
                            rhs=Wk_bf[:, E * j + 512 * pnl:E * j + 512 * (pnl + 1)],
                            start=(j == 0), stop=(j == NE - 1)),
                          "c_mmB", 1 if j == NE - 1 else 0)

            for i in range(NT):
                if 1 <= i <= min(FREEK, NT - 2) + 1:
                    W(tensor, "c_evA", GA * i)
                    emit_b_group(i - 1)
                W(tensor, "c_xn", i + 1)
                for g in range(GA):
                    idx = GA * i + g
                    if idx >= 2:
                        W(tensor, "c_evA", idx - 1)
                    dst = pb4 if idx % 2 == 0 else pb5
                    for t in range(TPG):
                        j = g * TPG + t
                        I(tensor.transpose(dst[:, 128 * t:128 * (t + 1)],
                                           in_=xnbuf[i % 2][:, 128 * j:128 * (j + 1)],
                                           identity=ident_f[:]),
                          "c_trA", 1 if t == TPG - 1 else 0)
            W(tensor, "c_evA", GA * NT)
            W(tensor, "c_dsc", 5)
            W(tensor, "c_idv", 3)
            I(tensor.matmul(pb6[:, 0:1], lhsT=ones_f[:], rhs=cat[:, 1:2],
                            start=True, stop=True), "c_bcW")
            for sbr_ in range(min(FREEK, NT - 2) + 1, NT):
                emit_b_group(sbr_)
            W(tensor, "c_evB", NT * NPE)
            W(tensor, "c_dsc", 6)
            W(tensor, "c_idp", 2)
            for p in range(NQP):
                W(tensor, f"d_wq{p % 2}", 16 * (p // 2 + 1))
                for sb_ in range(NT):
                    gi = p * NT + sb_
                    if gi >= 3:
                        W(tensor, "c_qev", gi - 2)
                    dst = pbank[gi % 3]
                    for j in range(NE):
                        I(tensor.matmul(
                            dst[:, 0:QW],
                            lhsT=xnT[:, S * j + 128 * sb_:S * j + 128 * (sb_ + 1)],
                            rhs=wqbf[p % 2][:, QW * j:QW * (j + 1)],
                            start=(j == 0), stop=(j == NE - 1)),
                          "c_mmQ", 1 if j == NE - 1 else 0)
                for kl in range(QB):
                    k = p * QB + kl
                    for qq in range(NPS):
                        if k >= 1:
                            half = 1 if (NPS > 1 and qq < NPS // 2) else 2
                            W(tensor, "c_exph", 2 * (k - 1) + half)
                        for s_ch in range(NT):
                            if kl == 0 and qq == 0:
                                W(tensor, "c_qev", p * NT + s_ch + 1)
                            I(tensor.matmul(
                                ps_big[:, 512 * qq:512 * (qq + 1)],
                                lhsT=q_pnl[:, QW * s_ch + 128 * kl:QW * s_ch + 128 * (kl + 1)],
                                rhs=K_ap(s_ch, 512 * qq, 512 * (qq + 1)),
                                start=(s_ch == 0), stop=(s_ch == NT - 1)),
                              "c_mmSq", 1 if s_ch == NT - 1 else 0)
                    for g2 in range(GK):
                        tcnt = min(8, TK - 8 * g2)
                        idxAT = GK * k + g2
                        need = 2 * k + (2 if GK == 1 else g2 + 1)
                        W(tensor, "c_exph", need)
                        W(tensor, "c_evAT", idxAT)
                        for t in range(tcnt):
                            j = g2 * 8 + t
                            I(tensor.transpose(pbt[:, 128 * t:128 * (t + 1)],
                                               in_=A_bf[k % 2][:, 128 * j:128 * (j + 1)],
                                               identity=ident_b[:]),
                              "c_trAT", 1 if t == tcnt - 1 else 0)
            W(tensor, "c_qev", NQP * NT)
            W(tensor, "c_dsc", 6)
            W(tensor, "c_exph", 2 * NE)
            W(tensor, "c_rcd", NE)
            for f in range(NE):
                W(tensor, f"d_wv{f % 4}", 16 * (f // 4 + 1))
                for sp in range(NPS):
                    idx = f * NPS + sp
                    if idx >= NBK7:
                        W(tensor, "c_evV", idx - NBK7 + 1)
                    bank = pbank7[idx % NBK7]
                    for j in range(NE):
                        I(tensor.matmul(
                            bank, lhsT=wvbf[f % 4][:, 128 * j:128 * (j + 1)],
                            rhs=xnT[:, S * j + 512 * sp:S * j + 512 * (sp + 1)],
                            start=(j == 0), stop=(j == NE - 1)),
                          "c_mmV", 1 if j == NE - 1 else 0)
            W(tensor, "c_evV", NE * NPS)
            for k in range(NE):
                if k == K0:
                    W(tensor, "c_dsc", 10)
                    W(tensor, "c_exph", 2 * NE)
                    I(tensor.matmul(ps_big[:, BCX0:BCX0 + 3], lhsT=ones_f[:], rhs=cat[:],
                                    start=True, stop=True), "c_bcX")
                W(tensor, f"d_atr{k % 2}", 16 * (k // 2 + 1))
                for sp in range(NPS):
                    idx = k * NPS + sp
                    if idx >= NBK:
                        W(tensor, "c_evO", idx - NBK + 1)
                    bank = pbank6[idx % NBK]
                    for c in range(NE):
                        I(tensor.matmul(
                            bank, lhsT=atb[k % 2][:, 128 * c:128 * (c + 1)],
                            rhs=VT[:, S * c + 512 * sp:S * c + 512 * (sp + 1)],
                            start=(c == 0), stop=(c == NE - 1)),
                          "c_mmO", 1 if c == NE - 1 else 0)
            W(tensor, "c_idv", 4)
            W(tensor, "d_bias", 16)
            W(tensor, "c_evO", NE * NPS)
            W(tensor, "c_dsc", 11)
            for t in range(NPE):
                I(tensor.matmul(ps_big[:, 512 * t:512 * (t + 1)], lhsT=ones_b[:],
                                rhs=brow[:, 512 * t:512 * (t + 1)], start=True,
                                stop=True), "c_bcb", 1 if t == NPE - 1 else 0)
            W(tensor, "c_q8x", NE)
            W(tensor, "d_otrb", 16 * (K0 // 2))
            for g in range(NPE):
                W(tensor, f"d_wo{g % 2}", 16 * (g // 2 + 1))
                W(tensor, f"d_wq8r{g % 2}", 16 * (g // 2 + 1))
                for s_ in range(NT):
                    u = g * NT + s_
                    pa = pb4 if s_ % 2 == 0 else pb5
                    if u >= 2:
                        W(tensor, "c_o1", u - 1)
                    for c in range(NE):
                        I(tensor.matmul(
                            pa[:],
                            lhsT=ot_res[:, S * c + 128 * s_:S * c + 128 * (s_ + 1)],
                            rhs=wobf[g % 2][:, 512 * c:512 * (c + 1)],
                            start=(c == 0), stop=(c == NE - 1)),
                          "c_mmEat", 1 if c == NE - 1 else 0)
                    pf = pb6[:] if s_ % 2 == 0 else ps_big[:, 0:512]
                    if u >= 2:
                        W(tensor, "c_fb", u - 1)
                    elif u == 0:
                        W(tensor, "c_bcb2", 1)
                    for d in range(NE // 2):
                        I(tensor.matmul(
                            pf,
                            lhsT=xq8[:].rearrange("p (c s) -> p c s", c=NE)[:, 2 * d:2 * d + 2, 128 * s_:128 * (s_ + 1)],
                            rhs=wq8r[g % 2][:].rearrange("p (c f) -> p c f", c=NE)[:, 2 * d:2 * d + 2, :],
                            start=(d == 0), stop=(d == NE // 2 - 1),
                            perf_mode=mybir.MatmulPerfMode.DoubleRow),
                          "c_mmEff", 1 if d == NE // 2 - 1 else 0)

        @block.gpsimd
        def _(gpsimd):
            W(gpsimd, "c_idv", 2)
            I(gpsimd.affine_select(out=ident_f[:], in_=ident_f[:],
                                   compare_op=ALU.not_equal, fill=1.0, base=0,
                                   pattern=[[-1, 128]], channel_multiplier=1), "c_idp")
            I(gpsimd.affine_select(out=ident_b[:], in_=ident_b[:],
                                   compare_op=ALU.not_equal, fill=1.0, base=0,
                                   pattern=[[-1, 128]], channel_multiplier=1), "c_idp")
            WKG = max(NE // 2, 1)
            CPG = NE // WKG
            for m in range(WKG):
                I(gpsimd.dma_start(
                    out=Wk_bf[:, CPG * m * E:CPG * (m + 1) * E].rearrange("p (c f) -> p c f", c=CPG),
                    in_=wk_c[CPG * m:CPG * (m + 1)].rearrange("c p f -> p c f")),
                  f"d_wk{m}", 16)
            W(gpsimd, "c_dsc", 3)
            I(gpsimd.dma_start(out=pscr2[:, None], in_=wf128[:, 0:1]), "d_p3", 16)
            W(gpsimd, "d_p3", 16)
            I(gpsimd.dma_start(out=prow2[:], in_=pscr2[None, :]), "d_p4", 16)
            for j in range(NE):
                W(gpsimd, "c_wq8", j + 1)
                I(gpsimd.dma_start(out=wq8_hbm[j], in_=wq8st[j % 2][:]), f"d_wq8w{j % 2}", 16)
            W(gpsimd, "c_mmB", NT * NPE)
            for p in range(min(2, NQP)):
                I(gpsimd.dma_start(out=wqbf[p % 2][:].rearrange("p (c f) -> p c f", c=NE),
                                   in_=wq_c[:, :, QW * p:QW * (p + 1)]), f"d_wq{p % 2}", 16)
            for p in range(NQP):
                pq = p + 2
                if pq < NQP:
                    W(gpsimd, "c_mmQ", (pq - 1) * NT)
                    I(gpsimd.dma_start(out=wqbf[pq % 2][:].rearrange("p (c f) -> p c f", c=NE),
                                       in_=wq_c[:, :, QW * pq:QW * (pq + 1)]), f"d_wq{pq % 2}", 16)
                emit_oth_load(gpsimd, 3 * QB)
            W(gpsimd, "c_mmQ", NQP * NT)
            for f in range(min(4, NE)):
                I(gpsimd.dma_start(out=wvbf[f % 4][:].rearrange("p (c f) -> p c f", c=NE),
                                   in_=wv_c[:, :, 128 * f:128 * (f + 1)]), f"d_wv{f % 4}", 16)
            for f in range(NE):
                fq = f + 4
                if fq < NE:
                    W(gpsimd, "c_mmV", (fq - 3) * NPS)
                    I(gpsimd.dma_start(out=wvbf[fq % 4][:].rearrange("p (c f) -> p c f", c=NE),
                                       in_=wv_c[:, :, 128 * fq:128 * (fq + 1)]), f"d_wv{fq % 4}", 16)
                emit_oth_load(gpsimd, 3)
            emit_oth_load(gpsimd, NTO)
            W(gpsimd, "c_mg", NT)
            W(gpsimd, "c_othmg", max(NB - 1, 1))
            I(gpsimd.dma_start(out=pscr1[:, None], in_=run128[:, 0:1]), "d_p1", 16)
            W(gpsimd, "d_p1", 16)
            I(gpsimd.dma_start(out=prow1[:], in_=pscr1[None, :]), "d_p2", 16)
            W(gpsimd, "c_mmO", NE * NPS)
            I(gpsimd.dma_start(out=brow[:], in_=bias_in[None, :]), "d_bias", 16)
            for g in range(min(2, NPE)):
                I(gpsimd.dma_start(out=wobf[g % 2][:].rearrange("p (c f) -> p c f", c=NE),
                                   in_=wo_c[:, :, 512 * g:512 * (g + 1)]), f"d_wo{g % 2}", 16)
            for gq in range(2, NPE):
                W(gpsimd, "c_mmEat", (gq - 1) * NT)
                I(gpsimd.dma_start(
                    out=wobf[gq % 2][:].rearrange("p (c f) -> p c f", c=NE),
                    in_=wo_c[:, :, 512 * gq:512 * (gq + 1)]), f"d_wo{gq % 2}", 16)

    for name, val in waits:
        assert val <= finals[name], f"wait {name}>={val} exceeds final {finals[name]}"
    return nc


def kernel(L0_attn_x, L0_attn_Wq, L0_attn_Wk, L0_attn_Wv, L0_attn_Wo,
           L0_ffn_weight, L0_ffn_bias):
    x = np.asarray(L0_attn_x)
    B, S, E = x.shape

    def f32(a):
        return np.ascontiguousarray(np.asarray(a), dtype=np.float32)

    def bf16(a):
        a = np.asarray(a)
        if a.dtype == np.uint16:
            a = a.view(ml_dtypes.bfloat16)
        assert a.dtype == ml_dtypes.bfloat16
        return a

    wq, wk, wv, wo = f32(L0_attn_Wq), f32(L0_attn_Wk), f32(L0_attn_Wv), f32(L0_attn_Wo)
    wft = np.ascontiguousarray(bf16(L0_ffn_weight).T)
    bias = np.ascontiguousarray(bf16(L0_ffn_bias))

    nc = build_program(S, E, n_cores=B)
    xs_f32 = [np.ascontiguousarray(x[c], dtype=np.float32) for c in range(B)]
    in_maps = [{"x": xs_f32[c],
                "xo": np.concatenate([xs_f32[d] for d in range(B) if d != c], axis=0),
                "wq": wq, "wk": wk, "wv": wv, "wo": wo,
                "wft": wft, "bias": bias} for c in range(B)]
    res = run_bass_kernel_spmd(nc, in_maps, core_ids=list(range(B)))
    out = np.stack([np.asarray(res.results[c]["out"]) for c in range(B)], axis=0)
    return out.astype(np.float32, copy=False)


# version 31
# speedup vs baseline: 1.0255x; 1.0014x over previous
import numpy as np
import ml_dtypes
import concourse.bass as bass
import concourse.mybir as mybir
from concourse.bass_utils import run_bass_kernel_spmd

F32 = mybir.dt.float32
BF16 = mybir.dt.bfloat16
FP8 = mybir.dt.float8e4
ALU = mybir.AluOpType
AF = mybir.ActivationFunctionType
AX = mybir.AxisListType
KIB = 1024


def build_program(S=2048, E=2048, n_cores=8):
    assert S == E
    NT = S // 128
    NE = E // 128
    GA = max(NE // 4, 1)
    TPG = NE // GA
    NPS = max(S // 512, 1)
    NPE = max(E // 512, 1)
    QW = 256
    NQP = E // QW
    QB = QW // 128
    U = NT * NPE
    SH = S // 2
    TK = S // 128
    GK = (TK + 7) // 8
    NB = n_cores
    NTO = (NB - 1) * NT

    nc = bass.Bass("TRN2", target_bir_lowering=False)

    x_in = nc.dram_tensor("x", [S, E], F32, kind="ExternalInput").ap()
    xo_in = nc.dram_tensor("xo", [NTO * 128, E], F32, kind="ExternalInput").ap()
    wq_in = nc.dram_tensor("wq", [E, E], F32, kind="ExternalInput").ap()
    wk_in = nc.dram_tensor("wk", [E, E], F32, kind="ExternalInput").ap()
    wv_in = nc.dram_tensor("wv", [E, E], F32, kind="ExternalInput").ap()
    wo_in = nc.dram_tensor("wo", [E, E], F32, kind="ExternalInput").ap()
    wft_in = nc.dram_tensor("wft", [E, E], BF16, kind="ExternalInput").ap()
    bias_in = nc.dram_tensor("bias", [E], BF16, kind="ExternalInput").ap()
    out_hbm = nc.dram_tensor("out", [S, E], F32, kind="ExternalOutput").ap()

    xn_hbm = nc.dram_tensor("xn_scr", [S, E], F32, kind="Internal").ap()
    at_hbm = nc.dram_tensor("at_scr", [NE, 128, S], BF16, kind="Internal").ap()
    ot_hbm = nc.dram_tensor("ot_scr", [NE, 128, S], BF16, kind="Internal").ap()
    wq8_hbm = nc.dram_tensor("wq8_scr", [NE, 128, E], FP8, kind="Internal").ap()
    pscr1 = nc.dram_tensor("pscr1", [128], F32, kind="Internal").ap()
    pscr2 = nc.dram_tensor("pscr2", [128], F32, kind="Internal").ap()

    x_t = x_in.rearrange("(t p) e -> t p e", p=128)
    xo_t = xo_in.rearrange("(t p) e -> t p e", p=128)
    xn_t = xn_hbm.rearrange("(t p) e -> t p e", p=128)
    out_t = out_hbm.rearrange("(t p) e -> t p e", p=128)
    wq_c = wq_in.rearrange("(c p) f -> p c f", p=128)
    wk_c = wk_in.rearrange("(c p) f -> c p f", p=128)
    wv_c = wv_in.rearrange("(c p) f -> p c f", p=128)
    wo_c = wo_in.rearrange("(c p) f -> p c f", p=128)
    wft_c = wft_in.rearrange("(c p) f -> c p f", p=128)

    specs = {}

    def sb(name, shape, dtype):
        specs[name] = (shape, dtype)
        return name

    addr_of = {}

    def place(names, base):
        off = base
        for n in names:
            shape, dtype = specs[n]
            per = 1
            for d in shape[1:]:
                per *= d
            per *= mybir.dt.size(dtype)
            off = (off + 63) & ~63
            addr_of[n] = off
            off += per
        return off

    names_small = []
    for n, shp, dt_ in [
        ("ident_f", [128, 128], F32), ("ident_b", [128, 128], BF16),
        ("ones_f", [1, 128], F32), ("ones_b", [1, 128], BF16),
        ("cat", [1, 3], F32), ("bc3", [128, 3], F32),
        ("run128", [128, 1], F32), ("wf128", [128, 1], F32),
        ("amb", [128, 2], F32), ("wab", [128, 2], F32),
        ("s1", [128, 2], F32), ("s2", [128, 2], F32),
        ("mstat", [128, 2], F32), ("qstat", [128, 2], F32),
        ("mm2", [128, 2], F32), ("varst", [128, 2], F32),
        ("rstd", [128, 2], F32), ("rinv", [128, 2], F32),
        ("bc3r", [128, 3], F32), ("prow1", [1, 128], F32),
        ("prow2", [1, 128], F32), ("lmax", [1, 1], F32),
        ("wmax", [1, 1], F32), ("gmax", [1, 1], F32),
        ("pmq", [128, max(NPS, 1)], F32), ("negm", [128, 1], F32),
        ("bmk", [128, 2], F32), ("den0", [128, 2], F32),
        ("den1", [128, 2], F32), ("dent", [128, 2], F32),
        ("recdall", [128, NE], F32),
        ("bias_rep", [128, E], BF16), ("wfb", [128, E], BF16),
        ("wsb", [128, 1], F32),
        ("wsbr", [128, 1], F32), ("oMx", [128, NT], F32),
        ("omn", [128, NT], F32), ("osm", [128, NT], F32),
        ("osq", [128, NT], F32), ("om", [128, NT], F32),
        ("oq", [128, NT], F32), ("omm", [128, NT], F32),
        ("ovar", [128, NT], F32), ("orstd", [128, NT], F32),
        ("orinv", [128, NT], F32), ("od1", [128, NT], F32),
        ("odm", [128, NT], F32), ("opx", [128, 1], F32),
    ]:
        names_small.append(sb(n, shp, dt_))
    sb("xnT", [128, NE * S], BF16)
    sb("ot_res", [128, NE * S], BF16)
    for i in range(2):
        sb(f"xbuf{i}", [128, E], F32)
    for i in range(2):
        sb(f"xnbuf{i}", [128, E], F32)
        sb(f"wq8st{i}", [128, E], FP8)
    sb("K_bf", [128, NT * E + (0 if S >= 2048 else 12 * E)], BF16)
    sb("VT", [128, NE * S], BF16)
    for i in range(2):
        sb(f"wobf{i}", [128, NE * 512], BF16)
    for i in range(6):
        sb(f"obuf{i}", [128, E], BF16)
    sb("brow", [1, E], BF16)
    sb("Wk_bf", [128, NE * E], BF16)
    for i in range(2):
        sb(f"wqbf{i}", [128, NE * QW], BF16)
    sb("q_pnl", [128, NT * QW], BF16)
    for i in range(2):
        sb(f"A_bf{i}", [128, S], BF16)
        sb(f"at_st{i}", [128, S], BF16)
        sb(f"atb{i}", [128, S], BF16)
    for i in range(4):
        sb(f"wvbf{i}", [128, NE * 128], BF16)
    for i in range(2):
        sb(f"otst{i}", [128, S], BF16)
    sb("xq8", [128, NE * S], FP8)
    for i in range(2):
        sb(f"wq8r{i}", [128, NE * 512], FP8)
        sb(f"fbf{i}", [128, 512], BF16)
        sb(f"f2t{i}", [128, 512], BF16)
        sb(f"xst{i}", [128, 512], F32)
        sb(f"outst{i}", [128, 512], F32)

    end0 = place(names_small, 0)
    R1 = (end0 + 1023) & ~1023
    e1 = max(place(["xnT"], R1), place(["ot_res"], R1))
    R2 = (e1 + 1023) & ~1023
    e2 = max(place([f"xbuf{i}" for i in range(2)]
                   + [f"xnbuf{i}" for i in range(2)]
                   + [f"wq8st{i}" for i in range(2)], R2),
             place(["K_bf"], R2), place(["VT"], R2),
             place(["wobf0", "wobf1"], R2))
    CHB = E * 2
    AT_END = ((addr_of["wq8st1"] + E - R2) + 63) & ~63
    FREEK = min(NT, max((NT * CHB - AT_END) // CHB, 0))
    e2 = max(e2, R2 + AT_END + FREEK * CHB)

    EARLY_G8 = NT >= 14 and min(FREEK, NT - 2) == FREEK and FREEK + 1 < NT - 1

    def kaddr(c):
        if c < FREEK:
            return AT_END + CHB * c
        return CHB * (c - FREEK)

    def K_ap(c, lo, hi):
        base = kaddr(c) // 2
        return K_bf[:, base + lo:base + hi]
    R3 = (e2 + 1023) & ~1023
    e3 = max(
        place(["Wk_bf"], R3),
        place(["wqbf0", "wqbf1", "q_pnl", "A_bf0", "A_bf1", "at_st0", "at_st1"], R3),
        place(["wvbf0", "wvbf1", "wvbf2", "wvbf3"], R3),
        place(["otst0", "otst1"], R3 + 8 * KIB),
        place(["atb0", "atb1"], R3),
        place(["xq8"], R3 + 16 * KIB),
        place(["wq8r0", "wq8r1"], R3 + 48 * KIB),
        place(["obuf0", "obuf1", "obuf2", "obuf3", "obuf4", "obuf5"], R3 + 40 * KIB),
        place(["brow"], (R2 + 32 * KIB) if S >= 2048 else (R3 + 120 * KIB)),
        place(["fbf0", "fbf1", "f2t0", "f2t1", "xst0", "xst1", "outst0", "outst1"], R3),
    )
    total = (e3 + 63) & ~63
    assert total <= 206 * KIB, f"sbuf over budget: {total}"
    backing = nc.alloc_sbuf_tensor("sbuf_backing", [128, total], mybir.dt.uint8)
    base = nc.lookup_mloc(backing).addr
    T = {}
    for n, (shp, dt_) in specs.items():
        T[n] = nc.alloc_sbuf_tensor_at(n, shp, dt_, offset=base + addr_of[n])

    G = T
    ident_f, ident_b, ones_f, ones_b = G["ident_f"], G["ident_b"], G["ones_f"], G["ones_b"]
    cat, bc3, run128, wf128 = G["cat"], G["bc3"], G["run128"], G["wf128"]
    amb, wab, s1, s2 = G["amb"], G["wab"], G["s1"], G["s2"]
    mstat, qstat, mm2, varst = G["mstat"], G["qstat"], G["mm2"], G["varst"]
    rstd, rinv, bc3r = G["rstd"], G["rinv"], G["bc3r"]
    prow1, prow2, lmax, wmax, gmax = G["prow1"], G["prow2"], G["lmax"], G["wmax"], G["gmax"]
    pmq, negm, bmk = G["pmq"], G["negm"], G["bmk"]
    den0, den1, dent, recdall = G["den0"], G["den1"], G["dent"], G["recdall"]
    brow, bias_rep, wfb, wsb, wsbr = G["brow"], G["bias_rep"], G["wfb"], G["wsb"], G["wsbr"]
    oMx, omn, osm, osq = G["oMx"], G["omn"], G["osm"], G["osq"]
    om, oq, omm, ovar = G["om"], G["oq"], G["omm"], G["ovar"]
    orstd, orinv, od1, odm, opx = G["orstd"], G["orinv"], G["od1"], G["odm"], G["opx"]

    xnT, ot_res, K_bf, VT = T["xnT"], T["ot_res"], T["K_bf"], T["VT"]
    Wk_bf, q_pnl, xq8 = T["Wk_bf"], T["q_pnl"], T["xq8"]
    xbuf = [T[f"xbuf{i}"] for i in range(2)]
    obuf = [T[f"obuf{i}"] for i in range(6)] + [T["bias_rep"], T["wfb"]]
    otst = [T[f"otst{i}"] for i in range(2)]
    NOB = 8
    xnbuf = [T[f"xnbuf{i}"] for i in range(2)]
    wq8st = [T[f"wq8st{i}"] for i in range(2)]
    wobf = [T[f"wobf{i}"] for i in range(2)]
    wqbf = [T[f"wqbf{i}"] for i in range(2)]
    A_bf = [T[f"A_bf{i}"] for i in range(2)]
    at_st = [T[f"at_st{i}"] for i in range(2)]
    wvbf = [T[f"wvbf{i}"] for i in range(4)]
    atb = [T[f"atb{i}"] for i in range(2)]
    wq8r = [T[f"wq8r{i}"] for i in range(2)]
    fbf = [T[f"fbf{i}"] for i in range(2)]
    f2t = [T[f"f2t{i}"] for i in range(2)]
    xst = [T[f"xst{i}"] for i in range(2)]
    outst = [T[f"outst{i}"] for i in range(2)]

    import contextlib
    est = contextlib.ExitStack()
    ps_big = est.enter_context(nc.psum_tensor("ps_big", [128, min(S, 2048)], F32))
    pb4 = est.enter_context(nc.psum_tensor("pb4", [128, 512], F32))
    pb5 = est.enter_context(nc.psum_tensor("pb5", [128, 512], F32))
    pb6 = est.enter_context(nc.psum_tensor("pb6", [128, 512], F32))
    pbt = est.enter_context(nc.psum_tensor("pbt", [128, 1024], BF16))
    pbank = [pb4, pb5, pb6]
    BCX0 = min(S, 2048) - 512
    K0 = min(6, max(NE - 2, 2)) & ~1
    pbank6 = [pb4[:], pb5[:], pb6[:], ps_big[:, 0:512], ps_big[:, 512:1024],
              ps_big[:, 1024:1536]] if S >= 2048 else [pb4[:], pb5[:], pb6[:]]
    NBK = len(pbank6)
    pbank7 = pbank6 + [ps_big[:, BCX0:BCX0 + 512]]
    NBK7 = len(pbank7)
    bb = ([ps_big[:, 512 * k:512 * (k + 1)] for k in range(4)] if S >= 2048
          else [ps_big[:, 0:512]])
    NBB = len(bb)

    finals = {}
    sem = {}

    def mksem(name, n_slots, final_counts):
        for s_ in range(n_slots):
            key = f"{name}{s_}" if n_slots > 1 else name
            fc = final_counts[s_] if isinstance(final_counts, list) else final_counts
            finals[key] = fc
            sem[key] = est.enter_context(nc.semaphore(key))

    def slotkey(name, slot, n_slots):
        return f"{name}{slot}" if n_slots > 1 else name

    def ring_finals(n_items, n_slots, inc=16):
        return [inc * len([i for i in range(n_items) if i % n_slots == s_])
                for s_ in range(n_slots)]

    mksem("d_x", 2, ring_finals(NT, 2))
    mksem("d_wfb", 1, 16 * 2 * NE)
    mksem("c_wab", 1, NE)
    mksem("d_p1", 1, 16)
    mksem("d_p2", 1, 16)
    mksem("d_p3", 1, 16)
    mksem("d_p4", 1, 16)
    mksem("d_bias", 1, 16)
    mksem("d_oth", 8, ring_finals(NTO, 8))
    mksem("d_wq8w", 2, ring_finals(NE, 2))
    for j in range(max(NE // 2, 1)):
        mksem(f"d_wk{j}", 1, 16)
    mksem("d_wq", 2, ring_finals(NQP, 2))
    mksem("d_atw", 2, ring_finals(NE, 2))
    mksem("d_atr", 2, ring_finals(NE, 2))
    mksem("d_wv", 4, ring_finals(NE, 4))
    mksem("d_otw", 2, ring_finals(K0, 2))
    mksem("d_otrb", 1, 16 * (K0 // 2))
    mksem("d_wq8r", 2, ring_finals(NPE, 2))
    mksem("d_wo", 2, ring_finals(NPE, 2))
    mksem("d_xe", 2, ring_finals(U, 2))
    mksem("d_out", 2, ring_finals(U, 2))
    for name, fc in [
        ("c_idv", 4), ("c_idp", 2), ("c_sacc", NT), ("c_scp", NT),
        ("c_pp", 4 * NT),
        ("c_rstd", NT), ("c_ri", NT), ("c_bc3r", 1), ("c_xn", NT), ("c_trA", GA * NT), ("c_evA", GA * NT),
        ("c_am", NT), ("c_mg", NT), ("c_dsc", 12),
        ("c_bcW", 1), ("c_bcX", 1), ("c_wq8", NE),
        ("c_oe", 9 * max(NB - 1, 1)), ("c_othv", NTO), ("c_otha", NTO),
        ("c_ocp", NTO),
        ("c_ovar", max(NB - 1, 1)),
        ("c_orst", max(NB - 1, 1)), ("c_othmg", max(NB - 1, 1)),
        ("c_mmB", NT * NPE), ("c_evB", NT * NPE), ("c_mmQ", NQP * NT),
        ("c_qev", NQP * NT), ("c_mmSq", NPS * NE), ("c_pmx", NPS * NE),
        ("c_ngm", NE), ("c_bmk", NE), ("c_den", NE), ("c_rcd", NE),
        ("c_exph", 2 * NE), ("c_trAT", GK * NE), ("c_evAT", GK * NE),
        ("c_mmV", NE * NPS), ("c_evV", NE * NPS), ("c_mmO", NE * NPS),
        ("c_evO", NE * NPS), ("c_q8x", NE),
        ("c_bcb", 1), ("c_bcb2", 1), ("c_mmEat", U), ("c_mmEff", U),
        ("c_fb", U), ("c_f2", U), ("c_o1", U), ("c_o2", U),
    ]:
        mksem(name, 1, fc)
    waits = []

    def W(eng, name, val):
        if val <= 0:
            return
        waits.append((name, val))
        eng.wait_ge(sem[name], val)

    def I(inst, name, n=1):
        if n > 0:
            inst.then_inc(sem[name], n)
        return inst

    def ck(t, j, w):
        return t[:, j * w:(j + 1) * w]

    oth_ld = [0]
    oth_dv = [0]
    oth_ac = [0]
    oth_oe = [0]
    wq8r_issued = [False]

    def emit_oth_load(eng, n):
        for _ in range(n):
            t = oth_ld[0]
            if t >= NTO:
                return
            oth_ld[0] += 1
            if t >= NOB:
                W(eng, "c_othv", t - NOB + 1)
                W(eng, "c_otha", t - NOB + 1)
            if t % NOB == 7 and t // NOB == 0:
                W(eng, "c_wq8", NE)
            I(eng.dma_start(out=obuf[t % NOB][:], in_=xo_t[t]), f"d_oth{t % NOB}", 16)

    def emit_oth_dve(vector, n):
        invn_ = 1.0 / E
        for _ in range(n):
            t = oth_dv[0]
            if t >= NTO:
                return
            oth_dv[0] += 1
            b, lt = divmod(t, NT)
            W(vector, f"d_oth{t % NOB}", 16 * (t // NOB + 1))
            vector.tensor_reduce(out=oMx[:, lt:lt + 1], in_=obuf[t % NOB][:],
                                 axis=AX.X, op=ALU.max)
            I(vector.tensor_reduce(out=omn[:, lt:lt + 1], in_=obuf[t % NOB][:],
                                   axis=AX.X, op=ALU.min), "c_othv")
            if lt == NT - 1:
                base = oth_oe[0]
                W(vector, "c_otha", (b + 1) * NT)
                W(vector, "c_othv", (b + 1) * NT)
                I(vector.tensor_scalar(out=om[:], in0=osm[:], scalar1=invn_,
                                       scalar2=None, op0=ALU.mult), "c_oe")
                I(vector.tensor_scalar(out=oq[:], in0=osq[:], scalar1=invn_,
                                       scalar2=1e-5, op0=ALU.mult,
                                       op1=ALU.add), "c_oe")
                W(vector, "c_oe", base + 1)
                I(vector.tensor_tensor(out=omm[:], in0=om[:], in1=om[:],
                                       op=ALU.mult), "c_oe")
                W(vector, "c_oe", base + 3)
                I(vector.tensor_tensor(out=ovar[:], in0=oq[:], in1=omm[:],
                                       op=ALU.subtract), "c_ovar")
                W(vector, "c_orst", b + 1)
                I(vector.reciprocal(out=orinv[:], in_=orstd[:]), "c_oe")
                I(vector.tensor_tensor(out=od1[:], in0=oMx[:], in1=om[:],
                                       op=ALU.subtract), "c_oe")
                I(vector.tensor_tensor(out=odm[:], in0=om[:], in1=omn[:],
                                       op=ALU.subtract), "c_oe")
                W(vector, "c_oe", base + 6)
                I(vector.tensor_tensor(out=odm[:], in0=od1[:], in1=odm[:],
                                       op=ALU.max), "c_oe")
                W(vector, "c_oe", base + 7)
                I(vector.tensor_tensor(out=odm[:], in0=odm[:], in1=orinv[:],
                                       op=ALU.mult), "c_oe")
                W(vector, "c_oe", base + 8)
                I(vector.tensor_reduce(out=opx[:], in_=odm[:], axis=AX.X,
                                       op=ALU.max), "c_oe")
                W(vector, "c_oe", base + 9)
                I(vector.tensor_tensor(out=run128[:], in0=run128[:], in1=opx[:],
                                       op=ALU.max), "c_othmg")
                oth_oe[0] = base + 9

    def emit_oth_act(scalar, n):
        for _ in range(n):
            t = oth_ac[0]
            if t >= NTO:
                return
            oth_ac[0] += 1
            b, lt = divmod(t, NT)
            W(scalar, f"d_oth{t % NOB}", 16 * (t // NOB + 1))
            W(scalar, "c_othv", t + 1)
            I(scalar.activation(out=obuf[t % NOB][:], in_=obuf[t % NOB][:], func=AF.Copy,
                                accum_out=osm[:, lt:lt + 1]), "c_ocp")
            W(scalar, "c_ocp", t + 1)
            I(scalar.activation(out=obuf[t % NOB][:], in_=obuf[t % NOB][:],
                                func=AF.Square, accum_out=osq[:, lt:lt + 1]),
              "c_otha")
            if lt == NT - 1:
                W(scalar, "c_ovar", b + 1)
                I(scalar.activation(out=orstd[:], in_=ovar[:], func=AF.Sqrt),
                  "c_orst")

    with nc.Block() as block:

        @block.sync
        def _(sync):
            for i in range(NT):
                if i >= 2:
                    W(sync, "c_xn", i - 1)
                    W(sync, "c_sacc", i - 1)
                I(sync.dma_start(out=xbuf[i % 2][:], in_=x_t[i]), f"d_x{i % 2}", 16)
                if i < NE:
                    if i >= 1:
                        W(sync, "c_wab", i)
                    I(sync.dma_start(out=wfb[:], in_=wft_c[i]), "d_wfb", 16)
            for k in range(NE):
                W(sync, "c_evAT", GK * (k + 1))
                I(sync.dma_start(out=at_hbm[k], in_=at_st[k % 2][:]), f"d_atw{k % 2}", 16)
            for k in range(NE):
                if k == 0:
                    W(sync, "c_mmV", NE * NPS)
                if k >= 2:
                    W(sync, "c_mmO", (k - 1) * NPS)
                W(sync, f"d_atw{k % 2}", 16 * (k // 2 + 1))
                I(sync.dma_start(out=atb[k % 2][:], in_=at_hbm[k]), f"d_atr{k % 2}", 16)
                if 1 <= k <= K0:
                    W(sync, "c_evO", k * NPS)
                    I(sync.dma_start(out=ot_hbm[k - 1], in_=otst[(k - 1) % 2][:]),
                      f"d_otw{(k - 1) % 2}", 16)
            W(sync, "c_mmO", NE * NPS)
            for g in range(NPE):
                for s_ in range(NT):
                    u = g * NT + s_
                    if u >= 2:
                        W(sync, "c_o1", u - 1)
                    I(sync.dma_start(out=xst[u % 2][:],
                                     in_=x_t[s_][:, 512 * g:512 * (g + 1)]),
                      f"d_xe{u % 2}", 16)
                    if u >= 1:
                        W(sync, "c_o2", u)
                        up = u - 1
                        I(sync.dma_start(
                            out=out_t[up % NT][:, 512 * (up // NT):512 * (up // NT + 1)],
                            in_=outst[up % 2][:]), f"d_out{up % 2}", 16)
            W(sync, "c_o2", U)
            up = U - 1
            I(sync.dma_start(
                out=out_t[up % NT][:, 512 * (up // NT):512 * (up // NT + 1)],
                in_=outst[up % 2][:]), f"d_out{up % 2}", 16)
            sync.wait_ge(sem["d_out0"], finals["d_out0"])
            sync.wait_ge(sem["d_out1"], finals["d_out1"])

        @block.scalar
        def _(scalar):
            for i in range(NT):
                W(scalar, f"d_x{i % 2}", 16 * (i // 2 + 1))
                if i >= 2:
                    W(scalar, "c_trA", GA * (i - 1))
                    W(scalar, "c_am", i - 1)
                I(scalar.activation(out=xnbuf[i % 2][:], in_=xbuf[i % 2][:],
                                    func=AF.Copy,
                                    accum_out=s1[:, i % 2:i % 2 + 1]), "c_scp")
                W(scalar, "c_scp", i + 1)
                I(scalar.activation(out=xnbuf[i % 2][:], in_=xbuf[i % 2][:],
                                    func=AF.Square,
                                    accum_out=s2[:, i % 2:i % 2 + 1]), "c_sacc")
                W(scalar, "c_pp", 4 * (i + 1))
                I(scalar.activation(out=rstd[:, i % 2:i % 2 + 1],
                                    in_=varst[:, i % 2:i % 2 + 1],
                                    func=AF.Sqrt), "c_rstd")
                if i >= 1:
                    for g in range(GA):
                        idx = GA * (i - 1) + g
                        W(scalar, "c_trA", idx + 1)
                        src = pb4 if idx % 2 == 0 else pb5
                        I(scalar.activation(
                            out=xnT[:].rearrange("p (c s) -> p c s", c=NE)[:, g * TPG:(g + 1) * TPG, 128 * (i - 1):128 * i],
                            in_=src[:].rearrange("p (c s) -> p c s", c=TPG),
                            func=AF.Copy), "c_evA")
            for g in range(GA):
                idx = GA * (NT - 1) + g
                W(scalar, "c_trA", idx + 1)
                src = pb4 if idx % 2 == 0 else pb5
                I(scalar.activation(
                    out=xnT[:].rearrange("p (c s) -> p c s", c=NE)[:, g * TPG:(g + 1) * TPG, 128 * (NT - 1):128 * NT],
                    in_=src[:].rearrange("p (c s) -> p c s", c=TPG),
                    func=AF.Copy), "c_evA")
            W(scalar, "c_wab", NE)
            for j in range(NE):
                if j >= 1:
                    W(scalar, "c_wq8", j)
                I(scalar.dma_start(out=wfb[:], in_=wft_c[j]), "d_wfb", 16)
            for p in range(NQP):
                for kl in range(QB):
                    k = p * QB + kl
                    W(scalar, "c_bmk", k + 1)
                    W(scalar, "c_mmSq", NPS * (k + 1))
                    if k >= 2:
                        W(scalar, "c_trAT", GK * (k - 1))
                        W(scalar, "c_den", k - 1)
                    I(scalar.activation(out=A_bf[k % 2][:, 0:SH], in_=ps_big[:, 0:SH],
                                        func=AF.Exp, scale=0.125,
                                        bias=bmk[:, k % 2:k % 2 + 1],
                                        accum_out=den0[:, k % 2:k % 2 + 1]), "c_exph")
                    I(scalar.activation(out=A_bf[k % 2][:, SH:S], in_=ps_big[:, SH:S],
                                        func=AF.Exp, scale=0.125,
                                        bias=bmk[:, k % 2:k % 2 + 1],
                                        accum_out=den1[:, k % 2:k % 2 + 1]), "c_exph")
                    for g2 in range(GK):
                        tcnt = min(8, TK - 8 * g2)
                        idxAT = GK * k + g2
                        W(scalar, "c_trAT", idxAT + 1)
                        if k >= 2 and g2 == 0:
                            W(scalar, f"d_atw{k % 2}", 16 * ((k - 2) // 2 + 1))
                        I(scalar.activation(
                            out=at_st[k % 2][:, 1024 * g2:1024 * g2 + 128 * tcnt],
                            in_=pbt[:, 0:128 * tcnt], func=AF.Copy), "c_evAT")
                    emit_oth_act(scalar, 3)
            for _f in range(NE):
                emit_oth_act(scalar, 3)
            for k in range(NE):
                if k < 4:
                    emit_oth_act(scalar, 4)
                for sp in range(NPS):
                    idx = k * NPS + sp
                    W(scalar, "c_mmO", idx + 1)
                    if k == 0 and sp == 0:
                        W(scalar, "c_mmV", NE * NPS)
                        W(scalar, "d_atw0", finals["d_atw0"])
                        W(scalar, "d_atw1", finals["d_atw1"])
                    if sp == 0:
                        W(scalar, "c_rcd", k + 1)
                        if k < K0:
                            if k >= 2:
                                W(scalar, f"d_otw{k % 2}", 16 * ((k - 2) // 2 + 1))
                        else:
                            W(scalar, "c_q8x", k + 1)
                    bank = pbank6[idx % NBK]
                    dst = (otst[k % 2][:, 512 * sp:512 * (sp + 1)] if k < K0 else
                           ot_res[:, S * k + 512 * sp:S * k + 512 * (sp + 1)])
                    I(scalar.activation(out=dst, in_=bank, func=AF.Copy,
                                        scale=recdall[:, k:k + 1]), "c_evO")
                if oth_ac[0] >= NTO and not wq8r_issued[0]:
                    wq8r_issued[0] = True
                    W(scalar, "d_wq8w0", finals["d_wq8w0"])
                    W(scalar, "d_wq8w1", finals["d_wq8w1"])
                    W(scalar, "c_mmB", NT * NPE)
                    I(scalar.dma_start(out=wq8r[0][:].rearrange("p (c f) -> p c f", c=NE),
                                       in_=wq8_hbm[:, :, 0:512].rearrange("c p f -> p c f")),
                      "d_wq8r0", 16)
                    if NPE > 1:
                        I(scalar.dma_start(out=wq8r[1][:].rearrange("p (c f) -> p c f", c=NE),
                                           in_=wq8_hbm[:, :, 512:1024].rearrange("c p f -> p c f")),
                          "d_wq8r1", 16)
            emit_oth_act(scalar, NTO)
            if not wq8r_issued[0]:
                wq8r_issued[0] = True
                W(scalar, "d_wq8w0", finals["d_wq8w0"])
                W(scalar, "d_wq8w1", finals["d_wq8w1"])
                W(scalar, "c_mmB", NT * NPE)
                I(scalar.dma_start(out=wq8r[0][:].rearrange("p (c f) -> p c f", c=NE),
                                   in_=wq8_hbm[:, :, 0:512].rearrange("c p f -> p c f")),
                  "d_wq8r0", 16)
                if NPE > 1:
                    I(scalar.dma_start(out=wq8r[1][:].rearrange("p (c f) -> p c f", c=NE),
                                       in_=wq8_hbm[:, :, 512:1024].rearrange("c p f -> p c f")),
                      "d_wq8r1", 16)
            for m in range(K0 // 2):
                W(scalar, "c_q8x", 2 * m + 2)
                W(scalar, "d_otw0", 16 * (m + 1))
                W(scalar, "d_otw1", 16 * (m + 1))
                I(scalar.dma_start(
                    out=ot_res[:, S * 2 * m:S * (2 * m + 2)].rearrange("p (k s) -> p k s", k=2),
                    in_=ot_hbm[2 * m:2 * m + 2].rearrange("k p s -> p k s")), "d_otrb", 16)
            for g in range(NPE):
                if g == 0:
                    W(scalar, "c_dsc", 11)
                for s_ in range(NT):
                    u = g * NT + s_
                    W(scalar, "c_mmEff", u + 1)
                    if u >= 2:
                        W(scalar, "c_f2", u - 1)
                    pf = pb6[:] if s_ % 2 == 0 else ps_big[:, 0:512]
                    I(scalar.activation(out=fbf[u % 2][:], in_=pf, func=AF.Copy,
                                        scale=bc3[:, 2:3]), "c_fb")
                gq = g + 2
                if gq < NPE:
                    W(scalar, "c_mmEff", (gq - 1) * NT)
                    I(scalar.dma_start(
                        out=wq8r[gq % 2][:].rearrange("p (c f) -> p c f", c=NE),
                        in_=wq8_hbm[:, :, 512 * gq:512 * (gq + 1)].rearrange("c p f -> p c f")),
                      f"d_wq8r{gq % 2}", 16)

        @block.vector
        def _(vector):
            invn = 1.0 / E
            I(vector.memset(ident_f[:], 0.0), "c_idv")
            I(vector.memset(ident_b[:], 0.0), "c_idv")
            I(vector.memset(ones_f[:], 1.0), "c_idv")
            I(vector.memset(ones_b[:], 1.0), "c_idv")
            for i in range(NT):
                if EARLY_G8 and i == NT - 1:
                    W(vector, "c_sacc", NT - 1)
                    for pnl in range(NPE):
                        idx = FREEK * NPE + pnl
                        W(vector, "c_mmB", idx + 1)
                        I(vector.tensor_copy(
                            out=K_ap(FREEK, 512 * pnl, 512 * (pnl + 1)),
                            in_=bb[idx % NBB]), "c_evB")
                if i < NE:
                    j = i
                    W(vector, "d_wfb", 16 * (j + 1))
                    I(vector.reduce_max(out=wab[:, j % 2:j % 2 + 1], in_=wfb[:],
                                        axis=AX.X, apply_absolute_value=True),
                      "c_wab")
                    W(vector, "c_wab", j + 1)
                    if j == 0:
                        iw = vector.tensor_copy(out=wf128[:], in_=wab[:, 0:1])
                    else:
                        iw = vector.tensor_tensor(out=wf128[:], in0=wf128[:],
                                                  in1=wab[:, j % 2:j % 2 + 1],
                                                  op=ALU.max)
                    if j == NE - 1:
                        iw.then_inc(sem["c_dsc"], 3)
                W(vector, "c_sacc", i + 1)
                if i >= 2:
                    W(vector, "c_trA", GA * (i - 1))
                    W(vector, "c_am", i - 1)
                I(vector.tensor_scalar(out=mstat[:, i % 2:i % 2 + 1],
                                       in0=s1[:, i % 2:i % 2 + 1], scalar1=invn,
                                       scalar2=None, op0=ALU.mult), "c_pp")
                I(vector.tensor_scalar(out=qstat[:, i % 2:i % 2 + 1],
                                       in0=s2[:, i % 2:i % 2 + 1], scalar1=invn,
                                       scalar2=1e-5, op0=ALU.mult,
                                       op1=ALU.add), "c_pp")
                W(vector, "c_pp", 4 * i + 2)
                I(vector.tensor_tensor(out=mm2[:, i % 2:i % 2 + 1],
                                       in0=mstat[:, i % 2:i % 2 + 1],
                                       in1=mstat[:, i % 2:i % 2 + 1],
                                       op=ALU.mult), "c_pp")
                W(vector, "c_pp", 4 * i + 3)
                I(vector.tensor_tensor(out=varst[:, i % 2:i % 2 + 1],
                                       in0=qstat[:, i % 2:i % 2 + 1],
                                       in1=mm2[:, i % 2:i % 2 + 1],
                                       op=ALU.subtract), "c_pp")
                W(vector, "c_rstd", i + 1)
                I(vector.reciprocal(out=rinv[:, i % 2:i % 2 + 1],
                                    in_=rstd[:, i % 2:i % 2 + 1]), "c_ri")
                W(vector, "c_ri", i + 1)
                I(vector.tensor_scalar(out=xnbuf[i % 2][:], in0=xbuf[i % 2][:],
                                       scalar1=mstat[:, i % 2:i % 2 + 1],
                                       scalar2=rinv[:, i % 2:i % 2 + 1],
                                       op0=ALU.subtract, op1=ALU.mult), "c_xn")
                W(vector, "c_xn", i + 1)
                I(vector.reduce_max(out=amb[:, i % 2:i % 2 + 1],
                                    in_=xnbuf[i % 2][:], axis=AX.X,
                                    apply_absolute_value=True), "c_am")
                W(vector, "c_am", i + 1)
                if i == 0:
                    I(vector.tensor_copy(out=run128[:], in_=amb[:, 0:1]), "c_mg")
                else:
                    I(vector.tensor_tensor(out=run128[:], in0=run128[:],
                                           in1=amb[:, i % 2:i % 2 + 1],
                                           op=ALU.max), "c_mg")
                if 0 <= i - 2 < min(FREEK, NT - 2):
                    sbv = i - 2
                    for pnl in range(NPE):
                        idx = sbv * NPE + pnl
                        W(vector, "c_mmB", idx + 1)
                        I(vector.tensor_copy(
                            out=K_ap(sbv, 512 * pnl, 512 * (pnl + 1)),
                            in_=bb[idx % NBB]), "c_evB")
            def emit_div(vector, j):
                if j == 0:
                    W(vector, "c_dsc", 7)
                W(vector, "d_wfb", 16 * (NE + j + 1))
                if j >= 2:
                    W(vector, f"d_wq8w{j % 2}", 16 * ((j - 2) // 2 + 1))
                I(vector.tensor_scalar(out=wq8st[j % 2][:], in0=wfb[:],
                                       scalar1=wsbr[:], scalar2=None,
                                       op0=ALU.mult), "c_wq8")
            W(vector, "d_p4", 16)
            I(vector.reduce_max(out=wmax[:], in_=prow2[:], axis=AX.X,
                                apply_absolute_value=True), "c_dsc")
            W(vector, "c_dsc", 4)
            I(vector.tensor_scalar_max(out=cat[:, 1:2], in0=wmax[:],
                                       scalar1=1e-6), "c_dsc")
            W(vector, "c_bcW", 1)
            I(vector.tensor_copy(out=wsb[:], in_=pb6[:, 0:1]), "c_dsc")
            W(vector, "c_dsc", 6)
            I(vector.reciprocal(out=wsbr[:], in_=wsb[:]), "c_dsc")
            done_early = max(min(FREEK, NT - 2), 0)
            for sb_ in range(NT):
                if sb_ < done_early:
                    continue
                if sb_ == FREEK:
                    W(vector, "c_xn", NT)
                    W(vector, "c_sacc", NT)
                    W(vector, "c_trA", GA * NT)
                    W(vector, "c_mg", NT)
                if sb_ >= FREEK and kaddr(sb_) == ((addr_of["wq8st0"] - R2) // CHB) * CHB:
                    W(vector, "d_wq8w0", finals["d_wq8w0"])
                    W(vector, "d_wq8w1", finals["d_wq8w1"])
                if not (EARLY_G8 and sb_ == FREEK):
                    for pnl in range(NPE):
                        idx = sb_ * NPE + pnl
                        W(vector, "c_mmB", idx + 1)
                        I(vector.tensor_copy(
                            out=K_ap(sb_, 512 * pnl, 512 * (pnl + 1)),
                            in_=bb[idx % NBB]), "c_evB")
                if NT >= 14:
                    climit = FREEK + (addr_of["wq8st0"] - R2) // CHB
                    nproc = max(climit - done_early, 1)
                    if done_early <= sb_ < climit:
                        k0 = (sb_ - done_early) * NE // nproc
                        k1 = (sb_ - done_early + 1) * NE // nproc
                        for j in range(k0, k1):
                            emit_div(vector, j)
            if NT < 14:
                for j in range(NE):
                    emit_div(vector, j)
            for p in range(NQP):
                for sb_ in range(NT):
                    gi = p * NT + sb_
                    W(vector, "c_mmQ", gi + 1)
                    src = pbank[gi % 3]
                    I(vector.tensor_copy(out=q_pnl[:, QW * sb_:QW * (sb_ + 1)],
                                         in_=src[:, 0:QW]), "c_qev")
                for kl in range(QB):
                    k = p * QB + kl
                    for qq in range(NPS):
                        W(vector, "c_mmSq", NPS * k + qq + 1)
                        I(vector.reduce_max(out=pmq[:, qq:qq + 1],
                                            in_=ps_big[:, 512 * qq:512 * (qq + 1)],
                                            axis=AX.X), "c_pmx")
                    W(vector, "c_pmx", NPS * (k + 1))
                    I(vector.tensor_reduce(out=negm[:], in_=pmq[:], axis=AX.X,
                                           op=ALU.max, negate=True), "c_ngm")
                    W(vector, "c_ngm", k + 1)
                    I(vector.tensor_scalar(out=bmk[:, k % 2:k % 2 + 1], in0=negm[:],
                                           scalar1=0.125, scalar2=None,
                                           op0=ALU.mult), "c_bmk")
                    W(vector, "c_exph", 2 * (k + 1))
                    I(vector.tensor_tensor(out=dent[:, k % 2:k % 2 + 1],
                                           in0=den0[:, k % 2:k % 2 + 1],
                                           in1=den1[:, k % 2:k % 2 + 1],
                                           op=ALU.add), "c_den")
                    W(vector, "c_den", k + 1)
                    I(vector.reciprocal(out=recdall[:, k:k + 1],
                                        in_=dent[:, k % 2:k % 2 + 1]), "c_rcd")
                    emit_oth_dve(vector, 3)
            for f in range(NE):
                for sp in range(NPS):
                    idx = f * NPS + sp
                    W(vector, "c_mmV", idx + 1)
                    bank = pbank7[idx % NBK7]
                    I(vector.tensor_copy(
                        out=VT[:, S * f + 512 * sp:S * f + 512 * (sp + 1)],
                        in_=bank), "c_evV")
                emit_oth_dve(vector, 3)
            emit_oth_dve(vector, NTO)
            W(vector, "d_p2", 16)
            I(vector.reduce_max(out=gmax[:], in_=prow1[:], axis=AX.X,
                                apply_absolute_value=True), "c_dsc")
            W(vector, "c_dsc", 8)
            I(vector.tensor_scalar_max(out=cat[:, 0:1], in0=gmax[:], scalar1=1e-6),
              "c_dsc")
            W(vector, "c_dsc", 9)
            I(vector.tensor_tensor(out=cat[:, 2:3], in0=cat[:, 0:1],
                                   in1=cat[:, 1:2], op=ALU.mult), "c_dsc")
            W(vector, "c_bcX", 1)
            I(vector.tensor_copy(out=bc3[:], in_=ps_big[:, BCX0:BCX0 + 3]), "c_dsc")
            W(vector, "c_dsc", 11)
            I(vector.reciprocal(out=bc3r[:], in_=bc3[:]), "c_dsc")
            W(vector, "c_dsc", 12)
            W(vector, "d_atw0", finals["d_atw0"])
            W(vector, "d_atw1", finals["d_atw1"])
            W(vector, "c_trAT", GK * NE)
            W(vector, "c_mmSq", NPS * NE)
            W(vector, "c_otha", NTO)
            for c in range(NE):
                I(vector.tensor_scalar(
                    out=xq8[:, S * c:S * (c + 1)],
                    in0=xnT[:, S * c:S * (c + 1)],
                    scalar1=bc3r[:, 0:1], scalar2=None, op0=ALU.mult), "c_q8x")
            for t in range(NPE):
                W(vector, "c_bcb", 1)
                I(vector.tensor_copy(out=bias_rep[:, 512 * t:512 * (t + 1)],
                                     in_=ps_big[:, 512 * t:512 * (t + 1)]),
                  "c_bcb2", 1 if t == NPE - 1 else 0)
            for g in range(NPE):
                for s_ in range(NT):
                    u = g * NT + s_
                    W(vector, "c_mmEat", u + 1)
                    W(vector, f"d_xe{u % 2}", 16 * (u // 2 + 1))
                    if u >= 2:
                        W(vector, f"d_out{u % 2}", 16 * ((u - 2) // 2 + 1))
                    if u < 2:
                        W(vector, "d_otw0", finals["d_otw0"])
                        W(vector, "d_otw1", finals["d_otw1"])
                    pa = pb4 if s_ % 2 == 0 else pb5
                    I(vector.tensor_tensor(out=outst[u % 2][:], in0=pa[:],
                                           in1=xst[u % 2][:], op=ALU.add), "c_o1")
                    W(vector, "c_fb", u + 1)
                    if u == 0:
                        W(vector, "c_bcb2", 1)
                    if u >= 2:
                        W(vector, "c_o2", u - 1)
                    I(vector.tensor_tensor(out=f2t[u % 2][:], in0=fbf[u % 2][:],
                                           in1=bias_rep[:, 512 * g:512 * (g + 1)],
                                           op=ALU.add), "c_f2")
                    W(vector, "c_f2", u + 1)
                    W(vector, "c_o1", u + 1)
                    I(vector.tensor_tensor(out=outst[u % 2][:], in0=outst[u % 2][:],
                                           in1=f2t[u % 2][:], op=ALU.add), "c_o2")

        @block.tensor
        def _(tensor):
            W(tensor, "c_idp", 1)
            CPGk = NE // max(NE // 2, 1)

            def emit_b_group(sb_):
                for pnl in range(NPE):
                    idx = sb_ * NPE + pnl
                    if idx >= NBB:
                        W(tensor, "c_evB", idx - NBB + 1)
                    dst = bb[idx % NBB]
                    for j in range(NE):
                        if sb_ == 0 and pnl == 0 and j % CPGk == 0:
                            W(tensor, f"d_wk{j // CPGk}", 16)
                        I(tensor.matmul(
                            dst,
                            lhsT=xnT[:, S * j + 128 * sb_:S * j + 128 * (sb_ + 1)],
                            rhs=Wk_bf[:, E * j + 512 * pnl:E * j + 512 * (pnl + 1)],
                            start=(j == 0), stop=(j == NE - 1)),
                          "c_mmB", 1 if j == NE - 1 else 0)

            for i in range(NT):
                if 1 <= i <= min(FREEK, NT - 2) + 1:
                    W(tensor, "c_evA", GA * i)
                    emit_b_group(i - 1)
                if EARLY_G8 and i == NT - 1:
                    W(tensor, "c_evA", GA * (FREEK + 2))
                    emit_b_group(FREEK + 1)
                W(tensor, "c_xn", i + 1)
                for g in range(GA):
                    idx = GA * i + g
                    if idx >= 2:
                        W(tensor, "c_evA", idx - 1)
                    dst = pb4 if idx % 2 == 0 else pb5
                    for t in range(TPG):
                        j = g * TPG + t
                        I(tensor.transpose(dst[:, 128 * t:128 * (t + 1)],
                                           in_=xnbuf[i % 2][:, 128 * j:128 * (j + 1)],
                                           identity=ident_f[:]),
                          "c_trA", 1 if t == TPG - 1 else 0)
            W(tensor, "c_evA", GA * NT)
            W(tensor, "c_dsc", 5)
            W(tensor, "c_idv", 3)
            I(tensor.matmul(pb6[:, 0:1], lhsT=ones_f[:], rhs=cat[:, 1:2],
                            start=True, stop=True), "c_bcW")
            for sbr_ in range(min(FREEK, NT - 2) + (2 if EARLY_G8 else 1), NT):
                emit_b_group(sbr_)
            W(tensor, "c_evB", NT * NPE)
            W(tensor, "c_dsc", 6)
            W(tensor, "c_idp", 2)
            for p in range(NQP):
                W(tensor, f"d_wq{p % 2}", 16 * (p // 2 + 1))
                for sb_ in range(NT):
                    gi = p * NT + sb_
                    if gi >= 3:
                        W(tensor, "c_qev", gi - 2)
                    dst = pbank[gi % 3]
                    for j in range(NE):
                        I(tensor.matmul(
                            dst[:, 0:QW],
                            lhsT=xnT[:, S * j + 128 * sb_:S * j + 128 * (sb_ + 1)],
                            rhs=wqbf[p % 2][:, QW * j:QW * (j + 1)],
                            start=(j == 0), stop=(j == NE - 1)),
                          "c_mmQ", 1 if j == NE - 1 else 0)
                for kl in range(QB):
                    k = p * QB + kl
                    for qq in range(NPS):
                        if k >= 1:
                            half = 1 if (NPS > 1 and qq < NPS // 2) else 2
                            W(tensor, "c_exph", 2 * (k - 1) + half)
                        for s_ch in range(NT):
                            if kl == 0 and qq == 0:
                                W(tensor, "c_qev", p * NT + s_ch + 1)
                            I(tensor.matmul(
                                ps_big[:, 512 * qq:512 * (qq + 1)],
                                lhsT=q_pnl[:, QW * s_ch + 128 * kl:QW * s_ch + 128 * (kl + 1)],
                                rhs=K_ap(s_ch, 512 * qq, 512 * (qq + 1)),
                                start=(s_ch == 0), stop=(s_ch == NT - 1)),
                              "c_mmSq", 1 if s_ch == NT - 1 else 0)
                    for g2 in range(GK):
                        tcnt = min(8, TK - 8 * g2)
                        idxAT = GK * k + g2
                        need = 2 * k + (2 if GK == 1 else g2 + 1)
                        W(tensor, "c_exph", need)
                        W(tensor, "c_evAT", idxAT)
                        for t in range(tcnt):
                            j = g2 * 8 + t
                            I(tensor.transpose(pbt[:, 128 * t:128 * (t + 1)],
                                               in_=A_bf[k % 2][:, 128 * j:128 * (j + 1)],
                                               identity=ident_b[:]),
                              "c_trAT", 1 if t == tcnt - 1 else 0)
            W(tensor, "c_qev", NQP * NT)
            W(tensor, "c_dsc", 6)
            W(tensor, "c_exph", 2 * NE)
            W(tensor, "c_rcd", NE)
            for f in range(NE):
                W(tensor, f"d_wv{f % 4}", 16 * (f // 4 + 1))
                for sp in range(NPS):
                    idx = f * NPS + sp
                    if idx >= NBK7:
                        W(tensor, "c_evV", idx - NBK7 + 1)
                    bank = pbank7[idx % NBK7]
                    for j in range(NE):
                        I(tensor.matmul(
                            bank, lhsT=wvbf[f % 4][:, 128 * j:128 * (j + 1)],
                            rhs=xnT[:, S * j + 512 * sp:S * j + 512 * (sp + 1)],
                            start=(j == 0), stop=(j == NE - 1)),
                          "c_mmV", 1 if j == NE - 1 else 0)
            W(tensor, "c_evV", NE * NPS)
            for k in range(NE):
                if k == K0:
                    W(tensor, "c_dsc", 10)
                    W(tensor, "c_exph", 2 * NE)
                    I(tensor.matmul(ps_big[:, BCX0:BCX0 + 3], lhsT=ones_f[:], rhs=cat[:],
                                    start=True, stop=True), "c_bcX")
                W(tensor, f"d_atr{k % 2}", 16 * (k // 2 + 1))
                for sp in range(NPS):
                    idx = k * NPS + sp
                    if idx >= NBK:
                        W(tensor, "c_evO", idx - NBK + 1)
                    bank = pbank6[idx % NBK]
                    for c in range(NE):
                        I(tensor.matmul(
                            bank, lhsT=atb[k % 2][:, 128 * c:128 * (c + 1)],
                            rhs=VT[:, S * c + 512 * sp:S * c + 512 * (sp + 1)],
                            start=(c == 0), stop=(c == NE - 1)),
                          "c_mmO", 1 if c == NE - 1 else 0)
            W(tensor, "c_idv", 4)
            W(tensor, "d_bias", 16)
            W(tensor, "c_evO", NE * NPS)
            W(tensor, "c_dsc", 11)
            for t in range(NPE):
                I(tensor.matmul(ps_big[:, 512 * t:512 * (t + 1)], lhsT=ones_b[:],
                                rhs=brow[:, 512 * t:512 * (t + 1)], start=True,
                                stop=True), "c_bcb", 1 if t == NPE - 1 else 0)
            W(tensor, "c_q8x", NE)
            W(tensor, "d_otrb", 16 * (K0 // 2))
            for g in range(NPE):
                W(tensor, f"d_wo{g % 2}", 16 * (g // 2 + 1))
                W(tensor, f"d_wq8r{g % 2}", 16 * (g // 2 + 1))
                for s_ in range(NT):
                    u = g * NT + s_
                    pa = pb4 if s_ % 2 == 0 else pb5
                    if u >= 2:
                        W(tensor, "c_o1", u - 1)
                    for c in range(NE):
                        I(tensor.matmul(
                            pa[:],
                            lhsT=ot_res[:, S * c + 128 * s_:S * c + 128 * (s_ + 1)],
                            rhs=wobf[g % 2][:, 512 * c:512 * (c + 1)],
                            start=(c == 0), stop=(c == NE - 1)),
                          "c_mmEat", 1 if c == NE - 1 else 0)
                    pf = pb6[:] if s_ % 2 == 0 else ps_big[:, 0:512]
                    if u >= 2:
                        W(tensor, "c_fb", u - 1)
                    elif u == 0:
                        W(tensor, "c_bcb2", 1)
                    for d in range(NE // 2):
                        I(tensor.matmul(
                            pf,
                            lhsT=xq8[:].rearrange("p (c s) -> p c s", c=NE)[:, 2 * d:2 * d + 2, 128 * s_:128 * (s_ + 1)],
                            rhs=wq8r[g % 2][:].rearrange("p (c f) -> p c f", c=NE)[:, 2 * d:2 * d + 2, :],
                            start=(d == 0), stop=(d == NE // 2 - 1),
                            perf_mode=mybir.MatmulPerfMode.DoubleRow),
                          "c_mmEff", 1 if d == NE // 2 - 1 else 0)

        @block.gpsimd
        def _(gpsimd):
            W(gpsimd, "c_idv", 2)
            I(gpsimd.affine_select(out=ident_f[:], in_=ident_f[:],
                                   compare_op=ALU.not_equal, fill=1.0, base=0,
                                   pattern=[[-1, 128]], channel_multiplier=1), "c_idp")
            I(gpsimd.affine_select(out=ident_b[:], in_=ident_b[:],
                                   compare_op=ALU.not_equal, fill=1.0, base=0,
                                   pattern=[[-1, 128]], channel_multiplier=1), "c_idp")
            WKG = max(NE // 2, 1)
            CPG = NE // WKG
            for m in range(WKG):
                I(gpsimd.dma_start(
                    out=Wk_bf[:, CPG * m * E:CPG * (m + 1) * E].rearrange("p (c f) -> p c f", c=CPG),
                    in_=wk_c[CPG * m:CPG * (m + 1)].rearrange("c p f -> p c f")),
                  f"d_wk{m}", 16)
            W(gpsimd, "c_dsc", 3)
            I(gpsimd.dma_start(out=pscr2[:, None], in_=wf128[:, 0:1]), "d_p3", 16)
            W(gpsimd, "d_p3", 16)
            I(gpsimd.dma_start(out=prow2[:], in_=pscr2[None, :]), "d_p4", 16)
            for j in range(NE):
                W(gpsimd, "c_wq8", j + 1)
                I(gpsimd.dma_start(out=wq8_hbm[j], in_=wq8st[j % 2][:]), f"d_wq8w{j % 2}", 16)
            W(gpsimd, "c_mmB", NT * NPE)
            for p in range(min(2, NQP)):
                I(gpsimd.dma_start(out=wqbf[p % 2][:].rearrange("p (c f) -> p c f", c=NE),
                                   in_=wq_c[:, :, QW * p:QW * (p + 1)]), f"d_wq{p % 2}", 16)
            for p in range(NQP):
                pq = p + 2
                if pq < NQP:
                    W(gpsimd, "c_mmQ", (pq - 1) * NT)
                    I(gpsimd.dma_start(out=wqbf[pq % 2][:].rearrange("p (c f) -> p c f", c=NE),
                                       in_=wq_c[:, :, QW * pq:QW * (pq + 1)]), f"d_wq{pq % 2}", 16)
                emit_oth_load(gpsimd, 3 * QB)
            W(gpsimd, "c_mmQ", NQP * NT)
            for f in range(min(4, NE)):
                I(gpsimd.dma_start(out=wvbf[f % 4][:].rearrange("p (c f) -> p c f", c=NE),
                                   in_=wv_c[:, :, 128 * f:128 * (f + 1)]), f"d_wv{f % 4}", 16)
            for f in range(NE):
                fq = f + 4
                if fq < NE:
                    W(gpsimd, "c_mmV", (fq - 3) * NPS)
                    I(gpsimd.dma_start(out=wvbf[fq % 4][:].rearrange("p (c f) -> p c f", c=NE),
                                       in_=wv_c[:, :, 128 * fq:128 * (fq + 1)]), f"d_wv{fq % 4}", 16)
                emit_oth_load(gpsimd, 3)
            emit_oth_load(gpsimd, NTO)
            W(gpsimd, "c_mg", NT)
            W(gpsimd, "c_othmg", max(NB - 1, 1))
            I(gpsimd.dma_start(out=pscr1[:, None], in_=run128[:, 0:1]), "d_p1", 16)
            W(gpsimd, "d_p1", 16)
            I(gpsimd.dma_start(out=prow1[:], in_=pscr1[None, :]), "d_p2", 16)
            W(gpsimd, "c_mmO", NE * NPS)
            I(gpsimd.dma_start(out=brow[:], in_=bias_in[None, :]), "d_bias", 16)
            for g in range(min(2, NPE)):
                I(gpsimd.dma_start(out=wobf[g % 2][:].rearrange("p (c f) -> p c f", c=NE),
                                   in_=wo_c[:, :, 512 * g:512 * (g + 1)]), f"d_wo{g % 2}", 16)
            for gq in range(2, NPE):
                W(gpsimd, "c_mmEat", (gq - 1) * NT)
                I(gpsimd.dma_start(
                    out=wobf[gq % 2][:].rearrange("p (c f) -> p c f", c=NE),
                    in_=wo_c[:, :, 512 * gq:512 * (gq + 1)]), f"d_wo{gq % 2}", 16)

    for name, val in waits:
        assert val <= finals[name], f"wait {name}>={val} exceeds final {finals[name]}"
    return nc


def kernel(L0_attn_x, L0_attn_Wq, L0_attn_Wk, L0_attn_Wv, L0_attn_Wo,
           L0_ffn_weight, L0_ffn_bias):
    x = np.asarray(L0_attn_x)
    B, S, E = x.shape

    def f32(a):
        return np.ascontiguousarray(np.asarray(a), dtype=np.float32)

    def bf16(a):
        a = np.asarray(a)
        if a.dtype == np.uint16:
            a = a.view(ml_dtypes.bfloat16)
        assert a.dtype == ml_dtypes.bfloat16
        return a

    wq, wk, wv, wo = f32(L0_attn_Wq), f32(L0_attn_Wk), f32(L0_attn_Wv), f32(L0_attn_Wo)
    wft = np.ascontiguousarray(bf16(L0_ffn_weight).T)
    bias = np.ascontiguousarray(bf16(L0_ffn_bias))

    nc = build_program(S, E, n_cores=B)
    xs_f32 = [np.ascontiguousarray(x[c], dtype=np.float32) for c in range(B)]
    in_maps = [{"x": xs_f32[c],
                "xo": np.concatenate([xs_f32[d] for d in range(B) if d != c], axis=0),
                "wq": wq, "wk": wk, "wv": wv, "wo": wo,
                "wft": wft, "bias": bias} for c in range(B)]
    res = run_bass_kernel_spmd(nc, in_maps, core_ids=list(range(B)))
    out = np.stack([np.asarray(res.results[c]["out"]) for c in range(B)], axis=0)
    return out.astype(np.float32, copy=False)
